# version 03
# baseline (speedup 1.0000x reference)
import numpy as np
import ml_dtypes
from contextlib import ExitStack

import concourse.bass as bass
import concourse.mybir as mybir
from concourse.ap import AP
from concourse.bass_utils import run_bass_kernel_spmd

F32 = mybir.dt.float32
BF16 = mybir.dt.bfloat16
AF = mybir.ActivationFunctionType
ALU = mybir.AluOpType
AX = mybir.AxisListType

CMAG = float(np.float32(1.5 * 2 ** 23))
NB, D, L, BL, NI = 4, 1024, 2048, 128, 16
NDC, NEC, NH = 8, 8, 8
FF, NIC, NS = 3072, 24, 4
EPS = 1e-5
BEI = NB * D * NI


def _h(t):
    return t.tensor if isinstance(t, AP) else t


class KB:
    def __init__(self, dbg_taps=False):
        nc = bass.Bass("TRN2", target_bir_lowering=False,
                       detect_race_conditions=False)
        self.nc = nc
        self.st = ExitStack()
        self.sem = {}
        self.cnt = {}
        for n in ("dS", "dG", "d2", "d3", "d4", "d5", "d6",
                  "sT", "sV", "sA", "sCC"):
            self.sem[n] = self.st.enter_context(nc.semaphore(n))
            self.cnt[n] = 0
        self.dbg_taps = dbg_taps

    def I(self, instr, name, amt=None):
        if amt is None:
            amt = 16 if name.startswith("d") else 1
        if name.startswith("d") and self.cnt[name] > 0:
            instr._wait_ge(self.sem[name], self.cnt[name])
        instr.then_inc(self.sem[name], amt)
        self.cnt[name] += amt
        return self.cnt[name]

    def W(self, eng, name, val=None):
        v = self.cnt[name] if val is None else val
        if v > 0:
            eng.wait_ge(self.sem[name], v)

    def sb(self, stack, n, s, d):
        return stack.enter_context(self.nc.sbuf_tensor(n, s, d))

    def ps(self, stack, n, s, d=F32):
        return stack.enter_context(self.nc.psum_tensor(n, s, d))

    def bar(self):
        self.nc.all_engine_barrier()


def build(dbg_taps=False, rg=None, upto='B'):
    K = KB(dbg_taps)
    K.rg = rg or [list(range(8))]
    K.upto = upto
    nc = K.nc

    di = lambda n, s: nc.dram_tensor(n, s, F32, kind="ExternalInput")
    K.u_fm = di("u_fm", [NB, D, L])
    K.wqT, K.wkT, K.wvT = di("wqT", [D, D]), di("wkT", [D, D]), di("wvT", [D, D])
    K.woTp = di("woTp", [D, D])
    K.biases = di("biases", [128, 32])
    K.kkpad = di("kkpad", [D, 2176])
    K.skD = di("skD", [128, 8])
    K.kcpad = di("kcpad", [128, 2176])
    K.Dc = di("Dc", [128, 1])
    K.wg, K.wu = di("wg", [FF, D]), di("wu", [FF, D])
    K.wdn = di("wdn", [D, FF])
    K.wmod = di("wmod", [D, 6 * D])
    K.c_own = di("c_own", [NB, D])
    K.outfm = nc.dram_tensor("outfm", [NB, D, L], F32, kind="ExternalOutput")

    ds = lambda n, s, d=F32: nc.dram_tensor(n, s, d)
    K.kkbf = ds("kkbf", [D, 2176], BF16)
    K.kcbf = ds("kcbf", [128, 2176], BF16)
    K.qtm, K.vtm = ds("qtm", [128, BEI], BF16), ds("vtm", [128, BEI], BF16)
    K.ktm, K.kptm = ds("ktm", [128, BEI], BF16), ds("kptm", [128, BEI], BF16)
    K.ytr = ds("ytr", [NB * 8, 128, L], BF16)
    K.x2dr = ds("x2dr", [NB, D, L], F32)
    K.st2dr = ds("st2dr", [NB, 2, L], F32)
    K.moddr = ds("moddr", [NB, 6 * D], F32)
    K.blhsdr = ds("blhsdr", [2, NB * 2 * D], F32)
    K.gatedr, K.updr = ds("gatedr", [NB, FF, L], BF16), ds("updr", [NB, FF, L], BF16)
    K.g2dr, K.odr = ds("g2dr", [NB, FF, L], BF16), ds("odr", [NB, FF, L], BF16)
    K.ar0i, K.ar0o = ds("ar0i", [1, 8]), ds("ar0o", [1, 8])
    K.ar1i, K.ar1o = ds("ar1i", [1, 8]), ds("ar1o", [1, 8])
    K.ar2i, K.ar2o = ds("ar2i", [1, 8]), ds("ar2o", [1, 8])
    K.ar3i, K.ar3o = ds("ar3i", [1, 8]), ds("ar3o", [1, 8])

    K.identbf = nc.inline_tensor(np.eye(128, dtype=ml_dtypes.bfloat16), "identbf")
    K.identf = nc.inline_tensor(np.eye(128, dtype=np.float32), "identf")
    K.c1024 = nc.inline_tensor(
        np.full((128, 1), 1.0 / 1024.0, dtype=ml_dtypes.bfloat16), "c1024")
    K.onescol = nc.inline_tensor(np.ones((1, 128), np.float32), "onescol")
    K.onesrow = nc.inline_tensor(np.ones((1, L), np.float32), "onesrow")

    if dbg_taps:
        K.taps = {}
        for n, s, d in [("tap_h", [128, NDC * L], BF16),
                        ("tap_ktm", [128, BEI], BF16),
                        ("tap_kptm", [128, BEI], BF16),
                        ("tap_ytr", [NB * 8, 128, L], BF16),
                        ("tap_x2", [NB, D, L], F32),
                        ("tap_scales", [128, 16], F32),
                        ("tap_oq", [NB, 128, NIC * L], BF16),
                        ("tap_vecs", [128, NB * 6 * NDC], F32),
                        ("tap_gabo", [128, NB * NDC], F32)]:
            K.taps[n] = nc.dram_tensor(n, s, d, kind="ExternalOutput")

    P = K.st
    K.ident_sb = K.sb(P, "ident_sb", [128, 128], BF16)
    K.identf_sb = K.sb(P, "identf_sb", [128, 128], F32)
    K.c1024_sb = K.sb(P, "c1024_sb", [128, 1], BF16)
    K.onescol_sb = K.sb(P, "onescol_sb", [1, 128], F32)
    K.bias_sb = K.sb(P, "bias_sb", [128, 32], F32)
    K.vecs = K.sb(P, "vecs", [128, NB * 6 * NDC], F32)
    K.gabo = K.sb(P, "gabo", [128, NB * NDC], F32)
    K.blhs = K.sb(P, "blhs", [2, D], F32)
    K.blhs_bf = K.sb(P, "blhs_bf", [2, D], BF16)
    K.rstd_bf = K.sb(P, "rstd_bf", [1, L], BF16)
    K.brhs_bf = K.sb(P, "brhs_bf", [2, L], BF16)
    K.brhs = K.sb(P, "brhs", [2, L], F32)
    K.rstd_sb = K.sb(P, "rstd_sb", [1, L], F32)
    K.scales = K.sb(P, "scales", [128, 16], F32)
    K.red1 = K.sb(P, "red1", [128, 1], F32)
    K.scrow = K.sb(P, "scrow", [1, 12], F32)
    K.redcg = K.sb(P, "redcg", [128, 384], F32)
    K.redcu = K.sb(P, "redcu", [128, 384], F32)

    K._blhs_cons = 0
    K._lnev = 0
    K._fin_t = 0

    if upto.startswith("BD"):
        order = [("P0", phase_P0), ("B", phase_B)]
    else:
        order = [("P0", phase_P0), ("A1", phase_A1), ("A2", phase_A2),
                 ("A3", phase_A3), ("A4", phase_A4)]
        if upto in ("B", "Bp", "B1", "B2", "B3", "B1h", "B1m", "B1m0", "B1m1",
                    "B1m2", "B1mA", "B1bar"):
            order.append(("B", phase_B))
    for name, fn in order:
        fn(K)
        K.bar()
        if name == upto or (name == "B"):
            break
    K.st.close()
    build._last_K = K
    return nc


def vcol(K, b, kind, dc):
    c = (kind * NDC + dc) * NB + b
    return K.vecs[:, c:c + 1]


def phase_P0(K):
    nc, I, W = K.nc, K.I, K.W
    st = ExitStack()
    kch = K.sb(st, "p0_kch", [128, 2176], F32)
    kchb = K.sb(st, "p0_kchb", [128, 2176], BF16)
    dcol = K.sb(st, "p0_dcol", [128, 8], F32)
    dccol = K.sb(st, "p0_dccol", [128, 1], F32)
    silc = K.sb(st, "p0_silc", [NB, D], F32)
    silcb = K.sb(st, "p0_silcb", [NB, D], BF16)
    silT = K.sb(st, "p0_silT", [128, NDC * NB], BF16)
    wmch = K.sb(st, "p0_wmch", [128, 2048], F32)
    wmchb = K.sb(st, "p0_wmchb", [128, 2 * 2048], BF16)
    mod_sb = K.sb(st, "p0_mod", [NB, 6 * D], F32)
    msk = K.sb(st, "p0_msk", [NB, 2 * D], mybir.dt.uint8)
    tpos = K.sb(st, "p0_tpos", [NB, 2 * D], F32)
    tneg = K.sb(st, "p0_tneg", [NB, 2 * D], F32)
    tps = K.ps(st, "p0_tps", [128, NB], BF16)
    tpsf = K.ps(st, "p0_tpsf", [128, 128], F32)
    mps = K.ps(st, "p0_mps", [NB, 2048], F32)

    nc.vector.memset(K.scales[:, :], 0.0)
    nc.vector.memset(K.scrow[:1, :], 0.0)
    I(nc.sync.dma_start(out=K.ident_sb[:, :], in_=K.identbf[:, :]), "dS")
    I(nc.sync.dma_start(out=K.identf_sb[:, :], in_=K.identf[:, :]), "dS")
    I(nc.sync.dma_start(out=K.c1024_sb[:, :], in_=K.c1024[:, :]), "dS")
    I(nc.sync.dma_start(out=K.onescol_sb[:, :], in_=K.onescol[:, :]), "dS")
    I(nc.sync.dma_start(out=K.bias_sb[:, :], in_=K.biases[:, :]), "dS")
    I(nc.sync.dma_start(out=K.brhs[1:2, :], in_=K.onesrow[:, :]), "dS")
    I(nc.sync.dma_start(out=dcol[:, :], in_=K.skD[:, :]), "dS")
    I(nc.sync.dma_start(out=dccol[:, :], in_=K.Dc[:, :]), "dS")
    t_small = I(nc.sync.dma_start(out=silc[:, :], in_=K.c_own[:, :]), "dS")

    t_store = 0
    for ci in range(9):
        src = K.kcpad[:, :] if ci == 0 else K.kkpad[(ci - 1) * 128:ci * 128, :]
        t_ld = I(nc.sync.dma_start(out=kch[:, :], in_=src), "dS")
        W(nc.vector, "dS", t_ld)
        W(nc.vector, "dG", t_store)
        dv = dccol[:, 0:1] if ci == 0 else dcol[:, ci - 1:ci]
        nc.vector.tensor_add(kch[:, 127:128], kch[:, 127:128], dv)
        t_cast = I(nc.vector.tensor_copy(kchb[:, :], kch[:, :]), "sV", 2)
        W(nc.sync, "sV", t_cast)
        W(nc.gpsimd, "sV", t_cast)
        dst = K.kcbf[:, :] if ci == 0 else K.kkbf[(ci - 1) * 128:ci * 128, :]
        t_store = I(nc.gpsimd.dma_start(out=dst, in_=kchb[:, :]), "dG")

    W(nc.scalar, "dS", t_small)
    t_sil = I(nc.scalar.activation(out=silc[:, :], in_=silc[:, :], func=AF.Silu),
              "sA")
    W(nc.vector, "sA", t_sil)
    t_silb = I(nc.vector.tensor_copy(silcb[:, :], silc[:, :]), "sV")
    for dc in range(NDC):
        W(nc.tensor, "sV", t_silb)
        if dc >= 1:
            W(nc.tensor, "sV", t_tpc)
        t_tp = I(nc.tensor.transpose(tps[:, :], silcb[:, dc * 128:(dc + 1) * 128],
                                     K.ident_sb[0:NB, 0:NB]), "sT")
        W(nc.vector, "sT", t_tp)
        t_tpc = I(nc.vector.tensor_copy(silT[:, dc * NB:(dc + 1) * NB], tps[:, :]),
                  "sV")
    t_mm = 0
    for s3 in range(3):
        for dc in range(NDC):
            W(nc.sync, "sV")
            t_wl = I(nc.sync.dma_start(
                out=wmch[:, :], in_=K.wmod[dc * 128:(dc + 1) * 128,
                                           s3 * 2048:(s3 + 1) * 2048]), "dS")
            W(nc.vector, "dS", t_wl)
            par = dc % 2
            if t_mm:
                W(nc.vector, "sT", t_mm)
            t_cb = I(nc.vector.tensor_copy(wmchb[:, par * 2048:(par + 1) * 2048],
                                           wmch[:, :]), "sV")
            W(nc.tensor, "sV", t_cb)
            for s5 in range(4):
                mm = nc.tensor.matmul(
                    mps[:, s5 * 512:(s5 + 1) * 512],
                    silT[:, dc * NB:(dc + 1) * NB],
                    wmchb[:, par * 2048 + s5 * 512: par * 2048 + (s5 + 1) * 512],
                    start=(dc == 0), stop=(dc == NDC - 1))
            t_mm = I(mm, "sT")
        W(nc.vector, "sT", t_mm)
        I(nc.vector.tensor_copy(mod_sb[:, s3 * 2048:(s3 + 1) * 2048], mps[:, :]),
          "sV")
    for kk_ in (2, 5):
        gsl = mod_sb[:, kk_ * D:(kk_ + 1) * D]
        nc.vector.tensor_scalar(out=msk[:, 0:D], in0=gsl, scalar1=0.0,
                                scalar2=None, op0=ALU.is_ge)
        nc.vector.tensor_scalar_max(tpos[:, 0:D], gsl, 0.25)
        nc.vector.tensor_scalar_min(tneg[:, 0:D], gsl, -0.25)
        nc.vector.select(gsl, msk[:, 0:D], tpos[:, 0:D], tneg[:, 0:D])
    csl = AP(_h(mod_sb), 1 * D, [[6 * D, NB], [3 * D, 2], [1, D]])
    t_mod = I(nc.vector.tensor_scalar_add(csl, csl, 1.0), "sV", 5)
    W(nc.gpsimd, "sV", t_mod)
    t_md = I(nc.gpsimd.dma_start(out=K.moddr[:, :], in_=mod_sb[:, :]), "dG")
    W(nc.gpsimd, "dG", t_md)
    for b in range(NB):
        for ln in range(2):
            base = (b * 2 + ln) * D
            I(nc.gpsimd.dma_start(
                out=AP(K.blhsdr, base, [[1, 1], [1, D]]),
                in_=K.moddr[b:b + 1, (1 + 3 * ln) * D:(2 + 3 * ln) * D]), "dG")
            I(nc.gpsimd.dma_start(
                out=AP(K.blhsdr, NB * 2 * D + base, [[1, 1], [1, D]]),
                in_=K.moddr[b:b + 1, (0 + 3 * ln) * D:(1 + 3 * ln) * D]), "dG")
    t_vc = t_mod
    for kind in range(6):
        for dc in range(NDC):
            if t_vc:
                W(nc.tensor, "sV", t_vc)
            t_tp = I(nc.tensor.transpose(
                tpsf[:, 0:NB], mod_sb[:, kind * D + dc * 128: kind * D + (dc + 1) * 128],
                K.identf_sb[0:NB, 0:NB]), "sT")
            W(nc.vector, "sT", t_tp)
            col = (kind * NDC + dc) * NB
            t_vc = I(nc.vector.tensor_copy(K.vecs[:, col:col + NB],
                                           tpsf[:, 0:NB]), "sV")
    for b in range(NB):
        gav = AP(_h(K.vecs), (2 * NDC) * NB + b, [[NB * 6 * NDC, 128], [NB, NDC]])
        t_gb = I(nc.vector.tensor_mul(K.gabo[:, b * NDC:(b + 1) * NDC], gav,
                                      K.bias_sb[:, 24:32]), "sV")
    if K.dbg_taps:
        W(nc.gpsimd, "sV", t_gb)
        I(nc.gpsimd.dma_start(out=K.taps["tap_vecs"][:, :], in_=K.vecs[:, :]), "dG")
        I(nc.gpsimd.dma_start(out=K.taps["tap_gabo"][:, :], in_=K.gabo[:, :]), "dG")
    st.close()


def row_stats(K, m_sb, s2_sb, rowt, rowt2):
    nc, I, W = K.nc, K.I, K.W
    v = nc.vector
    W(v, "sT", K._blhs_cons)
    v.tensor_mul(rowt[:1, :], m_sb[:1, :], m_sb[:1, :])
    v.tensor_sub(rowt[:1, :], s2_sb[:1, :], rowt[:1, :])
    t = I(v.tensor_scalar_add(rowt[:1, :], rowt[:1, :], EPS), "sV", 3)
    W(nc.scalar, "sV", t)
    t = I(nc.scalar.activation(out=K.rstd_sb[:1, :], in_=rowt[:1, :],
                               func=AF.Identity), "sA")
    W(v, "sA", t)
    t = I(v.reciprocal(rowt2[:1, :], K.rstd_sb[:1, :]), "sV", 1)
    W(nc.scalar, "sV", t)
    t = I(nc.scalar.activation(out=K.rstd_sb[:1, :], in_=rowt2[:1, :],
                               func=AF.Sqrt), "sA")
    W(v, "sA", t)
    v.tensor_mul(rowt2[:1, :], K.rstd_sb[:1, :], K.rstd_sb[:1, :])
    v.tensor_mul(rowt2[:1, :], rowt2[:1, :], rowt[:1, :])
    v.tensor_scalar(out=rowt2[:1, :], in0=rowt2[:1, :], scalar1=-0.5, scalar2=1.5,
                    op0=ALU.mult, op1=ALU.add)
    v.tensor_mul(K.rstd_sb[:1, :], K.rstd_sb[:1, :], rowt2[:1, :])
    return I(v.scalar_tensor_tensor(K.brhs[0:1, :], m_sb[:1, :], -1.0,
                                    K.rstd_sb[:1, :], ALU.mult, ALU.mult), "sV", 5)


def ln_stats(K, xsrc, m_sb, s2_sb, stps, xsq):
    nc, I, W = K.nc, K.I, K.W
    W(nc.scalar, "sV")
    t_mm = 0
    for s in range(NS):
        for dc in range(NDC):
            xs = xsrc(dc, s)
            if t_mm:
                W(nc.scalar, "sT", t_mm)
            t_sq = I(nc.scalar.activation(out=xsq[:, :], in_=xs, func=AF.Square),
                     "sA")
            W(nc.tensor, "sA", t_sq)
            if dc == 0:
                W(nc.tensor, "sV", K._lnev)
            nc.tensor.matmul(stps[0:1, :], K.c1024_sb[:, :], xs,
                             start=(dc == 0), stop=(dc == NDC - 1))
            t_mm = I(nc.tensor.matmul(stps[32:33, :], K.c1024_sb[:, :], xsq[:, :],
                                      start=(dc == 0), stop=(dc == NDC - 1)), "sT")
        W(nc.vector, "sT", t_mm)
        nc.vector.tensor_copy(m_sb[:1, s * 512:(s + 1) * 512], stps[0:1, :])
        K._lnev = I(nc.vector.tensor_copy(s2_sb[:1, s * 512:(s + 1) * 512],
                                          stps[32:33, :]), "sV", 2)


def ab_maps(K, dc, s, abA, abB):
    nc, I = K.nc, K.I
    lA = AP(_h(K.blhs_bf), dc * 128, [[D, 1], [1, 128]])
    lB = AP(_h(K.blhs_bf), dc * 128, [[D, 2], [1, 128]])
    nc.tensor.matmul(abA[:, :], lA, K.rstd_bf[:1, s * 512:(s + 1) * 512],
                     start=True, stop=True)
    t = I(nc.tensor.matmul(abB[:, :], lB, K.brhs_bf[:, s * 512:(s + 1) * 512],
                           start=True, stop=True), "sT")
    K._blhs_cons = t
    return t


def rows_to_bf(K):
    nc, I, W = K.nc, K.I, K.W
    W(nc.vector, "sT", K._blhs_cons)
    nc.vector.tensor_copy(K.rstd_bf[:1, :], K.rstd_sb[:1, :])
    return I(nc.vector.tensor_copy(K.brhs_bf[:, :], K.brhs[:, :]), "sV")


def blhs_to_bf(K):
    nc, I, W = K.nc, K.I, K.W
    W(nc.vector, "sT", K._blhs_cons)
    return I(nc.vector.tensor_copy(K.blhs_bf[:, :], K.blhs[:, :]), "sV")


def phase_A1(K):
    nc, I, W = K.nc, K.I, K.W
    nc.sync.wait_ge(K.sem["dG"], K.cnt["dG"])
    st = ExitStack()
    x_bf = K.sb(st, "a1_xbf", [128, NDC * L], BF16)
    h_bf = K.sb(st, "a1_hbf", [128, NDC * L], BF16)
    xch = K.sb(st, "a1_xch", [128, 2 * L], F32)
    xsq = K.sb(st, "a1_xsq", [128, 512], BF16)
    pfm = K.sb(st, "a1_pfm", [128, 2 * L], BF16)
    stag = K.sb(st, "a1_stag", [128, D * NI], BF16)
    wsl = K.sb(st, "a1_wsl", [128, 2 * 1024], F32)
    wslb = K.sb(st, "a1_wslb", [128, 2 * 1024], BF16)
    m_sb = K.sb(st, "a1_m", [1, L], F32)
    s2_sb = K.sb(st, "a1_s2", [1, L], F32)
    rowt = K.sb(st, "a1_rt", [1, L], F32)
    rowt2 = K.sb(st, "a1_rt2", [1, L], F32)
    stps = K.ps(st, "a1_st", [128, 512])
    abps = [K.ps(st, f"a1_ab{i}", [128, 512]) for i in range(2)]
    pps = [K.ps(st, f"a1_pp{i}", [128, 512]) for i in range(2)]
    tpsl = [K.ps(st, f"a1_tp{i}", [128, 128], BF16) for i in range(2)]

    t_abcons = [0, 0]
    t_tpev = [0, 0]
    t_lastTp = [0, 0]
    t_wmm = [0, 0]
    t_stagd = 0
    for b in range(NB):
        W(nc.sync, "sT", K._blhs_cons)
        t_bl = I(nc.sync.dma_start(
            out=K.blhs[:, :], in_=AP(K.blhsdr, (b * 2 + 0) * D,
                                     [[NB * 2 * D, 2], [1, D]])), "dS")
        t_xc = [0, 0]
        for dc in range(NDC):
            par = dc % 2
            W(nc.sync, "sV", t_xc[par])
            t_ld = I(nc.sync.dma_start(out=xch[:, par * L:(par + 1) * L],
                                       in_=K.u_fm[b, dc * 128:(dc + 1) * 128, :]),
                     "d2")
            W(nc.vector, "d2", t_ld)
            if b > 0 and dc == 0:
                W(nc.vector, "sT")
            t_xc[par] = I(nc.vector.tensor_copy(
                x_bf[:, dc * L:(dc + 1) * L], xch[:, par * L:(par + 1) * L]), "sV")
        ln_stats(K, lambda dc, s: x_bf[:, dc * L + s * 512: dc * L + (s + 1) * 512],
                 m_sb, s2_sb, stps, xsq)
        row_stats(K, m_sb, s2_sb, rowt, rowt2)
        W(nc.vector, "dS", t_bl)
        rows_to_bf(K)
        blhs_to_bf(K)
        W(nc.tensor, "sV")
        t_h = 0
        for s in range(NS):
            for dc in range(NDC):
                W(nc.tensor, "sV", t_abcons[0])
                t_ab = ab_maps(K, dc, s, abps[0], abps[1])
                W(nc.vector, "sT", t_ab)
                sl = h_bf[:, dc * L + s * 512: dc * L + (s + 1) * 512]
                xsl = x_bf[:, dc * L + s * 512: dc * L + (s + 1) * 512]
                nc.vector.scalar_tensor_tensor(sl, xsl, 1.0, abps[0][:, :],
                                               ALU.mult, ALU.mult)
                t_h = I(nc.vector.tensor_add(sl, sl, abps[1][:, :]),
                        "sV", 2)
                t_abcons[0] = t_h
        if not hasattr(K, "_a1wc"):
            K._a1wc = [0, 0]
        for ip, (nm, wdr, dr) in enumerate(
                (("q", K.wqT, K.qtm), ("k", K.wkT, K.ktm), ("v", K.wvT, K.vtm))):
            t_evprev = 0
            for ec in range(NEC):
                par = ec % 2
                W(nc.sync, "sV", K._a1wc[par])
                t_wl = I(nc.sync.dma_start(
                    out=wsl[:, par * 1024:(par + 1) * 1024],
                    in_=AP(wdr, ec * 128 * D, [[D, 128], [1, D]])),
                    "d3")
                W(nc.vector, "d3", t_wl)
                W(nc.vector, "sT", t_wmm[par])
                K._a1wc[par] = I(nc.vector.tensor_copy(
                    wslb[:, par * 1024:(par + 1) * 1024],
                    wsl[:, par * 1024:(par + 1) * 1024]), "sV")
                W(nc.tensor, "sV", max(K._a1wc[par], t_h))
                for hf in range(2):
                    W(nc.tensor, "sA", t_evprev)
                    for dc in range(NDC):
                        for s in range(2):
                            mm = nc.tensor.matmul(
                                pps[s][:, :],
                                wslb[:, par * 1024 + dc * 128: par * 1024 + (dc + 1) * 128],
                                h_bf[:, dc * L + hf * 1024 + s * 512: dc * L + hf * 1024 + (s + 1) * 512],
                                start=(dc == 0), stop=(dc == NDC - 1))
                    t_mm = I(mm, "sT")
                    W(nc.scalar, "sT", t_mm)
                    if hf == 1:
                        W(nc.scalar, "sT", t_lastTp[par])
                    for s in range(2):
                        rev = AP(pps[s], 127, [[512, 128], [128, 4], [-1, 128]])
                        ev = nc.scalar.activation(
                            out=pfm[:, par * L + hf * 1024 + s * 512: par * L + hf * 1024 + (s + 1) * 512],
                            in_=rev, func=AF.Identity,
                            bias=K.bias_sb[:, ip * 8 + ec: ip * 8 + ec + 1], scale=1.0)
                    t_evprev = I(ev, "sA", 2)
                t_wmm[par] = t_mm
                W(nc.tensor, "sA", t_evprev)
                W(nc.tensor, "dG", t_stagd)
                for Ib in range(NI):
                    tpar = Ib % 2
                    W(nc.tensor, "sV", t_tpev[tpar])
                    tt = nc.tensor.transpose(
                        tpsl[tpar][:, :],
                        pfm[:, par * L + Ib * 128: par * L + (Ib + 1) * 128],
                        K.ident_sb[:, :])
                    t_tp = I(tt, "sT")
                    t_lastTp[par] = t_tp
                    W(nc.vector, "sT", t_tp)
                    dst = AP(_h(stag), ec * 128 * NI + Ib,
                             [[D * NI, 128], [NI, 128]])
                    t_tpev[tpar] = I(nc.vector.tensor_copy(dst, tpsl[tpar][:, :]),
                                     "sV")
            W(nc.gpsimd, "sV", max(t_tpev))
            t_stagd = I(nc.gpsimd.dma_start(
                out=dr[:, b * D * NI:(b + 1) * D * NI], in_=stag[:, :]), "dG")
            if K.dbg_taps and nm == "k" and b == 0:
                I(nc.gpsimd.dma_start(out=K.taps["tap_h"][:, :], in_=h_bf[:, :]),
                  "dG")
    st.close()


def phase_A2(K):
    nc, I, W = K.nc, K.I, K.W
    nc.sync.wait_ge(K.sem["dG"], K.cnt["dG"])
    st = ExitStack()
    ksb = K.sb(st, "a2_ksb", [128, BEI], BF16)
    skwr = [K.sb(st, f"a2_skwr{i}", [128, 2048], BF16) for i in range(2)]
    skw = [K.sb(st, f"a2_skw{i}", [128, 2048], BF16) for i in range(2)]
    kstage = K.sb(st, "a2_kstage", [128, 4096], BF16)
    kps = [K.ps(st, f"a2_kps{i}", [128, 512]) for i in range(2)]

    t_kld = I(nc.sync.dma_start(out=ksb[:, :], in_=K.ktm[:, :]), "dS")
    t_raw = [0, 0]
    t_rev = [0, 0]
    t_mmd = [0, 0]
    t_bankev = [0, 0]
    t_flush = 0
    for e in range(D):
        par = e % 2
        W(nc.sync, "sV", t_rev[par])
        t_raw[par] = I(nc.sync.dma_start(
            out=skwr[par][:, :], in_=AP(K.kkbf, e * 2176, [[1, 128], [1, 2048]])),
            "d2")
        W(nc.vector, "d2", t_raw[par])
        W(nc.vector, "sT", t_mmd[par])
        rev = AP(_h(skwr[par]), 2047, [[2048, 128], [-1, 2048]])
        t_rev[par] = I(nc.vector.tensor_copy(skw[par][:, :], rev), "sV")
        bpar = (e // 8) % 2
        bank = kps[bpar]
        eg = e % 8
        W(nc.tensor, "sV", t_rev[par])
        if e == 0:
            W(nc.tensor, "dS", t_kld)
        if eg == 0:
            W(nc.tensor, "sA", t_bankev[bpar])
        for dd in range(NI):
            outw = AP(bank, eg * 64 + dd * 4, [[512, 128], [4, NI - dd], [1, 4]])
            rhs = AP(_h(ksb), e * NI, [[BEI, 128], [1, NI - dd], [D * NI, NB]])
            mm = nc.tensor.matmul(outw,
                                  skw[par][:, (NI - 1 - dd) * 128:(NI - dd) * 128],
                                  rhs, start=(dd == 0), stop=(dd == NI - 1))
        t_mmd[par] = I(mm, "sT")
        if eg == 7:
            grp = e // 8
            sgrp = grp % 8
            W(nc.scalar, "sT", t_mmd[par])
            if sgrp == 0:
                W(nc.scalar, "dG", t_flush)
            dst = AP(_h(kstage), sgrp * 8 * NI,
                     [[4096, 128], [NI, 8], [1, NI], [64 * NI, NB]])
            src = AP(bank, 0, [[512, 128], [64, 8], [4, NI], [1, 4]])
            t_bankev[bpar] = I(nc.scalar.copy(dst, src), "sA")
            if sgrp == 7:
                W(nc.gpsimd, "sA", t_bankev[bpar])
                e_start = (grp - 7) * 8
                t_flush = I(nc.gpsimd.dma_start(
                    out=AP(K.kptm, e_start * NI,
                           [[BEI, 128], [D * NI, NB], [1, 64 * NI]]),
                    in_=AP(_h(kstage), 0, [[4096, 128], [64 * NI, NB],
                                           [1, 64 * NI]])), "dG")
    if K.dbg_taps:
        W(nc.gpsimd, "dG", t_flush)
        I(nc.gpsimd.dma_start(out=K.taps["tap_kptm"][:, :], in_=K.kptm[:, :]), "dG")
        I(nc.gpsimd.dma_start(out=K.taps["tap_ktm"][:, :], in_=ksb[:, :]), "dG")
    st.close()


def phase_A3(K):
    nc, I, W = K.nc, K.I, K.W
    nc.sync.wait_ge(K.sem["dG"], K.cnt["dG"])
    st = ExitStack()
    skwr = [K.sb(st, f"a3_skwr{i}", [128, 2048], BF16) for i in range(2)]
    skw = [K.sb(st, f"a3_skw{i}", [128, 2048], BF16) for i in range(2)]
    qc = [K.sb(st, f"a3_qc{i}", [128, 4096], BF16) for i in range(2)]
    vc = [K.sb(st, f"a3_vc{i}", [128, 4096], BF16) for i in range(2)]
    kc = [K.sb(st, f"a3_kc{i}", [128, 4096], BF16) for i in range(2)]
    kv = [K.sb(st, f"a3_kv{i}", [128, 1024], BF16) for i in range(2)]
    cbf = [K.sb(st, f"a3_cbf{i}", [128, 1024], BF16) for i in range(2)]
    qxp = [K.sb(st, f"a3_qxp{i}", [128, 1024], BF16) for i in range(2)]
    zb = [K.sb(st, f"a3_zb{i}", [128, 1024], BF16) for i in range(2)]
    z1 = K.sb(st, "a3_z1", [128, 512], BF16)
    z2 = K.sb(st, "a3_z2", [128, 256], BF16)
    yslab = [K.sb(st, f"a3_ys{bq}", [128, 2048], BF16) for bq in range(NB)]
    ytst = K.sb(st, "a3_ytst", [128, 2048], BF16)
    cps = [K.ps(st, f"a3_cp{i}", [128, 1024]) for i in range(2)]
    tpsl = [K.ps(st, f"a3_tp{i}", [128, 128], BF16) for i in range(2)]

    t_ld = [0, 0]
    t_rev = [0, 0]
    t_raw = [0, 0]
    t_qvk_cons = [0, 0]
    t_mm = [0, 0]
    t_qc_done = [0, 0]
    t_cbf_cons = [0, 0]
    t_slabtpT = 0
    t_ytstd = 0
    t_tpev = [0, 0]

    def prefetch_skw(c):
        par = c % 2
        W(nc.sync, "sV", t_rev[par])
        t_raw[par] = I(nc.sync.dma_start(
            out=skwr[par][:, :], in_=AP(K.kcbf, c * 2176, [[1, 128], [1, 2048]])),
            "d2")

    def prefetch_qvk(n):
        par8 = n % 2
        W(nc.sync, "sV", t_qvk_cons[par8])
        for buf, dr in ((qc[par8], K.qtm), (vc[par8], K.vtm), (kc[par8], K.kptm)):
            t = I(nc.sync.dma_start(
                out=AP(_h(buf), 0, [[4096, 128], [1024, NB], [1, 1024]]),
                in_=AP(dr, n * 1024,
                       [[BEI, 128], [D * NI, NB], [1, 1024]])), "d3")
        t_ld[par8] = t

    prefetch_skw(0)
    prefetch_qvk(0)
    for c in range(128):
        par = c % 2
        par8 = (c // 8) % 2
        cb = (c % 8) * 128
        cl = c % 16
        cg = c // 16
        W(nc.vector, "d2", t_raw[par])
        W(nc.vector, "sT", t_mm[par])
        revv = AP(_h(skwr[par]), 2047, [[2048, 128], [-1, 2048]])
        t_rev[par] = I(nc.vector.tensor_copy(skw[par][:, :], revv), "sV")
        if c + 1 < 128:
            prefetch_skw(c + 1)
        if c % 8 == 0 and c // 8 + 1 < 16:
            prefetch_qvk(c // 8 + 1)
        for bq in range(NB):
            cbpar = (c * NB + bq) % 2
            W(nc.vector, "d3", t_ld[par8])
            W(nc.vector, "sT", t_mm[cbpar])
            in0 = AP(_h(kc[par8]), bq * 1024 + cb,
                     [[4096, 128], [1, NI], [NI, 8], [0, 8]])
            in1 = AP(_h(vc[par8]), bq * 1024 + cb,
                     [[4096, 128], [1, NI], [0, 8], [NI, 8]])
            t_kvb = I(nc.vector.tensor_tensor(kv[cbpar][:, :], in0, in1, ALU.mult),
                      "sV")
            W(nc.tensor, "sV", max(t_kvb, t_rev[par], t_qc_done[cbpar]))
            for dd in range(NI):
                s0 = dd * 64
                lt = skw[par][:, (NI - 1 - dd) * 128:(NI - dd) * 128]
                if s0 < 512:
                    nc.tensor.matmul(cps[cbpar][:, s0:512], lt,
                                     kv[cbpar][:, 0:512 - s0],
                                     start=(dd == 0), stop=(dd == 7))
                    mm = nc.tensor.matmul(cps[cbpar][:, 512:1024], lt,
                                          kv[cbpar][:, 512 - s0:1024 - s0],
                                          start=(dd == 0), stop=(dd == NI - 1))
                else:
                    mm = nc.tensor.matmul(cps[cbpar][:, s0:1024], lt,
                                          kv[cbpar][:, 0:1024 - s0],
                                          start=False, stop=(dd == NI - 1))
            t_mm[cbpar] = I(mm, "sT")
            W(nc.scalar, "sT", t_mm[cbpar])
            W(nc.scalar, "sV", t_cbf_cons[cbpar])
            t_cbf = I(nc.scalar.copy(cbf[cbpar][:, :], cps[cbpar][:, :]), "sA")
            qsrc = AP(_h(qc[par8]), bq * 1024 + cb,
                      [[4096, 128], [1, NI], [NI, 8], [0, 8]])
            W(nc.vector, "sA", t_cbf)
            t_qx = I(nc.vector.tensor_copy(qxp[cbpar][:, :], qsrc), "sV")
            t_qvk_cons[par8] = t_qx
            nc.vector.tensor_mul(zb[cbpar][:, :], cbf[cbpar][:, :],
                                 qxp[cbpar][:, :])
            t_qc_done[cbpar] = K.cnt["sV"] + 4
            i0a = AP(_h(zb[cbpar]), 0, [[1024, 128], [64, NI], [8, 4], [1, 8]])
            i0b = AP(_h(zb[cbpar]), 32, [[1024, 128], [64, NI], [8, 4], [1, 8]])
            o1 = AP(_h(z1), 0, [[512, 128], [32, NI], [8, 4], [1, 8]])
            nc.vector.tensor_tensor(o1, i0a, i0b, ALU.add)
            i1a = AP(_h(z1), 0, [[512, 128], [32, NI], [8, 2], [1, 8]])
            i1b = AP(_h(z1), 16, [[512, 128], [32, NI], [8, 2], [1, 8]])
            o2 = AP(_h(z2), 0, [[256, 128], [16, NI], [8, 2], [1, 8]])
            nc.vector.tensor_tensor(o2, i1a, i1b, ALU.add)
            i2a = AP(_h(z2), 0, [[256, 128], [16, NI], [1, 8]])
            i2b = AP(_h(z2), 8, [[256, 128], [16, NI], [1, 8]])
            if cl == 0 and bq == 0:
                W(nc.vector, "sT", t_slabtpT)
            oy = AP(_h(yslab[bq]), cl * 8, [[2048, 128], [128, NI], [1, 8]])
            t4 = I(nc.vector.tensor_tensor(oy, i2a, i2b, ALU.add), "sV", 4)
            t_cbf_cons[cbpar] = t4
        if cl == 15:
            W(nc.tensor, "sV")
            for bq in range(NB):
                for Ib in range(NI):
                    tpar = Ib % 2
                    W(nc.tensor, "sV", t_tpev[tpar])
                    tt = nc.tensor.transpose(
                        tpsl[tpar][:, :], yslab[bq][:, Ib * 128:(Ib + 1) * 128],
                        K.ident_sb[:, :])
                    t_tp = I(tt, "sT")
                    t_slabtpT = t_tp
                    W(nc.vector, "sT", t_tp)
                    if Ib == 0:
                        W(nc.vector, "dG", t_ytstd)
                    t_tpev[tpar] = I(nc.vector.tensor_copy(
                        ytst[:, Ib * 128:(Ib + 1) * 128], tpsl[tpar][:, :]), "sV")
                W(nc.gpsimd, "sV", max(t_tpev))
                t_ytstd = I(nc.gpsimd.dma_start(
                    out=K.ytr[bq * 8 + cg, :, :], in_=ytst[:, :]), "dG")
    if K.dbg_taps:
        W(nc.gpsimd, "dG", t_ytstd)
        I(nc.gpsimd.dma_start(out=K.taps["tap_ytr"][:, :, :], in_=K.ytr[:, :, :]),
          "dG")
    st.close()


def phase_A4(K):
    nc, I, W = K.nc, K.I, K.W
    nc.sync.wait_ge(K.sem["dG"], K.cnt["dG"])
    st = ExitStack()
    wop = K.sb(st, "a4_wop", [128, 1024], F32)
    wopb = K.sb(st, "a4_wopb", [128, NDC * 1024], BF16)
    ysb = K.sb(st, "a4_ysb", [128, 8 * L], BF16)
    uch = K.sb(st, "a4_uch", [128, L], F32)
    x2sb = K.sb(st, "a4_x2", [128, NDC * L], F32)
    xbfs = K.sb(st, "a4_xbfs", [128, 512], BF16)
    xsq = K.sb(st, "a4_xsq", [128, 512], BF16)
    t1b = K.sb(st, "a4_t1b", [128, 2 * 512], F32)
    h2s = K.sb(st, "a4_h2s", [128, 512], F32)
    m_sb = K.sb(st, "a4_m", [1, L], F32)
    s2_sb = K.sb(st, "a4_s2", [1, L], F32)
    rowt = K.sb(st, "a4_rt", [1, L], F32)
    rowt2 = K.sb(st, "a4_rt2", [1, L], F32)
    stps = K.ps(st, "a4_st", [128, 512])
    abps = [K.ps(st, f"a4_ab{i}", [128, 512]) for i in range(2)]
    wps = [K.ps(st, f"a4_wp{i}", [128, 512]) for i in range(2)]

    t_wc = 0
    for ch in range(NDC):
        W(nc.sync, "sV", t_wc)
        t_l = I(nc.sync.dma_start(
            out=wop[:, :], in_=K.woTp[ch * 128:(ch + 1) * 128, :]), "d3")
        W(nc.vector, "d3", t_l)
        t_wc = I(nc.vector.tensor_copy(
            wopb[:, ch * 1024:(ch + 1) * 1024], wop[:, :]), "sV")

    nmax = 0
    t_ev = [0, 0]
    t_abcons = [0, 0]
    t_woMM = 0
    t_spill = 0
    for b in range(NB):
        W(nc.sync, "sT", max(t_woMM, K._blhs_cons))
        t_y = 0
        for cg in range(8):
            t_y = I(nc.sync.dma_start(out=ysb[:, cg * L:(cg + 1) * L],
                                      in_=K.ytr[b * 8 + cg, :, :]), "dS")
        t_bl = I(nc.sync.dma_start(
            out=K.blhs[:, :], in_=AP(K.blhsdr, (b * 2 + 1) * D,
                                     [[NB * 2 * D, 2], [1, D]])), "dS")
        t_uc = [0, 0]
        t_x2 = 0
        W(nc.vector, "d4", t_spill)
        for fc in range(NDC):
            par = 0
            W(nc.sync, "sV", t_uc[par])
            t_u = I(nc.sync.dma_start(out=uch[:, :],
                                      in_=K.u_fm[b, fc * 128:(fc + 1) * 128, :]),
                    "d2")
            for s in range(NS):
                spar = (fc * NS + s) % 2
                W(nc.tensor, "dS", t_y)
                W(nc.tensor, "sV", max(t_wc, t_ev[spar]))
                for cg in range(8):
                    mm = nc.tensor.matmul(
                        wps[spar][:, :],
                        wopb[:, cg * 1024 + fc * 128: cg * 1024 + (fc + 1) * 128],
                        ysb[:, cg * L + s * 512: cg * L + (s + 1) * 512],
                        start=(cg == 0), stop=(cg == 7))
                t_woMM = t_mm = I(mm, "sT")
                W(nc.scalar, "sT", t_mm)
                W(nc.scalar, "sV", t_ev[spar])
                rev = AP(wps[spar], 127, [[512, 128], [128, 4], [-1, 128]])
                t_t1 = I(nc.scalar.activation(
                    out=t1b[:, spar * 512:(spar + 1) * 512], in_=rev,
                    func=AF.Identity, scale=vcol(K, b, 2, fc),
                    bias=K.gabo[:, b * NDC + fc: b * NDC + fc + 1]), "sA")
                W(nc.vector, "sA", t_t1)
                W(nc.vector, "d2", t_u)
                t_x2 = I(nc.vector.tensor_add(
                    x2sb[:, fc * L + s * 512: fc * L + (s + 1) * 512],
                    t1b[:, spar * 512:(spar + 1) * 512],
                    uch[:, s * 512:(s + 1) * 512]), "sV")
                t_ev[spar] = t_x2
                t_uc[par] = t_x2

        def xsrc(dc, s):
            W(nc.vector, "sA")
            t = I(nc.vector.tensor_copy(
                xbfs[:, :], x2sb[:, dc * L + s * 512: dc * L + (s + 1) * 512]),
                "sV")
            W(nc.scalar, "sV", t)
            return xbfs[:, :]
        ln_stats(K, xsrc, m_sb, s2_sb, stps, xsq)
        row_stats(K, m_sb, s2_sb, rowt, rowt2)
        W(nc.vector, "dS", t_bl)
        rows_to_bf(K)
        blhs_to_bf(K)
        W(nc.gpsimd, "sV")
        I(nc.gpsimd.dma_start(out=K.st2dr[b, 0:1, :], in_=K.rstd_sb[:1, :]), "dG")
        t_st2d = I(nc.gpsimd.dma_start(out=K.st2dr[b, 1:2, :], in_=K.brhs[0:1, :]),
                   "dG")
        K._blhs_cons = max(K._blhs_cons, 0)
        W(nc.vector, "dG", t_st2d)
        for s in range(NS):
            for dc in range(NDC):
                W(nc.tensor, "sV", t_abcons[0])
                t_ab = ab_maps(K, dc, s, abps[0], abps[1])
                W(nc.vector, "sT", t_ab)
                xsl = x2sb[:, dc * L + s * 512: dc * L + (s + 1) * 512]
                nc.vector.scalar_tensor_tensor(h2s[:, :], xsl, 1.0,
                                               abps[0][:, :], ALU.mult,
                                               ALU.mult)
                nc.vector.tensor_add(h2s[:, :], h2s[:, :], abps[1][:, :])
                t_r = I(nc.vector.tensor_reduce(
                    K.redcg[:, nmax:nmax + 1], h2s[:, :], AX.X, ALU.max,
                    apply_absolute_value=True), "sV", 3)
                nmax += 1
                t_abcons[0] = t_r
        W(nc.gpsimd, "sV")
        for dc in range(NDC):
            t_spill = I(nc.gpsimd.dma_start(
                out=K.x2dr[b, dc * 128:(dc + 1) * 128, :],
                in_=x2sb[:, dc * L:(dc + 1) * L]), "d4")
        if K.dbg_taps:
            W(nc.gpsimd, "d4", t_spill)
            I(nc.gpsimd.dma_start(out=K.taps["tap_x2"][b, :, :],
                                  in_=K.x2dr[b, :, :]), "dG")
    finalize_scale(K, K.redcg, nmax, K.ar0i, K.ar0o, [(0, 1)], 1)
    st.close()


def finalize_scale(K, redc, nidx, ari, aro, colpairs, n_sc):
    nc, I, W = K.nc, K.I, K.W
    v = nc.vector
    stf = ExitStack()
    K._fsn = getattr(K, "_fsn", 0) + 1
    rps = K.ps(stf, f"fs_rps{K._fsn}", [1, 128])
    bcps = K.ps(stf, f"fs_bcps{K._fsn}", [128, 1])
    reds = redc if isinstance(redc, list) else [redc]
    t = 0
    for i, rc in enumerate(reds):
        W(v, "sT", K._fin_t)
        t = I(v.tensor_reduce(K.red1[:, :], rc[:, :nidx], AX.X, ALU.max,
                              apply_absolute_value=True), "sV")
        W(nc.tensor, "sV", t)
        K._fin_t = tt = I(nc.tensor.transpose(rps[:, :], K.red1[:, :],
                                              K.identf_sb[:, :]), "sT")
        W(v, "sT", tt)
        t = I(v.tensor_reduce(K.scrow[:1, i:i + 1], rps[:1, :], AX.X, ALU.max,
                              apply_absolute_value=True), "sV")
    W(nc.gpsimd, "sV", t)
    td = I(nc.gpsimd.dma_start(out=ari[:, :], in_=K.scrow[:1, 0:8]), "dG")
    W(nc.gpsimd, "dG", td)
    tcc = I(nc.gpsimd.collective_compute(
        "AllReduce", ALU.max, replica_groups=K.rg,
        ins=[ari.ap().opt()], outs=[aro.ap().opt()]), "sCC")
    W(nc.sync, "sCC", tcc)
    tl = I(nc.sync.dma_start(out=K.scrow[:1, 4:4 + n_sc], in_=aro[:, 0:n_sc]), "dS")
    W(v, "dS", tl)
    t = I(v.tensor_scalar(out=K.scrow[:1, 4:4 + n_sc], in0=K.scrow[:1, 4:4 + n_sc],
                          scalar1=float(np.float32(1 / 32767.0)), scalar2=1e-12,
                          op0=ALU.mult, op1=ALU.max), "sV")
    W(nc.scalar, "sV", t)
    t = I(nc.scalar.activation(out=K.scrow[:1, 8:8 + n_sc],
                               in_=K.scrow[:1, 4:4 + n_sc],
                               func=AF.Identity), "sA")
    W(v, "sA", t)
    t = I(v.reciprocal(K.scrow[:1, 6:6 + n_sc], K.scrow[:1, 8:8 + n_sc]), "sV")
    for i, (icol, scol) in enumerate(colpairs):
        for src_off, dcol in ((6 + i, icol), (4 + i, scol)):
            W(nc.tensor, "sV", t)
            tm = I(nc.tensor.matmul(bcps[:, :], K.onescol_sb[:, :],
                                    K.scrow[:1, src_off:src_off + 1],
                                    start=True, stop=True), "sT")
            W(v, "sT", tm)
            t = I(v.tensor_copy(K.scales[:, dcol:dcol + 1], bcps[:, :]), "sV")
    if K.dbg_taps:
        W(nc.gpsimd, "sV", t)
        I(nc.gpsimd.dma_start(out=K.taps["tap_scales"][:, :], in_=K.scales[:, :]),
          "dG")
    W(nc.tensor, "sV", t)
    stf.close()


def phase_B(K):
    nc, I, W = K.nc, K.I, K.W
    BD = K.upto if K.upto.startswith("BD") else ""
    skip_prep = BD in ("BDmm",)
    stGU = ExitStack()
    gstage_lo = K.sb(stGU, "b_gstlo", [128, 2 * L], BF16)
    wgT = K.sb(stGU, "b_wgT", [128, NDC * FF], BF16)
    wuT = K.sb(stGU, "b_wuT", [128, NDC * FF], BF16)

    st = ExitStack()
    tpsl = [K.ps(st, f"bp_tp{i}", [128, 128], BF16) for i in range(2)]
    t_tpev = [0, 0]
    for (srcdr, dstsb, nch, R, tag) in ([] if skip_prep else
                                        [(K.wg, wgT, 24, 1024, "g"),
                                         (K.wu, wuT, 24, 1024, "u")][
                                            0:(1 if BD == "BDmm2g" else 2)]):
        nblk = R // 32
        stj = ExitStack()
        wraw = K.sb(stj, f"bp_wraw{tag}", [128, 2 * R], F32)
        blkm = K.sb(stj, f"bp_blkm{tag}", [128, nblk], F32)
        sblk = K.sb(stj, f"bp_sblk{tag}", [128, nblk], F32)
        sblk2 = K.sb(stj, f"bp_sblk2{tag}", [128, nblk], F32)
        iblk = K.sb(stj, f"bp_iblk{tag}", [128, nblk], F32)
        tf = K.sb(stj, f"bp_tf{tag}", [128, R], F32)
        wqb = K.sb(stj, f"bp_wqb{tag}", [128, R], BF16)
        t_wl = [0, 0]
        t_cons = [0, 0]
        for ch in range(nch):
            par = ch % 2
            W(nc.sync, "sV", t_cons[par])
            if tag == "d":
                srcsl = srcdr[(ch // 2) * 128:(ch // 2 + 1) * 128,
                              (ch % 2) * 1536:(ch % 2 + 1) * 1536]
            else:
                srcsl = srcdr[ch * 128:(ch + 1) * 128, :]
            t_wl[par] = I(nc.sync.dma_start(
                out=wraw[:, par * R: par * R + R], in_=srcsl), "dS")
            W(nc.vector, "dS", t_wl[par])
            W(nc.vector, "sT")
            wr = wraw[:, par * R: par * R + R]
            wv = wr.rearrange("p (nb t) -> p nb t", t=32)
            nc.vector.tensor_reduce(blkm[:, :nblk], wv, AX.X, ALU.max,
                                    apply_absolute_value=True)
            t_ts = I(nc.vector.tensor_scalar(out=sblk[:, :nblk], in0=blkm[:, :nblk],
                                             scalar1=float(np.float32(1 / 7.0)),
                                             scalar2=1e-12, op0=ALU.mult,
                                             op1=ALU.max), "sV", 2)
            W(nc.scalar, "sV", t_ts)
            t_cp = I(nc.scalar.activation(out=sblk2[:, :nblk], in_=sblk[:, :nblk],
                                          func=AF.Identity), "sA")
            W(nc.vector, "sA", t_cp)
            nc.vector.reciprocal(iblk[:, :nblk], sblk2[:, :nblk])
            ib = AP(_h(iblk), 0, [[nblk, 128], [1, nblk], [0, 32]])
            sb_ = AP(_h(sblk), 0, [[nblk, 128], [1, nblk], [0, 32]])
            nc.vector.tensor_tensor(tf[:, :R], wr, ib, ALU.mult)
            nc.vector.tensor_scalar(out=tf[:, :R], in0=tf[:, :R], scalar1=CMAG,
                                    scalar2=CMAG, op0=ALU.add, op1=ALU.subtract)
            t_cons[par] = I(nc.vector.tensor_tensor(wqb[:, :R], tf[:, :R], sb_,
                                                    ALU.mult), "sV", 4)
            W(nc.tensor, "sV", t_cons[par])
            for ti in range(R // 128):
                tpar = ti % 2
                W(nc.tensor, "sV", t_tpev[tpar])
                tt = nc.tensor.transpose(tpsl[tpar][:, :],
                                         wqb[:, ti * 128:(ti + 1) * 128],
                                         K.ident_sb[:, :])
                t_tp = I(tt, "sT")
                W(nc.vector, "sT", t_tp)
                if tag in ("g", "u"):
                    dst = dstsb[:, ti * FF + ch * 128: ti * FF + (ch + 1) * 128]
                else:
                    tig = (ch % 2) * 12 + ti
                    chg = ch // 2
                    dst = dstsb[:, tig * 1024 + chg * 128: tig * 1024 + (chg + 1) * 128]
                t_tpev[tpar] = I(nc.vector.tensor_copy(dst, tpsl[tpar][:, :]), "sV")
        W(nc.sync, "sV")
        stj.close()
    st.close()
    if K.upto == "Bp" or BD == "BDp":
        stGU.close()
        return

    nc.sync.wait_ge(K.sem["dG"], K.cnt["dG"])
    nc.sync.wait_ge(K.sem["d4"], K.cnt["d4"])
    st = ExitStack()
    x2ch = K.sb(st, "b1_x2ch", [128, 2 * L], F32)
    h2q = K.sb(st, "b1_h2q", [128, NDC * L], BF16)
    h2s = K.sb(st, "b1_h2s", [128, 512], F32)
    gstage = gstage_lo
    abps = [K.ps(st, f"b1_ab{i}", [128, 512]) for i in range(2)]
    gups = [K.ps(st, f"b1_gu{i}", [128, 512]) for i in range(2)]
    ng, nu = 0, 0
    t_gup_red = [0, 0]
    t_gst_d = [0, 0]
    t_act = [0, 0]
    t_abc = [0, 0]
    t_q = 0
    t_gupMM = 0
    nb_lim = 1 if BD else NB
    nic_lim = 2 if BD else NIC
    for b in range(nb_lim):
        if BD.startswith("BDmm"):
            break
        W(nc.sync, "sT", max(K._blhs_cons, t_gupMM))
        t_bl = I(nc.sync.dma_start(
            out=K.blhs[:, :], in_=AP(K.blhsdr, (b * 2 + 1) * D,
                                     [[NB * 2 * D, 2], [1, D]])), "dS")
        I(nc.sync.dma_start(out=K.rstd_sb[:1, :], in_=K.st2dr[b, 0:1, :]), "dS")
        t_st2 = I(nc.sync.dma_start(out=K.brhs[0:1, :], in_=K.st2dr[b, 1:2, :]),
                  "dS")
        W(nc.vector, "sT")
        W(nc.vector, "dS", t_st2)
        rows_to_bf(K)
        blhs_to_bf(K)
        t_xc = [0, 0]
        for dc in range(0 if BD in ("BDonly", "BDmm") else NDC):
            par = dc % 2
            W(nc.sync, "sV", t_xc[par])
            t_x = I(nc.sync.dma_start(out=x2ch[:, par * L:(par + 1) * L],
                                      in_=K.x2dr[b, dc * 128:(dc + 1) * 128, :]),
                    "d3")
            for s in range(NS):
                W(nc.tensor, "dS", max(t_bl, t_st2))
                W(nc.tensor, "sV", t_abc[0])
                t_ab = ab_maps(K, dc, s, abps[0], abps[1])
                W(nc.vector, "sT", t_ab)
                W(nc.vector, "d3", t_x)
                xsl = x2ch[:, par * L + s * 512: par * L + (s + 1) * 512]
                nc.vector.scalar_tensor_tensor(h2s[:, :], xsl, 1.0,
                                               abps[0][:, :], ALU.mult,
                                               ALU.mult)
                nc.vector.tensor_add(h2s[:, :], h2s[:, :], abps[1][:, :])
                nc.vector.tensor_scalar(out=h2s[:, :], in0=h2s[:, :],
                                        scalar1=K.scales[:, 0:1], scalar2=CMAG,
                                        op0=ALU.mult, op1=ALU.add)
                t_q = I(nc.vector.tensor_scalar(
                    out=h2q[:, dc * L + s * 512: dc * L + (s + 1) * 512],
                    in0=h2s[:, :], scalar1=CMAG, scalar2=K.scales[:, 1:2],
                    op0=ALU.subtract, op1=ALU.mult), "sV", 4)
                t_abc[0] = t_q
                t_xc[par] = t_q
        if K.upto == "B1h" or BD == "BDh":
            continue
        if BD in ("BDonly", "BDmm"):
            t_q = K.cnt["sV"]
        for Ic in range(nic_lim):
            if K.upto == "B1bar":
                K.bar()
            for it, (wsb_, drt, redc) in enumerate(((wgT, K.gatedr, K.redcg),
                                                    (wuT, K.updr, K.redcu))):
                for s in range(NS):
                    W(nc.tensor, "sV", t_q)
                    W(nc.tensor, "sA", t_act[it])
                    for dc in range(NDC):
                        lhsw = (K.ident_sb[:, :] if BD in ("BDident", "BDmm") else
                                wsb_[:, dc * FF + Ic * 128: dc * FF + (Ic + 1) * 128])
                        mm = nc.tensor.matmul(
                            gups[it][:, :], lhsw,
                            h2q[:, dc * L + s * 512: dc * L + (s + 1) * 512],
                            start=(dc == 0), stop=(dc == NDC - 1))
                    t_gupMM = t_mm = I(mm, "sT")
                    W(nc.vector, "sT", t_mm)
                    if s == 0:
                        W(nc.vector, "d4" if it == 0 else "d5", t_gst_d[it])
                    t_act[it] = I(nc.vector.tensor_copy(
                        gstage[:, it * L + s * 512: it * L + (s + 1) * 512],
                        gups[it][:, :]), "sA")
                idx = ng if it == 0 else nu
                t_gup_red[it] = I(nc.vector.tensor_reduce(
                    redc[:, idx:idx + 1], gstage[:, it * L:(it + 1) * L],
                    AX.X, ALU.max, apply_absolute_value=True), "sV")
                if it == 0:
                    ng += 1
                else:
                    nu += 1
                W(nc.gpsimd, "sA", t_act[it])
                W(nc.gpsimd, "sV", t_gup_red[it])
                t_gst_d[it] = I(nc.gpsimd.dma_start(
                    out=drt[b, Ic * 128:(Ic + 1) * 128, :],
                    in_=gstage[:, it * L:(it + 1) * L]), "d4" if it == 0 else "d5")
    if BD in ("BDmm", "BDmm2", "BDmm2g", "BDmm3", "BDmm4", "BDmm5", "BDmm6", "BDmm7", "BDmm8"):
        if BD in ("BDmm3", "BDmm4", "BDmm5", "BDmm6", "BDmm7", "BDmm8"):
            t_bl = I(nc.sync.dma_start(
                out=K.blhs[:, :], in_=AP(K.blhsdr, D, [[NB * 2 * D, 2], [1, D]])),
                "dS")
            I(nc.sync.dma_start(out=K.rstd_sb[:1, :], in_=K.st2dr[0, 0:1, :]), "dS")
            t_st2 = I(nc.sync.dma_start(out=K.brhs[0:1, :], in_=K.st2dr[0, 1:2, :]),
                      "dS")
        if BD in ("BDmm4", "BDmm5", "BDmm6", "BDmm7", "BDmm8"):
            W(nc.vector, "dS", t_st2)
            rows_to_bf(K)
            blhs_to_bf(K)
        t_act2 = [0, 0]
        for Ic in range(2):
            for it in range(2):
                for sx in range(NS):
                    W(nc.tensor, "sA", t_act2[it])
                    for dc in range(NDC):
                        lw = ((wgT if it == 0 else wuT)[:, dc * FF + Ic * 128:
                                                        dc * FF + (Ic + 1) * 128]
                              if BD == "BDmm8" else K.ident_sb[:, :])
                        mm = nc.tensor.matmul(
                            gups[it][:, :], lw,
                            h2q[:, dc * L + sx * 512: dc * L + (sx + 1) * 512],
                            start=(dc == 0), stop=(dc == NDC - 1))
                    t_mm = I(mm, "sT")
                    W(nc.vector, "sT", t_mm)
                    if BD in ("BDmm6", "BDmm7", "BDmm8"):
                        t_act2[it] = I(nc.vector.tensor_copy(
                            gstage[:, it * L + sx * 512: it * L + (sx + 1) * 512],
                            gups[it][:, :]), "sA")
                    else:
                        t_act2[it] = I(nc.vector.tensor_copy(h2s[:, :],
                                                             gups[it][:, :]), "sA")
                if BD in ("BDmm5", "BDmm7", "BDmm8"):
                    I(nc.vector.tensor_reduce(
                        K.redcg[:, Ic * 2 + it: Ic * 2 + it + 1],
                        gstage[:, it * L:(it + 1) * L], AX.X, ALU.max,
                        apply_absolute_value=True), "sV")
    if not BD and K.upto not in ("B1h", "B1m", "B1m0", "B1m1", "B1m2", "B1mA",
                                 "B1bar"):
        finalize_scale(K, [K.redcg, K.redcu], 96, K.ar1i, K.ar1o,
                       [(2, 3), (4, 5)], 2)
    st.close()
    if BD or K.upto in ("B1h", "B1m", "B1m0", "B1m1", "B1m2", "B1mA", "B1bar"):
        stGU.close()
        return
    if K.upto == "B1":
        stGU.close()
        return

    nc.sync.wait_ge(K.sem["d4"], K.cnt["d4"])
    nc.sync.wait_ge(K.sem["d5"], K.cnt["d5"])
    st = ExitStack()
    gch = K.sb(st, "b2_gch", [128, 2 * L], BF16)
    gq = K.sb(st, "b2_gq", [128, L], BF16)
    qsc = K.sb(st, "b2_qsc", [128, L], F32)
    sig = K.sb(st, "b2_sig", [128, L], F32)
    g2b = K.sb(st, "b2_g2", [128, 2 * L], BF16)
    n2 = 0
    t_g2d = [0, 0]
    t_cons = [0, 0]
    for b in range(NB):
        for Ic in range(NIC):
            par = (b * NIC + Ic) % 2
            W(nc.sync, "sV", t_cons[par])
            t_in = I(nc.sync.dma_start(
                out=gch[:, par * L:(par + 1) * L],
                in_=K.gatedr[b, Ic * 128:(Ic + 1) * 128, :]), "d2")
            W(nc.vector, "d2", t_in)
            gs = gch[:, par * L:(par + 1) * L]
            nc.vector.tensor_scalar(out=qsc[:, :], in0=gs, scalar1=K.scales[:, 2:3],
                                    scalar2=CMAG, op0=ALU.mult, op1=ALU.add)
            t_gq = I(nc.vector.tensor_scalar(
                out=gq[:, :], in0=qsc[:, :], scalar1=CMAG, scalar2=K.scales[:, 3:4],
                op0=ALU.subtract, op1=ALU.mult), "sV", 2)
            W(nc.scalar, "sV", t_gq)
            t_sg = I(nc.scalar.activation(out=sig[:, :], in_=gq[:, :],
                                          func=AF.Sigmoid), "sA")
            W(nc.vector, "sA", t_sg)
            nc.vector.tensor_scalar(out=sig[:, :], in0=sig[:, :], scalar1=65536.0,
                                    scalar2=CMAG, op0=ALU.mult, op1=ALU.add)
            nc.vector.tensor_scalar(out=sig[:, :], in0=sig[:, :], scalar1=CMAG,
                                    scalar2=65535.0, op0=ALU.subtract, op1=ALU.min)
            W(nc.vector, "d4", t_g2d[par])
            nc.vector.scalar_tensor_tensor(g2b[:, par * L:(par + 1) * L], gq[:, :],
                                           float(np.float32(1 / 65536.0)),
                                           sig[:, :], ALU.mult, ALU.mult)
            t_cons[par] = I(nc.vector.tensor_reduce(
                K.redcg[:, n2:n2 + 1], g2b[:, par * L:(par + 1) * L], AX.X,
                ALU.max, apply_absolute_value=True), "sV", 4)
            n2 += 1
            W(nc.gpsimd, "sV", t_cons[par])
            t_g2d[par] = I(nc.gpsimd.dma_start(
                out=K.g2dr[b, Ic * 128:(Ic + 1) * 128, :],
                in_=g2b[:, par * L:(par + 1) * L]), "d4")
    finalize_scale(K, K.redcg, n2, K.ar2i, K.ar2o, [(6, 7)], 1)
    st.close()
    if K.upto == "B2":
        stGU.close()
        return

    nc.sync.wait_ge(K.sem["d4"], K.cnt["d4"])
    nc.sync.wait_ge(K.sem["d5"], K.cnt["d5"])
    st = ExitStack()
    g2ch = K.sb(st, "b3_g2ch", [128, 2 * L], BF16)
    upch = K.sb(st, "b3_upch", [128, 2 * L], BF16)
    oq = K.sb(st, "b3_oq", [128, L], BF16)
    qsc3 = K.sb(st, "b3_qsc", [128, L], F32)
    upq = K.sb(st, "b3_upq", [128, L], BF16)
    ob = K.sb(st, "b3_ob", [128, 2 * L], BF16)
    n3 = 0
    t_od = [0, 0]
    t_cons = [0, 0]
    for b in range(NB):
        for Ic in range(NIC):
            par = (b * NIC + Ic) % 2
            W(nc.sync, "sV", t_cons[par])
            I(nc.sync.dma_start(out=g2ch[:, par * L:(par + 1) * L],
                                in_=K.g2dr[b, Ic * 128:(Ic + 1) * 128, :]), "d2")
            t_in = I(nc.sync.dma_start(
                out=upch[:, par * L:(par + 1) * L],
                in_=K.updr[b, Ic * 128:(Ic + 1) * 128, :]), "d2")
            W(nc.vector, "d2", t_in)
            g2s = g2ch[:, par * L:(par + 1) * L]
            ups = upch[:, par * L:(par + 1) * L]
            nc.vector.tensor_scalar(out=qsc3[:, :], in0=g2s, scalar1=K.scales[:, 6:7],
                                    scalar2=CMAG, op0=ALU.mult, op1=ALU.add)
            nc.vector.tensor_scalar(out=oq[:, :], in0=qsc3[:, :], scalar1=CMAG,
                                    scalar2=K.scales[:, 7:8], op0=ALU.subtract,
                                    op1=ALU.mult)
            nc.vector.tensor_scalar(out=qsc3[:, :], in0=ups, scalar1=K.scales[:, 4:5],
                                    scalar2=CMAG, op0=ALU.mult, op1=ALU.add)
            nc.vector.tensor_scalar(out=upq[:, :], in0=qsc3[:, :], scalar1=CMAG,
                                    scalar2=K.scales[:, 5:6], op0=ALU.subtract,
                                    op1=ALU.mult)
            W(nc.vector, "d4", t_od[par])
            nc.vector.tensor_mul(ob[:, par * L:(par + 1) * L], oq[:, :], upq[:, :])
            t_cons[par] = I(nc.vector.tensor_reduce(
                K.redcg[:, n3:n3 + 1], ob[:, par * L:(par + 1) * L], AX.X,
                ALU.max, apply_absolute_value=True), "sV", 6)
            n3 += 1
            W(nc.gpsimd, "sV", t_cons[par])
            t_od[par] = I(nc.gpsimd.dma_start(
                out=K.odr[b, Ic * 128:(Ic + 1) * 128, :],
                in_=ob[:, par * L:(par + 1) * L]), "d4")
    finalize_scale(K, K.redcg, n3, K.ar3i, K.ar3o, [(8, 9)], 1)
    st.close()
    stGU.close()
    if K.upto == "B3":
        return

    nc.sync.wait_ge(K.sem["d4"], K.cnt["d4"])
    stW = ExitStack()
    wdT = K.sb(stW, "b_wdT", [128, NIC * 1024], BF16)
    stp = ExitStack()
    tpsl = [K.ps(stp, f"b4p_tp{i}", [128, 128], BF16) for i in range(2)]
    t_tpev = [0, 0]
    for (srcdr, dstsb, nch, R, tag) in [(K.wdn, wdT, 16, 1536, "d")]:
        nblk = R // 32
        stj = ExitStack()
        wraw = K.sb(stj, f"bp_wraw{tag}", [128, 2 * R], F32)
        blkm = K.sb(stj, f"bp_blkm{tag}", [128, nblk], F32)
        sblk = K.sb(stj, f"bp_sblk{tag}", [128, nblk], F32)
        sblk2 = K.sb(stj, f"bp_sblk2{tag}", [128, nblk], F32)
        iblk = K.sb(stj, f"bp_iblk{tag}", [128, nblk], F32)
        tf = K.sb(stj, f"bp_tf{tag}", [128, R], F32)
        wqb = K.sb(stj, f"bp_wqb{tag}", [128, R], BF16)
        t_wl = [0, 0]
        t_cons = [0, 0]
        for ch in range(nch):
            par = ch % 2
            W(nc.sync, "sV", t_cons[par])
            srcsl = srcdr[(ch // 2) * 128:(ch // 2 + 1) * 128,
                          (ch % 2) * 1536:(ch % 2 + 1) * 1536]
            t_wl[par] = I(nc.sync.dma_start(
                out=wraw[:, par * R: par * R + R], in_=srcsl), "d6")
            W(nc.vector, "d6", t_wl[par])
            W(nc.vector, "sT")
            wr = wraw[:, par * R: par * R + R]
            wv = wr.rearrange("p (nb t) -> p nb t", t=32)
            nc.vector.tensor_reduce(blkm[:, :nblk], wv, AX.X, ALU.max,
                                    apply_absolute_value=True)
            t_ts = I(nc.vector.tensor_scalar(out=sblk[:, :nblk], in0=blkm[:, :nblk],
                                             scalar1=float(np.float32(1 / 7.0)),
                                             scalar2=1e-12, op0=ALU.mult,
                                             op1=ALU.max), "sV", 2)
            W(nc.scalar, "sV", t_ts)
            t_cp = I(nc.scalar.activation(out=sblk2[:, :nblk], in_=sblk[:, :nblk],
                                          func=AF.Identity), "sA")
            W(nc.vector, "sA", t_cp)
            nc.vector.reciprocal(iblk[:, :nblk], sblk2[:, :nblk])
            ib = AP(_h(iblk), 0, [[nblk, 128], [1, nblk], [0, 32]])
            sb_ = AP(_h(sblk), 0, [[nblk, 128], [1, nblk], [0, 32]])
            nc.vector.tensor_tensor(tf[:, :R], wr, ib, ALU.mult)
            nc.vector.tensor_scalar(out=tf[:, :R], in0=tf[:, :R], scalar1=CMAG,
                                    scalar2=CMAG, op0=ALU.add, op1=ALU.subtract)
            t_cons[par] = I(nc.vector.tensor_tensor(wqb[:, :R], tf[:, :R], sb_,
                                                    ALU.mult), "sV", 4)
            W(nc.tensor, "sV", t_cons[par])
            for ti in range(R // 128):
                tpar = ti % 2
                W(nc.tensor, "sV", t_tpev[tpar])
                tt = nc.tensor.transpose(tpsl[tpar][:, :],
                                         wqb[:, ti * 128:(ti + 1) * 128],
                                         K.ident_sb[:, :])
                t_tp = I(tt, "sT")
                W(nc.vector, "sT", t_tp)
                tig = (ch % 2) * 12 + ti
                chg = ch // 2
                dst = dstsb[:, tig * 1024 + chg * 128: tig * 1024 + (chg + 1) * 128]
                t_tpev[tpar] = I(nc.vector.tensor_copy(dst, tpsl[tpar][:, :]), "sV")
        W(nc.sync, "sV")
        stj.close()
    stp.close()
    st = ExitStack()
    oqsb = K.sb(st, "b4_oq", [128, NIC * L], BF16)
    och = K.sb(st, "b4_och", [128, L], BF16)
    x2ch = K.sb(st, "b4_x2", [128, L], F32)
    qsc4 = x2ch
    t1b = K.sb(st, "b4_t1b", [128, 2 * 512], F32)
    ost = K.sb(st, "b4_ost", [128, L], F32)
    dnps = [K.ps(st, f"b4_dn{i}", [128, 512]) for i in range(2)]
    t_in = [0, 0]
    t_oq = 0
    t_dnMM = 0
    t_ostd = [0, 0]
    t_ev = [0, 0]
    t_t1c = 0
    for b in range(NB):
        W(nc.vector, "sT", t_dnMM)
        for Ic in range(NIC):
            par = 0
            W(nc.sync, "sV", t_in[par])
            t_l = I(nc.sync.dma_start(out=och[:, :],
                                      in_=K.odr[b, Ic * 128:(Ic + 1) * 128, :]),
                    "d2")
            W(nc.vector, "d2", t_l)
            nc.vector.tensor_scalar(out=qsc4[:, :], in0=och[:, :],
                                    scalar1=K.scales[:, 8:9], scalar2=CMAG,
                                    op0=ALU.mult, op1=ALU.add)
            t_oq = I(nc.vector.tensor_scalar(
                out=oqsb[:, Ic * L:(Ic + 1) * L], in0=qsc4[:, :],
                scalar1=CMAG, scalar2=K.scales[:, 9:10], op0=ALU.subtract,
                op1=ALU.mult), "sV", 2)
            t_in[par] = t_oq
        if K.dbg_taps:
            W(nc.gpsimd, "sV", t_oq)
            I(nc.gpsimd.dma_start(out=K.taps["tap_oq"][b, :, :], in_=oqsb[:, :]),
              "dG")
        t_x2c = [0, 0]
        for dc in range(NDC):
            par = dc % 2
            W(nc.sync, "sV", max(t_x2c[0], t_oq))
            t_x = I(nc.sync.dma_start(out=x2ch[:, :],
                                      in_=K.x2dr[b, dc * 128:(dc + 1) * 128, :]),
                    "d3")
            for s in range(NS):
                spar = s % 2
                W(nc.tensor, "sV", max(t_oq, t_ev[spar]))
                W(nc.tensor, "sA", t_t1c)
                for Ic in range(NIC):
                    mm = nc.tensor.matmul(
                        dnps[spar][:, :],
                        wdT[:, Ic * 1024 + dc * 128: Ic * 1024 + (dc + 1) * 128],
                        oqsb[:, Ic * L + s * 512: Ic * L + (s + 1) * 512],
                        start=(Ic == 0), stop=(Ic == NIC - 1))
                t_dnMM = t_mm = I(mm, "sT")
                W(nc.scalar, "sT", t_mm)
                W(nc.scalar, "sV", t_ev[spar])
                t_t1c = I(nc.scalar.activation(
                    out=t1b[:, spar * 512:(spar + 1) * 512], in_=dnps[spar][:, :],
                    func=AF.Identity, scale=vcol(K, b, 5, dc), bias=0.0), "sA")
                W(nc.vector, "sA", t_t1c)
                W(nc.vector, "d3", t_x)
                if s == 0:
                    W(nc.vector, "d4", t_ostd[0])
                t_o = I(nc.vector.tensor_add(
                    ost[:, s * 512:(s + 1) * 512],
                    t1b[:, spar * 512:(spar + 1) * 512],
                    x2ch[:, s * 512:(s + 1) * 512]), "sV")
                t_ev[spar] = t_o
                t_x2c[0] = t_o
            W(nc.gpsimd, "sV", t_o)
            t_ostd[0] = I(nc.gpsimd.dma_start(
                out=K.outfm[b, dc * 128:(dc + 1) * 128, :],
                in_=ost[:, :]), "d4")
    W(nc.gpsimd, "d4")
    W(nc.gpsimd, "dG")
    st.close()
    stW.close()


def _prep_inputs(inputs, r):
    f32 = np.float32
    u = np.ascontiguousarray(
        np.asarray(inputs["L0_attn_u"], f32)[4 * r:4 * r + 4].transpose(0, 2, 1))
    def _projp(w):
        wt = np.asarray(w, f32).T
        return np.ascontiguousarray(
            wt.reshape(8, 128, 8, 128).transpose(2, 1, 0, 3).reshape(D, D))

    wqT = _projp(inputs["L0_attn_Wq"])
    wkT = _projp(inputs["L0_attn_Wk"])
    wvT = _projp(inputs["L0_attn_Wv"])
    p = np.arange(1024)
    perm = (p % 8) * 128 + (p // 128) * 16 + ((p % 128) // 8)
    woTp = np.ascontiguousarray(np.asarray(inputs["L0_attn_Wo"], f32).T[perm])
    biases = np.ascontiguousarray(
        np.concatenate([np.asarray(inputs[k], f32).reshape(8, 128).T
                        for k in ("L0_attn_bq", "L0_attn_bk", "L0_attn_bv",
                                  "L0_attn_bo")], axis=1))
    kkpad = np.pad(np.asarray(inputs["L0_attn_ssm_k_kernel"], f32),
                   ((0, 0), (127, 1)))
    skD = np.ascontiguousarray(
        np.asarray(inputs["L0_attn_ssm_k_D"], f32).reshape(8, 128).T)
    kcpad = np.pad(np.asarray(inputs["L0_attn_ssm_kernel"], f32),
                   ((0, 0), (127, 1)))
    Dc = np.ascontiguousarray(np.asarray(inputs["L0_attn_D"], f32).reshape(128, 1))
    return {
        "u_fm": u, "wqT": wqT, "wkT": wkT, "wvT": wvT, "woTp": woTp,
        "biases": biases, "kkpad": kkpad, "skD": skD, "kcpad": kcpad, "Dc": Dc,
        "wg": np.asarray(inputs["L0_ffn_w_gate"], f32),
        "wu": np.asarray(inputs["L0_ffn_w_up"], f32),
        "wdn": np.asarray(inputs["L0_ffn_w_down"], f32),
        "wmod": np.asarray(inputs["L0_wmod"], f32),
        "c_own": np.ascontiguousarray(np.asarray(inputs["L0_c"], f32)[4 * r:4 * r + 4]),
    }


def kernel(**inputs):
    nc = build(dbg_taps=False)
    in_maps = [_prep_inputs(inputs, r) for r in range(8)]
    res = run_bass_kernel_spmd(nc, in_maps, core_ids=list(range(8)))
    out = np.concatenate([np.asarray(res.results[r]["outfm"]) for r in range(8)],
                         axis=0)
    return np.ascontiguousarray(out.transpose(0, 2, 1))


# version 04
# speedup vs baseline: 1.1173x; 1.1173x over previous
import numpy as np
import ml_dtypes
from contextlib import ExitStack

import concourse.bass as bass
import concourse.mybir as mybir
from concourse.ap import AP
from concourse.bass_utils import run_bass_kernel_spmd

F32 = mybir.dt.float32
BF16 = mybir.dt.bfloat16
AF = mybir.ActivationFunctionType
ALU = mybir.AluOpType
AX = mybir.AxisListType

CMAG = float(np.float32(1.5 * 2 ** 23))
NB, D, L, BL, NI = 4, 1024, 2048, 128, 16
NDC, NEC, NH = 8, 8, 8
FF, NIC, NS = 3072, 24, 4
EPS = 1e-5
BEI = NB * D * NI


def _h(t):
    return t.tensor if isinstance(t, AP) else t


class KB:
    def __init__(self, dbg_taps=False):
        nc = bass.Bass("TRN2", target_bir_lowering=False,
                       detect_race_conditions=False)
        self.nc = nc
        self.st = ExitStack()
        self.sem = {}
        self.cnt = {}
        for n in ("dS", "dG", "d2", "d3", "d4", "d5", "d6",
                  "d7", "d8", "d9", "dA", "dB",
                  "sT", "sV", "sA", "sCC"):
            self.sem[n] = self.st.enter_context(nc.semaphore(n))
            self.cnt[n] = 0
        self.dbg_taps = dbg_taps

    def I(self, instr, name, amt=None):
        if amt is None:
            amt = 16 if name.startswith("d") else 1
        if name.startswith("d") and self.cnt[name] > 0:
            instr._wait_ge(self.sem[name], self.cnt[name])
        instr.then_inc(self.sem[name], amt)
        self.cnt[name] += amt
        return self.cnt[name]

    def W(self, eng, name, val=None):
        v = self.cnt[name] if val is None else val
        if v > 0:
            eng.wait_ge(self.sem[name], v)

    def sb(self, stack, n, s, d):
        return stack.enter_context(self.nc.sbuf_tensor(n, s, d))

    def ps(self, stack, n, s, d=F32):
        return stack.enter_context(self.nc.psum_tensor(n, s, d))

    def bar(self):
        self.nc.all_engine_barrier()

    def joinD(self, eng):
        for n in ("dS", "dG", "d2", "d3", "d4", "d5", "d6",
                  "d7", "d8", "d9", "dA", "dB"):
            if self.cnt[n] > 0:
                eng.wait_ge(self.sem[n], self.cnt[n])


def build(dbg_taps=False, rg=None, upto='B'):
    K = KB(dbg_taps)
    K.rg = rg or [list(range(8))]
    K.upto = upto
    nc = K.nc

    di = lambda n, s: nc.dram_tensor(n, s, F32, kind="ExternalInput")
    K.u_fm = di("u_fm", [NB, D, L])
    K.wqT, K.wkT, K.wvT = di("wqT", [D, D]), di("wkT", [D, D]), di("wvT", [D, D])
    K.woTp = di("woTp", [D, D])
    K.biases = di("biases", [128, 32])
    K.kkpad = di("kkpad", [D, 2176])
    K.skD = di("skD", [128, 8])
    K.kcpad = di("kcpad", [128, 2176])
    K.Dc = di("Dc", [128, 1])
    K.wg, K.wu = di("wg", [FF, D]), di("wu", [FF, D])
    K.wdn = di("wdn", [D, FF])
    K.wmod = di("wmod", [D, 6 * D])
    K.c_own = di("c_own", [NB, D])
    K.outfm = nc.dram_tensor("outfm", [NB, D, L], F32, kind="ExternalOutput")

    ds = lambda n, s, d=F32: nc.dram_tensor(n, s, d)
    K.kkbf = ds("kkbf", [D, 2176], BF16)
    K.kcbf = ds("kcbf", [128, 2176], BF16)
    K.qtm, K.vtm = ds("qtm", [128, BEI], BF16), ds("vtm", [128, BEI], BF16)
    K.ktm, K.kptm = ds("ktm", [128, BEI], BF16), ds("kptm", [128, BEI], BF16)
    K.ytr = ds("ytr", [NB * 8, 128, L], BF16)
    K.x2dr = ds("x2dr", [NB, D, L], F32)
    K.st2dr = ds("st2dr", [NB, 2, L], F32)
    K.moddr = ds("moddr", [NB, 6 * D], F32)
    K.blhsdr = ds("blhsdr", [2, NB * 2 * D], F32)
    K.gatedr, K.updr = ds("gatedr", [NB, FF, L], BF16), ds("updr", [NB, FF, L], BF16)
    K.g2dr, K.odr = ds("g2dr", [NB, FF, L], BF16), ds("odr", [NB, FF, L], BF16)
    K.ar0i, K.ar0o = ds("ar0i", [1, 8]), ds("ar0o", [1, 8])
    K.ar1i, K.ar1o = ds("ar1i", [1, 8]), ds("ar1o", [1, 8])
    K.ar2i, K.ar2o = ds("ar2i", [1, 8]), ds("ar2o", [1, 8])
    K.ar3i, K.ar3o = ds("ar3i", [1, 8]), ds("ar3o", [1, 8])

    K.identbf = nc.inline_tensor(np.eye(128, dtype=ml_dtypes.bfloat16), "identbf")
    K.identf = nc.inline_tensor(np.eye(128, dtype=np.float32), "identf")
    K.c1024 = nc.inline_tensor(
        np.full((128, 1), 1.0 / 1024.0, dtype=ml_dtypes.bfloat16), "c1024")
    K.onescol = nc.inline_tensor(np.ones((1, 128), np.float32), "onescol")
    K.onesrow = nc.inline_tensor(np.ones((1, L), np.float32), "onesrow")

    if dbg_taps:
        K.taps = {}
        for n, s, d in [("tap_h", [128, NDC * L], BF16),
                        ("tap_ktm", [128, BEI], BF16),
                        ("tap_kptm", [128, BEI], BF16),
                        ("tap_ytr", [NB * 8, 128, L], BF16),
                        ("tap_x2", [NB, D, L], F32),
                        ("tap_scales", [128, 16], F32),
                        ("tap_oq", [NB, 128, NIC * L], BF16),
                        ("tap_vecs", [128, NB * 6 * NDC], F32),
                        ("tap_gabo", [128, NB * NDC], F32)]:
            K.taps[n] = nc.dram_tensor(n, s, d, kind="ExternalOutput")

    P = K.st
    K.ident_sb = K.sb(P, "ident_sb", [128, 128], BF16)
    K.identf_sb = K.sb(P, "identf_sb", [128, 128], F32)
    K.c1024_sb = K.sb(P, "c1024_sb", [128, 1], BF16)
    K.onescol_sb = K.sb(P, "onescol_sb", [1, 128], F32)
    K.bias_sb = K.sb(P, "bias_sb", [128, 32], F32)
    K.vecs = K.sb(P, "vecs", [128, NB * 6 * NDC], F32)
    K.gabo = K.sb(P, "gabo", [128, NB * NDC], F32)
    K.blhs = K.sb(P, "blhs", [2, D], F32)
    K.blhs_bf = K.sb(P, "blhs_bf", [2, D], BF16)
    K.rstd_bf = K.sb(P, "rstd_bf", [1, L], BF16)
    K.brhs_bf = K.sb(P, "brhs_bf", [2, L], BF16)
    K.brhs = K.sb(P, "brhs", [2, L], F32)
    K.rstd_sb = K.sb(P, "rstd_sb", [1, L], F32)
    K.scales = K.sb(P, "scales", [128, 16], F32)
    K.red1 = K.sb(P, "red1", [128, 1], F32)
    K.scrow = K.sb(P, "scrow", [1, 12], F32)
    K.redcg = K.sb(P, "redcg", [128, 384], F32)
    K.redcu = K.sb(P, "redcu", [128, 384], F32)

    K._blhs_cons = 0
    K._lnev = 0
    K._fin_t = 0

    if upto.startswith("BD"):
        order = [("P0", phase_P0), ("B", phase_B)]
    else:
        order = [("P0", phase_P0), ("A1", phase_A1), ("A2", phase_A2),
                 ("A3", phase_A3), ("A4", phase_A4)]
        if upto in ("B", "Bp", "B1", "B2", "B3", "B1h", "B1m", "B1m0", "B1m1",
                    "B1m2", "B1mA", "B1bar"):
            order.append(("B", phase_B))
    for name, fn in order:
        fn(K)
        K.bar()
        if name == upto or (name == "B"):
            break
    K.st.close()
    build._last_K = K
    return nc


def vcol(K, b, kind, dc):
    c = (kind * NDC + dc) * NB + b
    return K.vecs[:, c:c + 1]


def phase_P0(K):
    nc, I, W = K.nc, K.I, K.W
    st = ExitStack()
    kch = K.sb(st, "p0_kch", [128, 2176], F32)
    kchb = K.sb(st, "p0_kchb", [128, 2176], BF16)
    dcol = K.sb(st, "p0_dcol", [128, 8], F32)
    dccol = K.sb(st, "p0_dccol", [128, 1], F32)
    silc = K.sb(st, "p0_silc", [NB, D], F32)
    silcb = K.sb(st, "p0_silcb", [NB, D], BF16)
    silT = K.sb(st, "p0_silT", [128, NDC * NB], BF16)
    wmch = K.sb(st, "p0_wmch", [128, 2048], F32)
    wmchb = K.sb(st, "p0_wmchb", [128, 2 * 2048], BF16)
    mod_sb = K.sb(st, "p0_mod", [NB, 6 * D], F32)
    msk = K.sb(st, "p0_msk", [NB, 2 * D], mybir.dt.uint8)
    tpos = K.sb(st, "p0_tpos", [NB, 2 * D], F32)
    tneg = K.sb(st, "p0_tneg", [NB, 2 * D], F32)
    tps = K.ps(st, "p0_tps", [128, NB], BF16)
    tpsf = K.ps(st, "p0_tpsf", [128, 128], F32)
    mps = K.ps(st, "p0_mps", [NB, 2048], F32)

    nc.vector.memset(K.scales[:, :], 0.0)
    nc.vector.memset(K.scrow[:1, :], 0.0)
    I(nc.sync.dma_start(out=K.ident_sb[:, :], in_=K.identbf[:, :]), "dS")
    I(nc.sync.dma_start(out=K.identf_sb[:, :], in_=K.identf[:, :]), "dS")
    I(nc.sync.dma_start(out=K.c1024_sb[:, :], in_=K.c1024[:, :]), "dS")
    I(nc.sync.dma_start(out=K.onescol_sb[:, :], in_=K.onescol[:, :]), "dS")
    I(nc.sync.dma_start(out=K.bias_sb[:, :], in_=K.biases[:, :]), "dS")
    I(nc.sync.dma_start(out=K.brhs[1:2, :], in_=K.onesrow[:, :]), "dS")
    I(nc.sync.dma_start(out=dcol[:, :], in_=K.skD[:, :]), "dS")
    I(nc.sync.dma_start(out=dccol[:, :], in_=K.Dc[:, :]), "dS")
    t_small = I(nc.sync.dma_start(out=silc[:, :], in_=K.c_own[:, :]), "dS")

    t_store = 0
    for ci in range(9):
        src = K.kcpad[:, :] if ci == 0 else K.kkpad[(ci - 1) * 128:ci * 128, :]
        t_ld = I(nc.sync.dma_start(out=kch[:, :], in_=src), "dS")
        W(nc.vector, "dS", t_ld)
        W(nc.vector, "dG", t_store)
        dv = dccol[:, 0:1] if ci == 0 else dcol[:, ci - 1:ci]
        nc.vector.tensor_add(kch[:, 127:128], kch[:, 127:128], dv)
        t_cast = I(nc.vector.tensor_copy(kchb[:, :], kch[:, :]), "sV", 2)
        W(nc.sync, "sV", t_cast)
        W(nc.gpsimd, "sV", t_cast)
        dst = K.kcbf[:, :] if ci == 0 else K.kkbf[(ci - 1) * 128:ci * 128, :]
        t_store = I(nc.gpsimd.dma_start(out=dst, in_=kchb[:, :]), "dG")

    W(nc.scalar, "dS", t_small)
    t_sil = I(nc.scalar.activation(out=silc[:, :], in_=silc[:, :], func=AF.Silu),
              "sA")
    W(nc.vector, "sA", t_sil)
    t_silb = I(nc.vector.tensor_copy(silcb[:, :], silc[:, :]), "sV")
    for dc in range(NDC):
        W(nc.tensor, "sV", t_silb)
        if dc >= 1:
            W(nc.tensor, "sV", t_tpc)
        t_tp = I(nc.tensor.transpose(tps[:, :], silcb[:, dc * 128:(dc + 1) * 128],
                                     K.ident_sb[0:NB, 0:NB]), "sT")
        W(nc.vector, "sT", t_tp)
        t_tpc = I(nc.vector.tensor_copy(silT[:, dc * NB:(dc + 1) * NB], tps[:, :]),
                  "sV")
    t_mm = 0
    for s3 in range(3):
        for dc in range(NDC):
            W(nc.sync, "sV")
            t_wl = I(nc.sync.dma_start(
                out=wmch[:, :], in_=K.wmod[dc * 128:(dc + 1) * 128,
                                           s3 * 2048:(s3 + 1) * 2048]), "dS")
            W(nc.vector, "dS", t_wl)
            par = dc % 2
            if t_mm:
                W(nc.vector, "sT", t_mm)
            t_cb = I(nc.vector.tensor_copy(wmchb[:, par * 2048:(par + 1) * 2048],
                                           wmch[:, :]), "sV")
            W(nc.tensor, "sV", t_cb)
            for s5 in range(4):
                mm = nc.tensor.matmul(
                    mps[:, s5 * 512:(s5 + 1) * 512],
                    silT[:, dc * NB:(dc + 1) * NB],
                    wmchb[:, par * 2048 + s5 * 512: par * 2048 + (s5 + 1) * 512],
                    start=(dc == 0), stop=(dc == NDC - 1))
            t_mm = I(mm, "sT")
        W(nc.vector, "sT", t_mm)
        I(nc.vector.tensor_copy(mod_sb[:, s3 * 2048:(s3 + 1) * 2048], mps[:, :]),
          "sV")
    for kk_ in (2, 5):
        gsl = mod_sb[:, kk_ * D:(kk_ + 1) * D]
        nc.vector.tensor_scalar(out=msk[:, 0:D], in0=gsl, scalar1=0.0,
                                scalar2=None, op0=ALU.is_ge)
        nc.vector.tensor_scalar_max(tpos[:, 0:D], gsl, 0.25)
        nc.vector.tensor_scalar_min(tneg[:, 0:D], gsl, -0.25)
        nc.vector.select(gsl, msk[:, 0:D], tpos[:, 0:D], tneg[:, 0:D])
    csl = AP(_h(mod_sb), 1 * D, [[6 * D, NB], [3 * D, 2], [1, D]])
    t_mod = I(nc.vector.tensor_scalar_add(csl, csl, 1.0), "sV", 5)
    W(nc.gpsimd, "sV", t_mod)
    t_md = I(nc.gpsimd.dma_start(out=K.moddr[:, :], in_=mod_sb[:, :]), "dG")
    W(nc.gpsimd, "dG", t_md)
    for b in range(NB):
        for ln in range(2):
            base = (b * 2 + ln) * D
            I(nc.gpsimd.dma_start(
                out=AP(K.blhsdr, base, [[1, 1], [1, D]]),
                in_=K.moddr[b:b + 1, (1 + 3 * ln) * D:(2 + 3 * ln) * D]), "dG")
            I(nc.gpsimd.dma_start(
                out=AP(K.blhsdr, NB * 2 * D + base, [[1, 1], [1, D]]),
                in_=K.moddr[b:b + 1, (0 + 3 * ln) * D:(1 + 3 * ln) * D]), "dG")
    t_vc = t_mod
    for kind in range(6):
        for dc in range(NDC):
            if t_vc:
                W(nc.tensor, "sV", t_vc)
            t_tp = I(nc.tensor.transpose(
                tpsf[:, 0:NB], mod_sb[:, kind * D + dc * 128: kind * D + (dc + 1) * 128],
                K.identf_sb[0:NB, 0:NB]), "sT")
            W(nc.vector, "sT", t_tp)
            col = (kind * NDC + dc) * NB
            t_vc = I(nc.vector.tensor_copy(K.vecs[:, col:col + NB],
                                           tpsf[:, 0:NB]), "sV")
    for b in range(NB):
        gav = AP(_h(K.vecs), (2 * NDC) * NB + b, [[NB * 6 * NDC, 128], [NB, NDC]])
        t_gb = I(nc.vector.tensor_mul(K.gabo[:, b * NDC:(b + 1) * NDC], gav,
                                      K.bias_sb[:, 24:32]), "sV")
    if K.dbg_taps:
        W(nc.gpsimd, "sV", t_gb)
        I(nc.gpsimd.dma_start(out=K.taps["tap_vecs"][:, :], in_=K.vecs[:, :]), "dG")
        I(nc.gpsimd.dma_start(out=K.taps["tap_gabo"][:, :], in_=K.gabo[:, :]), "dG")
    st.close()


def row_stats(K, m_sb, s2_sb, rowt, rowt2):
    nc, I, W = K.nc, K.I, K.W
    v = nc.vector
    W(v, "sT", K._blhs_cons)
    v.tensor_mul(rowt[:1, :], m_sb[:1, :], m_sb[:1, :])
    v.tensor_sub(rowt[:1, :], s2_sb[:1, :], rowt[:1, :])
    t = I(v.tensor_scalar_add(rowt[:1, :], rowt[:1, :], EPS), "sV", 3)
    W(nc.scalar, "sV", t)
    t = I(nc.scalar.activation(out=K.rstd_sb[:1, :], in_=rowt[:1, :],
                               func=AF.Identity), "sA")
    W(v, "sA", t)
    t = I(v.reciprocal(rowt2[:1, :], K.rstd_sb[:1, :]), "sV", 1)
    W(nc.scalar, "sV", t)
    t = I(nc.scalar.activation(out=K.rstd_sb[:1, :], in_=rowt2[:1, :],
                               func=AF.Sqrt), "sA")
    W(v, "sA", t)
    v.tensor_mul(rowt2[:1, :], K.rstd_sb[:1, :], K.rstd_sb[:1, :])
    v.tensor_mul(rowt2[:1, :], rowt2[:1, :], rowt[:1, :])
    v.tensor_scalar(out=rowt2[:1, :], in0=rowt2[:1, :], scalar1=-0.5, scalar2=1.5,
                    op0=ALU.mult, op1=ALU.add)
    v.tensor_mul(K.rstd_sb[:1, :], K.rstd_sb[:1, :], rowt2[:1, :])
    return I(v.scalar_tensor_tensor(K.brhs[0:1, :], m_sb[:1, :], -1.0,
                                    K.rstd_sb[:1, :], ALU.mult, ALU.mult), "sV", 5)


def ln_stats(K, xsrc, m_sb, s2_sb, stps, xsq):
    nc, I, W = K.nc, K.I, K.W
    W(nc.scalar, "sV")
    t_mm = 0
    for s in range(NS):
        for dc in range(NDC):
            xs = xsrc(dc, s)
            if t_mm:
                W(nc.scalar, "sT", t_mm)
            t_sq = I(nc.scalar.activation(out=xsq[:, :], in_=xs, func=AF.Square),
                     "sA")
            W(nc.tensor, "sA", t_sq)
            if dc == 0:
                W(nc.tensor, "sV", K._lnev)
            nc.tensor.matmul(stps[0:1, :], K.c1024_sb[:, :], xs,
                             start=(dc == 0), stop=(dc == NDC - 1))
            t_mm = I(nc.tensor.matmul(stps[32:33, :], K.c1024_sb[:, :], xsq[:, :],
                                      start=(dc == 0), stop=(dc == NDC - 1)), "sT")
        W(nc.vector, "sT", t_mm)
        nc.vector.tensor_copy(m_sb[:1, s * 512:(s + 1) * 512], stps[0:1, :])
        K._lnev = I(nc.vector.tensor_copy(s2_sb[:1, s * 512:(s + 1) * 512],
                                          stps[32:33, :]), "sV", 2)


def ab_maps(K, dc, s, abA, abB):
    nc, I = K.nc, K.I
    lA = AP(_h(K.blhs_bf), dc * 128, [[D, 1], [1, 128]])
    lB = AP(_h(K.blhs_bf), dc * 128, [[D, 2], [1, 128]])
    nc.tensor.matmul(abA[:, :], lA, K.rstd_bf[:1, s * 512:(s + 1) * 512],
                     start=True, stop=True)
    t = I(nc.tensor.matmul(abB[:, :], lB, K.brhs_bf[:, s * 512:(s + 1) * 512],
                           start=True, stop=True), "sT")
    K._blhs_cons = t
    return t


def rows_to_bf(K):
    nc, I, W = K.nc, K.I, K.W
    W(nc.vector, "sT", K._blhs_cons)
    nc.vector.tensor_copy(K.rstd_bf[:1, :], K.rstd_sb[:1, :])
    return I(nc.vector.tensor_copy(K.brhs_bf[:, :], K.brhs[:, :]), "sV")


def blhs_to_bf(K):
    nc, I, W = K.nc, K.I, K.W
    W(nc.vector, "sT", K._blhs_cons)
    return I(nc.vector.tensor_copy(K.blhs_bf[:, :], K.blhs[:, :]), "sV")


def phase_A1(K):
    nc, I, W = K.nc, K.I, K.W
    nc.sync.wait_ge(K.sem["dG"], K.cnt["dG"])
    st = ExitStack()
    x_bf = K.sb(st, "a1_xbf", [128, NDC * L], BF16)
    h_bf = K.sb(st, "a1_hbf", [128, NDC * L], BF16)
    xch = K.sb(st, "a1_xch", [128, 2 * L], F32)
    xsq = K.sb(st, "a1_xsq", [128, 512], BF16)
    pfm = K.sb(st, "a1_pfm", [128, 2 * L], BF16)
    stag = K.sb(st, "a1_stag", [128, D * NI], BF16)
    wsl = K.sb(st, "a1_wsl", [128, 2 * 1024], F32)
    wslb = K.sb(st, "a1_wslb", [128, 2 * 1024], BF16)
    m_sb = K.sb(st, "a1_m", [1, L], F32)
    s2_sb = K.sb(st, "a1_s2", [1, L], F32)
    rowt = K.sb(st, "a1_rt", [1, L], F32)
    rowt2 = K.sb(st, "a1_rt2", [1, L], F32)
    stps = K.ps(st, "a1_st", [128, 512])
    abps = [K.ps(st, f"a1_ab{i}", [128, 512]) for i in range(2)]
    pps = [K.ps(st, f"a1_pp{i}", [128, 512]) for i in range(2)]
    tpsl = [K.ps(st, f"a1_tp{i}", [128, 128], BF16) for i in range(2)]

    t_abcons = [0, 0]
    t_tpev = [0, 0]
    t_lastTp = [0, 0]
    t_wmm = [0, 0]
    stg_tok = [("dG", 0), ("dB", 0), ("dG", 0), ("dB", 0)]
    for b in range(NB):
        W(nc.sync, "sT", K._blhs_cons)
        t_bl = I(nc.sync.dma_start(
            out=K.blhs[:, :], in_=AP(K.blhsdr, (b * 2 + 0) * D,
                                     [[NB * 2 * D, 2], [1, D]])), "dS")
        t_xc = [0, 0]
        for dc in range(NDC):
            par = dc % 2
            xs_ = ("d2", "d7")[par]
            W(nc.sync, "sV", t_xc[par])
            t_ld = I(nc.sync.dma_start(out=xch[:, par * L:(par + 1) * L],
                                       in_=K.u_fm[b, dc * 128:(dc + 1) * 128, :]),
                     xs_)
            W(nc.vector, xs_, t_ld)
            if b > 0 and dc == 0:
                W(nc.vector, "sT")
            t_xc[par] = I(nc.vector.tensor_copy(
                x_bf[:, dc * L:(dc + 1) * L], xch[:, par * L:(par + 1) * L]), "sV")
        ln_stats(K, lambda dc, s: x_bf[:, dc * L + s * 512: dc * L + (s + 1) * 512],
                 m_sb, s2_sb, stps, xsq)
        row_stats(K, m_sb, s2_sb, rowt, rowt2)
        W(nc.vector, "dS", t_bl)
        rows_to_bf(K)
        blhs_to_bf(K)
        W(nc.tensor, "sV")
        t_h = 0
        for s in range(NS):
            for dc in range(NDC):
                W(nc.tensor, "sV", t_abcons[0])
                t_ab = ab_maps(K, dc, s, abps[0], abps[1])
                W(nc.vector, "sT", t_ab)
                sl = h_bf[:, dc * L + s * 512: dc * L + (s + 1) * 512]
                xsl = x_bf[:, dc * L + s * 512: dc * L + (s + 1) * 512]
                nc.vector.scalar_tensor_tensor(sl, xsl, 1.0, abps[0][:, :],
                                               ALU.mult, ALU.mult)
                t_h = I(nc.vector.tensor_add(sl, sl, abps[1][:, :]),
                        "sV", 2)
                t_abcons[0] = t_h
        if not hasattr(K, "_a1wc"):
            K._a1wc = [0, 0]
        for ip, (nm, wdr, dr) in enumerate(
                (("q", K.wqT, K.qtm), ("k", K.wkT, K.ktm), ("v", K.wvT, K.vtm))):
            t_evprev = 0
            for ec in range(NEC):
                par = ec % 2
                ws_ = ("d3", "d8")[par]
                W(nc.sync, "sV", K._a1wc[par])
                t_wl = I(nc.sync.dma_start(
                    out=wsl[:, par * 1024:(par + 1) * 1024],
                    in_=AP(wdr, ec * 128 * D, [[D, 128], [1, D]])),
                    ws_)
                W(nc.vector, ws_, t_wl)
                W(nc.vector, "sT", t_wmm[par])
                K._a1wc[par] = I(nc.vector.tensor_copy(
                    wslb[:, par * 1024:(par + 1) * 1024],
                    wsl[:, par * 1024:(par + 1) * 1024]), "sV")
                W(nc.tensor, "sV", max(K._a1wc[par], t_h))
                for hf in range(2):
                    W(nc.tensor, "sA", t_evprev)
                    for dc in range(NDC):
                        for s in range(2):
                            mm = nc.tensor.matmul(
                                pps[s][:, :],
                                wslb[:, par * 1024 + dc * 128: par * 1024 + (dc + 1) * 128],
                                h_bf[:, dc * L + hf * 1024 + s * 512: dc * L + hf * 1024 + (s + 1) * 512],
                                start=(dc == 0), stop=(dc == NDC - 1))
                    t_mm = I(mm, "sT")
                    W(nc.scalar, "sT", t_mm)
                    if hf == 1:
                        W(nc.scalar, "sT", t_lastTp[par])
                    for s in range(2):
                        rev = AP(pps[s], 127, [[512, 128], [128, 4], [-1, 128]])
                        ev = nc.scalar.activation(
                            out=pfm[:, par * L + hf * 1024 + s * 512: par * L + hf * 1024 + (s + 1) * 512],
                            in_=rev, func=AF.Identity,
                            bias=K.bias_sb[:, ip * 8 + ec: ip * 8 + ec + 1], scale=1.0)
                    t_evprev = I(ev, "sA", 2)
                t_wmm[par] = t_mm
                W(nc.tensor, "sA", t_evprev)
                if ec % 2 == 0:
                    W(nc.tensor, *stg_tok[ec // 2])
                for Ib in range(NI):
                    tpar = Ib % 2
                    W(nc.tensor, "sV", t_tpev[tpar])
                    tt = nc.tensor.transpose(
                        tpsl[tpar][:, :],
                        pfm[:, par * L + Ib * 128: par * L + (Ib + 1) * 128],
                        K.ident_sb[:, :])
                    t_tp = I(tt, "sT")
                    t_lastTp[par] = t_tp
                    W(nc.vector, "sT", t_tp)
                    dst = AP(_h(stag), ec * 128 * NI + Ib,
                             [[D * NI, 128], [NI, 128]])
                    t_tpev[tpar] = I(nc.vector.tensor_copy(dst, tpsl[tpar][:, :]),
                                     "sV")
                if ec % 2 == 1:
                    g = ec // 2
                    sts_ = ("dG", "dB")[g % 2]
                    W(nc.gpsimd, "sV", max(t_tpev))
                    cw = 2 * 128 * NI
                    stg_tok[g] = (sts_, I(nc.gpsimd.dma_start(
                        out=dr[:, b * D * NI + g * cw: b * D * NI + (g + 1) * cw],
                        in_=stag[:, g * cw:(g + 1) * cw]), sts_))
            if K.dbg_taps and nm == "k" and b == 0:
                I(nc.gpsimd.dma_start(out=K.taps["tap_h"][:, :], in_=h_bf[:, :]),
                  "dG")
    st.close()


def phase_A2(K):
    nc, I, W = K.nc, K.I, K.W
    K.joinD(nc.sync)
    st = ExitStack()
    ksb = K.sb(st, "a2_ksb", [128, BEI], BF16)
    skwr = [K.sb(st, f"a2_skwr{i}", [128, 2048], BF16) for i in range(4)]
    skw = [K.sb(st, f"a2_skw{i}", [128, 2048], BF16) for i in range(2)]
    kstage = K.sb(st, "a2_kstage", [128, 4096], BF16)
    kps = [K.ps(st, f"a2_kps{i}", [128, 512]) for i in range(2)]

    t_kld = I(nc.sync.dma_start(out=ksb[:, :], in_=K.ktm[:, :]), "dS")
    SKS = ("d2", "d7", "d9", "dA")
    t_raw = [0, 0, 0, 0]
    t_rvw = [0, 0, 0, 0]
    t_rev = [0, 0]
    t_mmd = [0, 0]
    t_bankev = [0, 0]
    t_flush = 0

    def pf_skw(e):
        par4 = e % 4
        W(nc.sync, "sV", t_rvw[par4])
        t_raw[par4] = I(nc.sync.dma_start(
            out=skwr[par4][:, :], in_=AP(K.kkbf, e * 2176, [[1, 128], [1, 2048]])),
            SKS[par4])

    for ee in range(4):
        pf_skw(ee)
    for e in range(D):
        par = e % 2
        par4 = e % 4
        W(nc.vector, SKS[par4], t_raw[par4])
        W(nc.vector, "sT", t_mmd[par])
        rev = AP(_h(skwr[par4]), 2047, [[2048, 128], [-1, 2048]])
        t_rev[par] = t_rvw[par4] = I(nc.vector.tensor_copy(skw[par][:, :], rev),
                                     "sV")
        if e + 4 < D:
            pf_skw(e + 4)
        bpar = (e // 8) % 2
        bank = kps[bpar]
        eg = e % 8
        W(nc.tensor, "sV", t_rev[par])
        if e == 0:
            W(nc.tensor, "dS", t_kld)
        if eg == 0:
            W(nc.tensor, "sA", t_bankev[bpar])
        for dd in range(NI):
            outw = AP(bank, eg * 64 + dd * 4, [[512, 128], [4, NI - dd], [1, 4]])
            rhs = AP(_h(ksb), e * NI, [[BEI, 128], [1, NI - dd], [D * NI, NB]])
            mm = nc.tensor.matmul(outw,
                                  skw[par][:, (NI - 1 - dd) * 128:(NI - dd) * 128],
                                  rhs, start=(dd == 0), stop=(dd == NI - 1))
        t_mmd[par] = I(mm, "sT")
        if eg == 7:
            grp = e // 8
            sgrp = grp % 8
            W(nc.scalar, "sT", t_mmd[par])
            if sgrp == 0:
                W(nc.scalar, "dG", t_flush)
            dst = AP(_h(kstage), sgrp * 8 * NI,
                     [[4096, 128], [NI, 8], [1, NI], [64 * NI, NB]])
            src = AP(bank, 0, [[512, 128], [64, 8], [4, NI], [1, 4]])
            t_bankev[bpar] = I(nc.scalar.copy(dst, src), "sA")
            if sgrp == 7:
                W(nc.gpsimd, "sA", t_bankev[bpar])
                e_start = (grp - 7) * 8
                t_flush = I(nc.gpsimd.dma_start(
                    out=AP(K.kptm, e_start * NI,
                           [[BEI, 128], [D * NI, NB], [1, 64 * NI]]),
                    in_=AP(_h(kstage), 0, [[4096, 128], [64 * NI, NB],
                                           [1, 64 * NI]])), "dG")
    if K.dbg_taps:
        W(nc.gpsimd, "dG", t_flush)
        I(nc.gpsimd.dma_start(out=K.taps["tap_kptm"][:, :], in_=K.kptm[:, :]), "dG")
        I(nc.gpsimd.dma_start(out=K.taps["tap_ktm"][:, :], in_=ksb[:, :]), "dG")
    st.close()


def phase_A3(K):
    nc, I, W = K.nc, K.I, K.W
    K.joinD(nc.sync)
    st = ExitStack()
    skwr = [K.sb(st, f"a3_skwr{i}", [128, 2048], BF16) for i in range(4)]
    skw = [K.sb(st, f"a3_skw{i}", [128, 2048], BF16) for i in range(2)]
    qc = [K.sb(st, f"a3_qc{i}", [128, 4096], BF16) for i in range(2)]
    vc = [K.sb(st, f"a3_vc{i}", [128, 4096], BF16) for i in range(2)]
    kc = [K.sb(st, f"a3_kc{i}", [128, 4096], BF16) for i in range(2)]
    kv = [K.sb(st, f"a3_kv{i}", [128, 1024], BF16) for i in range(2)]
    cbf = [K.sb(st, f"a3_cbf{i}", [128, 1024], BF16) for i in range(2)]
    qxp = [K.sb(st, f"a3_qxp{i}", [128, 1024], BF16) for i in range(2)]
    zb = [K.sb(st, f"a3_zb{i}", [128, 1024], BF16) for i in range(2)]
    z1 = K.sb(st, "a3_z1", [128, 512], BF16)
    z2 = K.sb(st, "a3_z2", [128, 256], BF16)
    yslab = [K.sb(st, f"a3_ys{bq}", [128, 2048], BF16) for bq in range(NB)]
    ytst = K.sb(st, "a3_ytst", [128, 2048], BF16)
    cps = [K.ps(st, f"a3_cp{i}", [128, 1024]) for i in range(2)]
    tpsl = [K.ps(st, f"a3_tp{i}", [128, 128], BF16) for i in range(2)]

    t_ld = [0, 0]
    t_rev = [0, 0]
    t_raw = [0, 0, 0, 0]
    t_rvw = [0, 0, 0, 0]
    t_qvk_cons = [0, 0]
    t_mm = [0, 0]
    t_qc_done = [0, 0]
    t_cbf_cons = [0, 0]
    t_slabtpT = 0
    t_ytstd = 0
    t_tpev = [0, 0]

    SKS = ("d2", "d7", "d9", "dA")

    def prefetch_skw(c):
        par4 = c % 4
        W(nc.sync, "sV", t_rvw[par4])
        t_raw[par4] = I(nc.sync.dma_start(
            out=skwr[par4][:, :], in_=AP(K.kcbf, c * 2176, [[1, 128], [1, 2048]])),
            SKS[par4])

    def prefetch_qvk(n):
        par8 = n % 2
        W(nc.sync, "sV", t_qvk_cons[par8])
        for buf, dr in ((qc[par8], K.qtm), (vc[par8], K.vtm), (kc[par8], K.kptm)):
            t = I(nc.sync.dma_start(
                out=AP(_h(buf), 0, [[4096, 128], [1024, NB], [1, 1024]]),
                in_=AP(dr, n * 1024,
                       [[BEI, 128], [D * NI, NB], [1, 1024]])), "d3")
        t_ld[par8] = t

    for cc in range(4):
        prefetch_skw(cc)
    prefetch_qvk(0)
    for c in range(128):
        par = c % 2
        par4 = c % 4
        par8 = (c // 8) % 2
        cb = (c % 8) * 128
        cl = c % 16
        cg = c // 16
        W(nc.vector, SKS[par4], t_raw[par4])
        W(nc.vector, "sT", t_mm[par])
        revv = AP(_h(skwr[par4]), 2047, [[2048, 128], [-1, 2048]])
        t_rev[par] = t_rvw[par4] = I(nc.vector.tensor_copy(skw[par][:, :], revv),
                                     "sV")
        if c + 4 < 128:
            prefetch_skw(c + 4)
        if c % 8 == 0 and c // 8 + 1 < 16:
            prefetch_qvk(c // 8 + 1)
        for bq in range(NB):
            cbpar = (c * NB + bq) % 2
            W(nc.vector, "d3", t_ld[par8])
            W(nc.vector, "sT", t_mm[cbpar])
            in0 = AP(_h(kc[par8]), bq * 1024 + cb,
                     [[4096, 128], [1, NI], [NI, 8], [0, 8]])
            in1 = AP(_h(vc[par8]), bq * 1024 + cb,
                     [[4096, 128], [1, NI], [0, 8], [NI, 8]])
            t_kvb = I(nc.vector.tensor_tensor(kv[cbpar][:, :], in0, in1, ALU.mult),
                      "sV")
            W(nc.tensor, "sV", max(t_kvb, t_rev[par], t_qc_done[cbpar]))
            for dd in range(NI):
                s0 = dd * 64
                lt = skw[par][:, (NI - 1 - dd) * 128:(NI - dd) * 128]
                if s0 < 512:
                    nc.tensor.matmul(cps[cbpar][:, s0:512], lt,
                                     kv[cbpar][:, 0:512 - s0],
                                     start=(dd == 0), stop=(dd == 7))
                    mm = nc.tensor.matmul(cps[cbpar][:, 512:1024], lt,
                                          kv[cbpar][:, 512 - s0:1024 - s0],
                                          start=(dd == 0), stop=(dd == NI - 1))
                else:
                    mm = nc.tensor.matmul(cps[cbpar][:, s0:1024], lt,
                                          kv[cbpar][:, 0:1024 - s0],
                                          start=False, stop=(dd == NI - 1))
            t_mm[cbpar] = I(mm, "sT")
            W(nc.scalar, "sT", t_mm[cbpar])
            W(nc.scalar, "sV", t_cbf_cons[cbpar])
            t_cbf = I(nc.scalar.copy(cbf[cbpar][:, :], cps[cbpar][:, :]), "sA")
            qsrc = AP(_h(qc[par8]), bq * 1024 + cb,
                      [[4096, 128], [1, NI], [NI, 8], [0, 8]])
            W(nc.vector, "sA", t_cbf)
            t_qx = I(nc.vector.tensor_copy(qxp[cbpar][:, :], qsrc), "sV")
            t_qvk_cons[par8] = t_qx
            nc.vector.tensor_mul(zb[cbpar][:, :], cbf[cbpar][:, :],
                                 qxp[cbpar][:, :])
            t_qc_done[cbpar] = K.cnt["sV"] + 4
            i0a = AP(_h(zb[cbpar]), 0, [[1024, 128], [64, NI], [8, 4], [1, 8]])
            i0b = AP(_h(zb[cbpar]), 32, [[1024, 128], [64, NI], [8, 4], [1, 8]])
            o1 = AP(_h(z1), 0, [[512, 128], [32, NI], [8, 4], [1, 8]])
            nc.vector.tensor_tensor(o1, i0a, i0b, ALU.add)
            i1a = AP(_h(z1), 0, [[512, 128], [32, NI], [8, 2], [1, 8]])
            i1b = AP(_h(z1), 16, [[512, 128], [32, NI], [8, 2], [1, 8]])
            o2 = AP(_h(z2), 0, [[256, 128], [16, NI], [8, 2], [1, 8]])
            nc.vector.tensor_tensor(o2, i1a, i1b, ALU.add)
            i2a = AP(_h(z2), 0, [[256, 128], [16, NI], [1, 8]])
            i2b = AP(_h(z2), 8, [[256, 128], [16, NI], [1, 8]])
            if cl == 0 and bq == 0:
                W(nc.vector, "sT", t_slabtpT)
            oy = AP(_h(yslab[bq]), cl * 8, [[2048, 128], [128, NI], [1, 8]])
            t4 = I(nc.vector.tensor_tensor(oy, i2a, i2b, ALU.add), "sV", 4)
            t_cbf_cons[cbpar] = t4
        if cl == 15:
            W(nc.tensor, "sV")
            for bq in range(NB):
                for Ib in range(NI):
                    tpar = Ib % 2
                    W(nc.tensor, "sV", t_tpev[tpar])
                    tt = nc.tensor.transpose(
                        tpsl[tpar][:, :], yslab[bq][:, Ib * 128:(Ib + 1) * 128],
                        K.ident_sb[:, :])
                    t_tp = I(tt, "sT")
                    t_slabtpT = t_tp
                    W(nc.vector, "sT", t_tp)
                    if Ib == 0:
                        W(nc.vector, "dG", t_ytstd)
                    t_tpev[tpar] = I(nc.vector.tensor_copy(
                        ytst[:, Ib * 128:(Ib + 1) * 128], tpsl[tpar][:, :]), "sV")
                W(nc.gpsimd, "sV", max(t_tpev))
                t_ytstd = I(nc.gpsimd.dma_start(
                    out=K.ytr[bq * 8 + cg, :, :], in_=ytst[:, :]), "dG")
    if K.dbg_taps:
        W(nc.gpsimd, "dG", t_ytstd)
        I(nc.gpsimd.dma_start(out=K.taps["tap_ytr"][:, :, :], in_=K.ytr[:, :, :]),
          "dG")
    st.close()


def phase_A4(K):
    nc, I, W = K.nc, K.I, K.W
    K.joinD(nc.sync)
    st = ExitStack()
    wop = K.sb(st, "a4_wop", [128, 1024], F32)
    wopb = K.sb(st, "a4_wopb", [128, NDC * 1024], BF16)
    ysb = K.sb(st, "a4_ysb", [128, 8 * L], BF16)
    uch = K.sb(st, "a4_uch", [128, L], F32)
    x2sb = K.sb(st, "a4_x2", [128, NDC * L], F32)
    xbfs = K.sb(st, "a4_xbfs", [128, 512], BF16)
    xsq = K.sb(st, "a4_xsq", [128, 512], BF16)
    t1b = K.sb(st, "a4_t1b", [128, 2 * 512], F32)
    h2s = K.sb(st, "a4_h2s", [128, 512], F32)
    m_sb = K.sb(st, "a4_m", [1, L], F32)
    s2_sb = K.sb(st, "a4_s2", [1, L], F32)
    rowt = K.sb(st, "a4_rt", [1, L], F32)
    rowt2 = K.sb(st, "a4_rt2", [1, L], F32)
    stps = K.ps(st, "a4_st", [128, 512])
    abps = [K.ps(st, f"a4_ab{i}", [128, 512]) for i in range(2)]
    wps = [K.ps(st, f"a4_wp{i}", [128, 512]) for i in range(2)]

    t_wc = 0
    for ch in range(NDC):
        W(nc.sync, "sV", t_wc)
        t_l = I(nc.sync.dma_start(
            out=wop[:, :], in_=K.woTp[ch * 128:(ch + 1) * 128, :]), "d3")
        W(nc.vector, "d3", t_l)
        t_wc = I(nc.vector.tensor_copy(
            wopb[:, ch * 1024:(ch + 1) * 1024], wop[:, :]), "sV")

    nmax = 0
    t_ev = [0, 0]
    t_abcons = [0, 0]
    t_woMM = 0
    t_spill = 0
    for b in range(NB):
        W(nc.sync, "sT", max(t_woMM, K._blhs_cons))
        t_y = 0
        for cg in range(8):
            t_y = I(nc.sync.dma_start(out=ysb[:, cg * L:(cg + 1) * L],
                                      in_=K.ytr[b * 8 + cg, :, :]), "dS")
        t_bl = I(nc.sync.dma_start(
            out=K.blhs[:, :], in_=AP(K.blhsdr, (b * 2 + 1) * D,
                                     [[NB * 2 * D, 2], [1, D]])), "dS")
        t_uc = [0, 0]
        t_x2 = 0
        W(nc.vector, "d4", t_spill)
        for fc in range(NDC):
            par = 0
            W(nc.sync, "sV", t_uc[par])
            t_u = I(nc.sync.dma_start(out=uch[:, :],
                                      in_=K.u_fm[b, fc * 128:(fc + 1) * 128, :]),
                    "d2")
            for s in range(NS):
                spar = (fc * NS + s) % 2
                W(nc.tensor, "dS", t_y)
                W(nc.tensor, "sV", max(t_wc, t_ev[spar]))
                for cg in range(8):
                    mm = nc.tensor.matmul(
                        wps[spar][:, :],
                        wopb[:, cg * 1024 + fc * 128: cg * 1024 + (fc + 1) * 128],
                        ysb[:, cg * L + s * 512: cg * L + (s + 1) * 512],
                        start=(cg == 0), stop=(cg == 7))
                t_woMM = t_mm = I(mm, "sT")
                W(nc.scalar, "sT", t_mm)
                W(nc.scalar, "sV", t_ev[spar])
                rev = AP(wps[spar], 127, [[512, 128], [128, 4], [-1, 128]])
                t_t1 = I(nc.scalar.activation(
                    out=t1b[:, spar * 512:(spar + 1) * 512], in_=rev,
                    func=AF.Identity, scale=vcol(K, b, 2, fc),
                    bias=K.gabo[:, b * NDC + fc: b * NDC + fc + 1]), "sA")
                W(nc.vector, "sA", t_t1)
                W(nc.vector, "d2", t_u)
                t_x2 = I(nc.vector.tensor_add(
                    x2sb[:, fc * L + s * 512: fc * L + (s + 1) * 512],
                    t1b[:, spar * 512:(spar + 1) * 512],
                    uch[:, s * 512:(s + 1) * 512]), "sV")
                t_ev[spar] = t_x2
                t_uc[par] = t_x2

        def xsrc(dc, s):
            W(nc.vector, "sA")
            t = I(nc.vector.tensor_copy(
                xbfs[:, :], x2sb[:, dc * L + s * 512: dc * L + (s + 1) * 512]),
                "sV")
            W(nc.scalar, "sV", t)
            return xbfs[:, :]
        ln_stats(K, xsrc, m_sb, s2_sb, stps, xsq)
        row_stats(K, m_sb, s2_sb, rowt, rowt2)
        W(nc.vector, "dS", t_bl)
        rows_to_bf(K)
        blhs_to_bf(K)
        W(nc.gpsimd, "sV")
        I(nc.gpsimd.dma_start(out=K.st2dr[b, 0:1, :], in_=K.rstd_sb[:1, :]), "dG")
        t_st2d = I(nc.gpsimd.dma_start(out=K.st2dr[b, 1:2, :], in_=K.brhs[0:1, :]),
                   "dG")
        K._blhs_cons = max(K._blhs_cons, 0)
        W(nc.vector, "dG", t_st2d)
        for s in range(NS):
            for dc in range(NDC):
                W(nc.tensor, "sV", t_abcons[0])
                t_ab = ab_maps(K, dc, s, abps[0], abps[1])
                W(nc.vector, "sT", t_ab)
                xsl = x2sb[:, dc * L + s * 512: dc * L + (s + 1) * 512]
                nc.vector.scalar_tensor_tensor(h2s[:, :], xsl, 1.0,
                                               abps[0][:, :], ALU.mult,
                                               ALU.mult)
                nc.vector.tensor_add(h2s[:, :], h2s[:, :], abps[1][:, :])
                t_r = I(nc.vector.tensor_reduce(
                    K.redcg[:, nmax:nmax + 1], h2s[:, :], AX.X, ALU.max,
                    apply_absolute_value=True), "sV", 3)
                nmax += 1
                t_abcons[0] = t_r
        W(nc.gpsimd, "sV")
        for dc in range(NDC):
            t_spill = I(nc.gpsimd.dma_start(
                out=K.x2dr[b, dc * 128:(dc + 1) * 128, :],
                in_=x2sb[:, dc * L:(dc + 1) * L]), "d4")
        if K.dbg_taps:
            W(nc.gpsimd, "d4", t_spill)
            I(nc.gpsimd.dma_start(out=K.taps["tap_x2"][b, :, :],
                                  in_=K.x2dr[b, :, :]), "dG")
    finalize_scale(K, K.redcg, nmax, K.ar0i, K.ar0o, [(0, 1)], 1)
    st.close()


def finalize_scale(K, redc, nidx, ari, aro, colpairs, n_sc):
    nc, I, W = K.nc, K.I, K.W
    v = nc.vector
    stf = ExitStack()
    K._fsn = getattr(K, "_fsn", 0) + 1
    rps = K.ps(stf, f"fs_rps{K._fsn}", [1, 128])
    bcps = K.ps(stf, f"fs_bcps{K._fsn}", [128, 1])
    reds = redc if isinstance(redc, list) else [redc]
    t = 0
    for i, rc in enumerate(reds):
        W(v, "sT", K._fin_t)
        t = I(v.tensor_reduce(K.red1[:, :], rc[:, :nidx], AX.X, ALU.max,
                              apply_absolute_value=True), "sV")
        W(nc.tensor, "sV", t)
        K._fin_t = tt = I(nc.tensor.transpose(rps[:, :], K.red1[:, :],
                                              K.identf_sb[:, :]), "sT")
        W(v, "sT", tt)
        t = I(v.tensor_reduce(K.scrow[:1, i:i + 1], rps[:1, :], AX.X, ALU.max,
                              apply_absolute_value=True), "sV")
    W(nc.gpsimd, "sV", t)
    td = I(nc.gpsimd.dma_start(out=ari[:, :], in_=K.scrow[:1, 0:8]), "dG")
    W(nc.gpsimd, "dG", td)
    tcc = I(nc.gpsimd.collective_compute(
        "AllReduce", ALU.max, replica_groups=K.rg,
        ins=[ari.ap().opt()], outs=[aro.ap().opt()]), "sCC")
    W(nc.sync, "sCC", tcc)
    tl = I(nc.sync.dma_start(out=K.scrow[:1, 4:4 + n_sc], in_=aro[:, 0:n_sc]), "dS")
    W(v, "dS", tl)
    t = I(v.tensor_scalar(out=K.scrow[:1, 4:4 + n_sc], in0=K.scrow[:1, 4:4 + n_sc],
                          scalar1=float(np.float32(1 / 32767.0)), scalar2=1e-12,
                          op0=ALU.mult, op1=ALU.max), "sV")
    W(nc.scalar, "sV", t)
    t = I(nc.scalar.activation(out=K.scrow[:1, 8:8 + n_sc],
                               in_=K.scrow[:1, 4:4 + n_sc],
                               func=AF.Identity), "sA")
    W(v, "sA", t)
    t = I(v.reciprocal(K.scrow[:1, 6:6 + n_sc], K.scrow[:1, 8:8 + n_sc]), "sV")
    for i, (icol, scol) in enumerate(colpairs):
        for src_off, dcol in ((6 + i, icol), (4 + i, scol)):
            W(nc.tensor, "sV", t)
            tm = I(nc.tensor.matmul(bcps[:, :], K.onescol_sb[:, :],
                                    K.scrow[:1, src_off:src_off + 1],
                                    start=True, stop=True), "sT")
            W(v, "sT", tm)
            t = I(v.tensor_copy(K.scales[:, dcol:dcol + 1], bcps[:, :]), "sV")
    if K.dbg_taps:
        W(nc.gpsimd, "sV", t)
        I(nc.gpsimd.dma_start(out=K.taps["tap_scales"][:, :], in_=K.scales[:, :]),
          "dG")
    W(nc.tensor, "sV", t)
    stf.close()


def phase_B(K):
    nc, I, W = K.nc, K.I, K.W
    BD = K.upto if K.upto.startswith("BD") else ""
    skip_prep = BD in ("BDmm",)
    stGU = ExitStack()
    gstage_lo = K.sb(stGU, "b_gstlo", [128, 2 * L], BF16)
    wgT = K.sb(stGU, "b_wgT", [128, NDC * FF], BF16)
    wuT = K.sb(stGU, "b_wuT", [128, NDC * FF], BF16)

    st = ExitStack()
    tpsl = [K.ps(st, f"bp_tp{i}", [128, 128], BF16) for i in range(2)]
    t_tpev = [0, 0]
    for (srcdr, dstsb, nch, R, tag) in ([] if skip_prep else
                                        [(K.wg, wgT, 24, 1024, "g"),
                                         (K.wu, wuT, 24, 1024, "u")][
                                            0:(1 if BD == "BDmm2g" else 2)]):
        nblk = R // 32
        stj = ExitStack()
        wraw = K.sb(stj, f"bp_wraw{tag}", [128, 2 * R], F32)
        blkm = K.sb(stj, f"bp_blkm{tag}", [128, nblk], F32)
        sblk = K.sb(stj, f"bp_sblk{tag}", [128, nblk], F32)
        sblk2 = K.sb(stj, f"bp_sblk2{tag}", [128, nblk], F32)
        iblk = K.sb(stj, f"bp_iblk{tag}", [128, nblk], F32)
        tf = K.sb(stj, f"bp_tf{tag}", [128, R], F32)
        wqb = K.sb(stj, f"bp_wqb{tag}", [128, R], BF16)
        t_wl = [0, 0]
        t_cons = [0, 0]
        for ch in range(nch):
            par = ch % 2
            W(nc.sync, "sV", t_cons[par])
            if tag == "d":
                srcsl = srcdr[(ch // 2) * 128:(ch // 2 + 1) * 128,
                              (ch % 2) * 1536:(ch % 2 + 1) * 1536]
            else:
                srcsl = srcdr[ch * 128:(ch + 1) * 128, :]
            t_wl[par] = I(nc.sync.dma_start(
                out=wraw[:, par * R: par * R + R], in_=srcsl), "dS")
            W(nc.vector, "dS", t_wl[par])
            W(nc.vector, "sT")
            wr = wraw[:, par * R: par * R + R]
            wv = wr.rearrange("p (nb t) -> p nb t", t=32)
            nc.vector.tensor_reduce(blkm[:, :nblk], wv, AX.X, ALU.max,
                                    apply_absolute_value=True)
            t_ts = I(nc.vector.tensor_scalar(out=sblk[:, :nblk], in0=blkm[:, :nblk],
                                             scalar1=float(np.float32(1 / 7.0)),
                                             scalar2=1e-12, op0=ALU.mult,
                                             op1=ALU.max), "sV", 2)
            W(nc.scalar, "sV", t_ts)
            t_cp = I(nc.scalar.activation(out=sblk2[:, :nblk], in_=sblk[:, :nblk],
                                          func=AF.Identity), "sA")
            W(nc.vector, "sA", t_cp)
            nc.vector.reciprocal(iblk[:, :nblk], sblk2[:, :nblk])
            ib = AP(_h(iblk), 0, [[nblk, 128], [1, nblk], [0, 32]])
            sb_ = AP(_h(sblk), 0, [[nblk, 128], [1, nblk], [0, 32]])
            nc.vector.tensor_tensor(tf[:, :R], wr, ib, ALU.mult)
            nc.vector.tensor_scalar(out=tf[:, :R], in0=tf[:, :R], scalar1=CMAG,
                                    scalar2=CMAG, op0=ALU.add, op1=ALU.subtract)
            t_cons[par] = I(nc.vector.tensor_tensor(wqb[:, :R], tf[:, :R], sb_,
                                                    ALU.mult), "sV", 4)
            W(nc.tensor, "sV", t_cons[par])
            for ti in range(R // 128):
                tpar = ti % 2
                W(nc.tensor, "sV", t_tpev[tpar])
                tt = nc.tensor.transpose(tpsl[tpar][:, :],
                                         wqb[:, ti * 128:(ti + 1) * 128],
                                         K.ident_sb[:, :])
                t_tp = I(tt, "sT")
                W(nc.vector, "sT", t_tp)
                if tag in ("g", "u"):
                    dst = dstsb[:, ti * FF + ch * 128: ti * FF + (ch + 1) * 128]
                else:
                    tig = (ch % 2) * 12 + ti
                    chg = ch // 2
                    dst = dstsb[:, tig * 1024 + chg * 128: tig * 1024 + (chg + 1) * 128]
                t_tpev[tpar] = I(nc.vector.tensor_copy(dst, tpsl[tpar][:, :]), "sV")
        W(nc.sync, "sV")
        stj.close()
    st.close()
    if K.upto == "Bp" or BD == "BDp":
        stGU.close()
        return

    K.joinD(nc.sync)
    nc.sync.wait_ge(K.sem["d4"], K.cnt["d4"])
    st = ExitStack()
    x2ch = K.sb(st, "b1_x2ch", [128, 2 * L], F32)
    h2q = K.sb(st, "b1_h2q", [128, NDC * L], BF16)
    h2s = K.sb(st, "b1_h2s", [128, 512], F32)
    gstage = gstage_lo
    abps = [K.ps(st, f"b1_ab{i}", [128, 512]) for i in range(2)]
    gups = [K.ps(st, f"b1_gu{i}", [128, 512]) for i in range(2)]
    ng, nu = 0, 0
    t_gup_red = [0, 0]
    t_gst_d = [0, 0]
    t_act = [0, 0]
    t_abc = [0, 0]
    t_q = 0
    t_gupMM = 0
    nb_lim = 1 if BD else NB
    nic_lim = 2 if BD else NIC
    for b in range(nb_lim):
        if BD.startswith("BDmm"):
            break
        W(nc.sync, "sT", max(K._blhs_cons, t_gupMM))
        t_bl = I(nc.sync.dma_start(
            out=K.blhs[:, :], in_=AP(K.blhsdr, (b * 2 + 1) * D,
                                     [[NB * 2 * D, 2], [1, D]])), "dS")
        I(nc.sync.dma_start(out=K.rstd_sb[:1, :], in_=K.st2dr[b, 0:1, :]), "dS")
        t_st2 = I(nc.sync.dma_start(out=K.brhs[0:1, :], in_=K.st2dr[b, 1:2, :]),
                  "dS")
        W(nc.vector, "sT")
        W(nc.vector, "dS", t_st2)
        rows_to_bf(K)
        blhs_to_bf(K)
        t_xc = [0, 0]
        for dc in range(0 if BD in ("BDonly", "BDmm") else NDC):
            par = dc % 2
            W(nc.sync, "sV", t_xc[par])
            t_x = I(nc.sync.dma_start(out=x2ch[:, par * L:(par + 1) * L],
                                      in_=K.x2dr[b, dc * 128:(dc + 1) * 128, :]),
                    "d3")
            for s in range(NS):
                W(nc.tensor, "dS", max(t_bl, t_st2))
                W(nc.tensor, "sV", t_abc[0])
                t_ab = ab_maps(K, dc, s, abps[0], abps[1])
                W(nc.vector, "sT", t_ab)
                W(nc.vector, "d3", t_x)
                xsl = x2ch[:, par * L + s * 512: par * L + (s + 1) * 512]
                nc.vector.scalar_tensor_tensor(h2s[:, :], xsl, 1.0,
                                               abps[0][:, :], ALU.mult,
                                               ALU.mult)
                nc.vector.tensor_add(h2s[:, :], h2s[:, :], abps[1][:, :])
                nc.vector.tensor_scalar(out=h2s[:, :], in0=h2s[:, :],
                                        scalar1=K.scales[:, 0:1], scalar2=CMAG,
                                        op0=ALU.mult, op1=ALU.add)
                t_q = I(nc.vector.tensor_scalar(
                    out=h2q[:, dc * L + s * 512: dc * L + (s + 1) * 512],
                    in0=h2s[:, :], scalar1=CMAG, scalar2=K.scales[:, 1:2],
                    op0=ALU.subtract, op1=ALU.mult), "sV", 4)
                t_abc[0] = t_q
                t_xc[par] = t_q
        if K.upto == "B1h" or BD == "BDh":
            continue
        if BD in ("BDonly", "BDmm"):
            t_q = K.cnt["sV"]
        for Ic in range(nic_lim):
            if K.upto == "B1bar":
                K.bar()
            for it, (wsb_, drt, redc) in enumerate(((wgT, K.gatedr, K.redcg),
                                                    (wuT, K.updr, K.redcu))):
                for s in range(NS):
                    W(nc.tensor, "sV", t_q)
                    W(nc.tensor, "sA", t_act[it])
                    for dc in range(NDC):
                        lhsw = (K.ident_sb[:, :] if BD in ("BDident", "BDmm") else
                                wsb_[:, dc * FF + Ic * 128: dc * FF + (Ic + 1) * 128])
                        mm = nc.tensor.matmul(
                            gups[it][:, :], lhsw,
                            h2q[:, dc * L + s * 512: dc * L + (s + 1) * 512],
                            start=(dc == 0), stop=(dc == NDC - 1))
                    t_gupMM = t_mm = I(mm, "sT")
                    W(nc.vector, "sT", t_mm)
                    if s == 0:
                        W(nc.vector, "d4" if it == 0 else "d5", t_gst_d[it])
                    t_act[it] = I(nc.vector.tensor_copy(
                        gstage[:, it * L + s * 512: it * L + (s + 1) * 512],
                        gups[it][:, :]), "sA")
                idx = ng if it == 0 else nu
                t_gup_red[it] = I(nc.vector.tensor_reduce(
                    redc[:, idx:idx + 1], gstage[:, it * L:(it + 1) * L],
                    AX.X, ALU.max, apply_absolute_value=True), "sV")
                if it == 0:
                    ng += 1
                else:
                    nu += 1
                W(nc.gpsimd, "sA", t_act[it])
                W(nc.gpsimd, "sV", t_gup_red[it])
                t_gst_d[it] = I(nc.gpsimd.dma_start(
                    out=drt[b, Ic * 128:(Ic + 1) * 128, :],
                    in_=gstage[:, it * L:(it + 1) * L]), "d4" if it == 0 else "d5")
    if BD in ("BDmm", "BDmm2", "BDmm2g", "BDmm3", "BDmm4", "BDmm5", "BDmm6", "BDmm7", "BDmm8"):
        if BD in ("BDmm3", "BDmm4", "BDmm5", "BDmm6", "BDmm7", "BDmm8"):
            t_bl = I(nc.sync.dma_start(
                out=K.blhs[:, :], in_=AP(K.blhsdr, D, [[NB * 2 * D, 2], [1, D]])),
                "dS")
            I(nc.sync.dma_start(out=K.rstd_sb[:1, :], in_=K.st2dr[0, 0:1, :]), "dS")
            t_st2 = I(nc.sync.dma_start(out=K.brhs[0:1, :], in_=K.st2dr[0, 1:2, :]),
                      "dS")
        if BD in ("BDmm4", "BDmm5", "BDmm6", "BDmm7", "BDmm8"):
            W(nc.vector, "dS", t_st2)
            rows_to_bf(K)
            blhs_to_bf(K)
        t_act2 = [0, 0]
        for Ic in range(2):
            for it in range(2):
                for sx in range(NS):
                    W(nc.tensor, "sA", t_act2[it])
                    for dc in range(NDC):
                        lw = ((wgT if it == 0 else wuT)[:, dc * FF + Ic * 128:
                                                        dc * FF + (Ic + 1) * 128]
                              if BD == "BDmm8" else K.ident_sb[:, :])
                        mm = nc.tensor.matmul(
                            gups[it][:, :], lw,
                            h2q[:, dc * L + sx * 512: dc * L + (sx + 1) * 512],
                            start=(dc == 0), stop=(dc == NDC - 1))
                    t_mm = I(mm, "sT")
                    W(nc.vector, "sT", t_mm)
                    if BD in ("BDmm6", "BDmm7", "BDmm8"):
                        t_act2[it] = I(nc.vector.tensor_copy(
                            gstage[:, it * L + sx * 512: it * L + (sx + 1) * 512],
                            gups[it][:, :]), "sA")
                    else:
                        t_act2[it] = I(nc.vector.tensor_copy(h2s[:, :],
                                                             gups[it][:, :]), "sA")
                if BD in ("BDmm5", "BDmm7", "BDmm8"):
                    I(nc.vector.tensor_reduce(
                        K.redcg[:, Ic * 2 + it: Ic * 2 + it + 1],
                        gstage[:, it * L:(it + 1) * L], AX.X, ALU.max,
                        apply_absolute_value=True), "sV")
    if not BD and K.upto not in ("B1h", "B1m", "B1m0", "B1m1", "B1m2", "B1mA",
                                 "B1bar"):
        finalize_scale(K, [K.redcg, K.redcu], 96, K.ar1i, K.ar1o,
                       [(2, 3), (4, 5)], 2)
    st.close()
    if BD or K.upto in ("B1h", "B1m", "B1m0", "B1m1", "B1m2", "B1mA", "B1bar"):
        stGU.close()
        return
    if K.upto == "B1":
        stGU.close()
        return

    nc.sync.wait_ge(K.sem["d4"], K.cnt["d4"])
    nc.sync.wait_ge(K.sem["d5"], K.cnt["d5"])
    st = ExitStack()
    gch = K.sb(st, "b2_gch", [128, 2 * L], BF16)
    gq = K.sb(st, "b2_gq", [128, L], BF16)
    qsc = K.sb(st, "b2_qsc", [128, L], F32)
    sig = K.sb(st, "b2_sig", [128, L], F32)
    g2b = K.sb(st, "b2_g2", [128, 2 * L], BF16)
    n2 = 0
    t_g2d = [0, 0]
    t_cons = [0, 0]
    for b in range(NB):
        for Ic in range(NIC):
            par = (b * NIC + Ic) % 2
            W(nc.sync, "sV", t_cons[par])
            t_in = I(nc.sync.dma_start(
                out=gch[:, par * L:(par + 1) * L],
                in_=K.gatedr[b, Ic * 128:(Ic + 1) * 128, :]), "d2")
            W(nc.vector, "d2", t_in)
            gs = gch[:, par * L:(par + 1) * L]
            nc.vector.tensor_scalar(out=qsc[:, :], in0=gs, scalar1=K.scales[:, 2:3],
                                    scalar2=CMAG, op0=ALU.mult, op1=ALU.add)
            t_gq = I(nc.vector.tensor_scalar(
                out=gq[:, :], in0=qsc[:, :], scalar1=CMAG, scalar2=K.scales[:, 3:4],
                op0=ALU.subtract, op1=ALU.mult), "sV", 2)
            W(nc.scalar, "sV", t_gq)
            t_sg = I(nc.scalar.activation(out=sig[:, :], in_=gq[:, :],
                                          func=AF.Sigmoid), "sA")
            W(nc.vector, "sA", t_sg)
            nc.vector.tensor_scalar(out=sig[:, :], in0=sig[:, :], scalar1=65536.0,
                                    scalar2=CMAG, op0=ALU.mult, op1=ALU.add)
            nc.vector.tensor_scalar(out=sig[:, :], in0=sig[:, :], scalar1=CMAG,
                                    scalar2=65535.0, op0=ALU.subtract, op1=ALU.min)
            W(nc.vector, "d4", t_g2d[par])
            nc.vector.scalar_tensor_tensor(g2b[:, par * L:(par + 1) * L], gq[:, :],
                                           float(np.float32(1 / 65536.0)),
                                           sig[:, :], ALU.mult, ALU.mult)
            t_cons[par] = I(nc.vector.tensor_reduce(
                K.redcg[:, n2:n2 + 1], g2b[:, par * L:(par + 1) * L], AX.X,
                ALU.max, apply_absolute_value=True), "sV", 4)
            n2 += 1
            W(nc.gpsimd, "sV", t_cons[par])
            t_g2d[par] = I(nc.gpsimd.dma_start(
                out=K.g2dr[b, Ic * 128:(Ic + 1) * 128, :],
                in_=g2b[:, par * L:(par + 1) * L]), "d4")
    finalize_scale(K, K.redcg, n2, K.ar2i, K.ar2o, [(6, 7)], 1)
    st.close()
    if K.upto == "B2":
        stGU.close()
        return

    nc.sync.wait_ge(K.sem["d4"], K.cnt["d4"])
    nc.sync.wait_ge(K.sem["d5"], K.cnt["d5"])
    st = ExitStack()
    g2ch = K.sb(st, "b3_g2ch", [128, 2 * L], BF16)
    upch = K.sb(st, "b3_upch", [128, 2 * L], BF16)
    oq = K.sb(st, "b3_oq", [128, L], BF16)
    qsc3 = K.sb(st, "b3_qsc", [128, L], F32)
    upq = K.sb(st, "b3_upq", [128, L], BF16)
    ob = K.sb(st, "b3_ob", [128, 2 * L], BF16)
    n3 = 0
    t_od = [0, 0]
    t_cons = [0, 0]
    for b in range(NB):
        for Ic in range(NIC):
            par = (b * NIC + Ic) % 2
            W(nc.sync, "sV", t_cons[par])
            I(nc.sync.dma_start(out=g2ch[:, par * L:(par + 1) * L],
                                in_=K.g2dr[b, Ic * 128:(Ic + 1) * 128, :]), "d2")
            t_in = I(nc.sync.dma_start(
                out=upch[:, par * L:(par + 1) * L],
                in_=K.updr[b, Ic * 128:(Ic + 1) * 128, :]), "d2")
            W(nc.vector, "d2", t_in)
            g2s = g2ch[:, par * L:(par + 1) * L]
            ups = upch[:, par * L:(par + 1) * L]
            nc.vector.tensor_scalar(out=qsc3[:, :], in0=g2s, scalar1=K.scales[:, 6:7],
                                    scalar2=CMAG, op0=ALU.mult, op1=ALU.add)
            nc.vector.tensor_scalar(out=oq[:, :], in0=qsc3[:, :], scalar1=CMAG,
                                    scalar2=K.scales[:, 7:8], op0=ALU.subtract,
                                    op1=ALU.mult)
            nc.vector.tensor_scalar(out=qsc3[:, :], in0=ups, scalar1=K.scales[:, 4:5],
                                    scalar2=CMAG, op0=ALU.mult, op1=ALU.add)
            nc.vector.tensor_scalar(out=upq[:, :], in0=qsc3[:, :], scalar1=CMAG,
                                    scalar2=K.scales[:, 5:6], op0=ALU.subtract,
                                    op1=ALU.mult)
            W(nc.vector, "d4", t_od[par])
            nc.vector.tensor_mul(ob[:, par * L:(par + 1) * L], oq[:, :], upq[:, :])
            t_cons[par] = I(nc.vector.tensor_reduce(
                K.redcg[:, n3:n3 + 1], ob[:, par * L:(par + 1) * L], AX.X,
                ALU.max, apply_absolute_value=True), "sV", 6)
            n3 += 1
            W(nc.gpsimd, "sV", t_cons[par])
            t_od[par] = I(nc.gpsimd.dma_start(
                out=K.odr[b, Ic * 128:(Ic + 1) * 128, :],
                in_=ob[:, par * L:(par + 1) * L]), "d4")
    finalize_scale(K, K.redcg, n3, K.ar3i, K.ar3o, [(8, 9)], 1)
    st.close()
    stGU.close()
    if K.upto == "B3":
        return

    nc.sync.wait_ge(K.sem["d4"], K.cnt["d4"])
    stW = ExitStack()
    wdT = K.sb(stW, "b_wdT", [128, NIC * 1024], BF16)
    stp = ExitStack()
    tpsl = [K.ps(stp, f"b4p_tp{i}", [128, 128], BF16) for i in range(2)]
    t_tpev = [0, 0]
    for (srcdr, dstsb, nch, R, tag) in [(K.wdn, wdT, 16, 1536, "d")]:
        nblk = R // 32
        stj = ExitStack()
        wraw = K.sb(stj, f"bp_wraw{tag}", [128, 2 * R], F32)
        blkm = K.sb(stj, f"bp_blkm{tag}", [128, nblk], F32)
        sblk = K.sb(stj, f"bp_sblk{tag}", [128, nblk], F32)
        sblk2 = K.sb(stj, f"bp_sblk2{tag}", [128, nblk], F32)
        iblk = K.sb(stj, f"bp_iblk{tag}", [128, nblk], F32)
        tf = K.sb(stj, f"bp_tf{tag}", [128, R], F32)
        wqb = K.sb(stj, f"bp_wqb{tag}", [128, R], BF16)
        t_wl = [0, 0]
        t_cons = [0, 0]
        for ch in range(nch):
            par = ch % 2
            W(nc.sync, "sV", t_cons[par])
            srcsl = srcdr[(ch // 2) * 128:(ch // 2 + 1) * 128,
                          (ch % 2) * 1536:(ch % 2 + 1) * 1536]
            t_wl[par] = I(nc.sync.dma_start(
                out=wraw[:, par * R: par * R + R], in_=srcsl), "d6")
            W(nc.vector, "d6", t_wl[par])
            W(nc.vector, "sT")
            wr = wraw[:, par * R: par * R + R]
            wv = wr.rearrange("p (nb t) -> p nb t", t=32)
            nc.vector.tensor_reduce(blkm[:, :nblk], wv, AX.X, ALU.max,
                                    apply_absolute_value=True)
            t_ts = I(nc.vector.tensor_scalar(out=sblk[:, :nblk], in0=blkm[:, :nblk],
                                             scalar1=float(np.float32(1 / 7.0)),
                                             scalar2=1e-12, op0=ALU.mult,
                                             op1=ALU.max), "sV", 2)
            W(nc.scalar, "sV", t_ts)
            t_cp = I(nc.scalar.activation(out=sblk2[:, :nblk], in_=sblk[:, :nblk],
                                          func=AF.Identity), "sA")
            W(nc.vector, "sA", t_cp)
            nc.vector.reciprocal(iblk[:, :nblk], sblk2[:, :nblk])
            ib = AP(_h(iblk), 0, [[nblk, 128], [1, nblk], [0, 32]])
            sb_ = AP(_h(sblk), 0, [[nblk, 128], [1, nblk], [0, 32]])
            nc.vector.tensor_tensor(tf[:, :R], wr, ib, ALU.mult)
            nc.vector.tensor_scalar(out=tf[:, :R], in0=tf[:, :R], scalar1=CMAG,
                                    scalar2=CMAG, op0=ALU.add, op1=ALU.subtract)
            t_cons[par] = I(nc.vector.tensor_tensor(wqb[:, :R], tf[:, :R], sb_,
                                                    ALU.mult), "sV", 4)
            W(nc.tensor, "sV", t_cons[par])
            for ti in range(R // 128):
                tpar = ti % 2
                W(nc.tensor, "sV", t_tpev[tpar])
                tt = nc.tensor.transpose(tpsl[tpar][:, :],
                                         wqb[:, ti * 128:(ti + 1) * 128],
                                         K.ident_sb[:, :])
                t_tp = I(tt, "sT")
                W(nc.vector, "sT", t_tp)
                tig = (ch % 2) * 12 + ti
                chg = ch // 2
                dst = dstsb[:, tig * 1024 + chg * 128: tig * 1024 + (chg + 1) * 128]
                t_tpev[tpar] = I(nc.vector.tensor_copy(dst, tpsl[tpar][:, :]), "sV")
        W(nc.sync, "sV")
        stj.close()
    stp.close()
    st = ExitStack()
    oqsb = K.sb(st, "b4_oq", [128, NIC * L], BF16)
    och = K.sb(st, "b4_och", [128, L], BF16)
    x2ch = K.sb(st, "b4_x2", [128, L], F32)
    qsc4 = x2ch
    t1b = K.sb(st, "b4_t1b", [128, 2 * 512], F32)
    ost = K.sb(st, "b4_ost", [128, L], F32)
    dnps = [K.ps(st, f"b4_dn{i}", [128, 512]) for i in range(2)]
    t_in = [0, 0]
    t_oq = 0
    t_dnMM = 0
    t_ostd = [0, 0]
    t_ev = [0, 0]
    t_t1c = 0
    for b in range(NB):
        W(nc.vector, "sT", t_dnMM)
        for Ic in range(NIC):
            par = 0
            W(nc.sync, "sV", t_in[par])
            t_l = I(nc.sync.dma_start(out=och[:, :],
                                      in_=K.odr[b, Ic * 128:(Ic + 1) * 128, :]),
                    "d2")
            W(nc.vector, "d2", t_l)
            nc.vector.tensor_scalar(out=qsc4[:, :], in0=och[:, :],
                                    scalar1=K.scales[:, 8:9], scalar2=CMAG,
                                    op0=ALU.mult, op1=ALU.add)
            t_oq = I(nc.vector.tensor_scalar(
                out=oqsb[:, Ic * L:(Ic + 1) * L], in0=qsc4[:, :],
                scalar1=CMAG, scalar2=K.scales[:, 9:10], op0=ALU.subtract,
                op1=ALU.mult), "sV", 2)
            t_in[par] = t_oq
        if K.dbg_taps:
            W(nc.gpsimd, "sV", t_oq)
            I(nc.gpsimd.dma_start(out=K.taps["tap_oq"][b, :, :], in_=oqsb[:, :]),
              "dG")
        t_x2c = [0, 0]
        for dc in range(NDC):
            par = dc % 2
            W(nc.sync, "sV", max(t_x2c[0], t_oq))
            t_x = I(nc.sync.dma_start(out=x2ch[:, :],
                                      in_=K.x2dr[b, dc * 128:(dc + 1) * 128, :]),
                    "d3")
            for s in range(NS):
                spar = s % 2
                W(nc.tensor, "sV", max(t_oq, t_ev[spar]))
                W(nc.tensor, "sA", t_t1c)
                for Ic in range(NIC):
                    mm = nc.tensor.matmul(
                        dnps[spar][:, :],
                        wdT[:, Ic * 1024 + dc * 128: Ic * 1024 + (dc + 1) * 128],
                        oqsb[:, Ic * L + s * 512: Ic * L + (s + 1) * 512],
                        start=(Ic == 0), stop=(Ic == NIC - 1))
                t_dnMM = t_mm = I(mm, "sT")
                W(nc.scalar, "sT", t_mm)
                W(nc.scalar, "sV", t_ev[spar])
                t_t1c = I(nc.scalar.activation(
                    out=t1b[:, spar * 512:(spar + 1) * 512], in_=dnps[spar][:, :],
                    func=AF.Identity, scale=vcol(K, b, 5, dc), bias=0.0), "sA")
                W(nc.vector, "sA", t_t1c)
                W(nc.vector, "d3", t_x)
                if s == 0:
                    W(nc.vector, "d4", t_ostd[0])
                t_o = I(nc.vector.tensor_add(
                    ost[:, s * 512:(s + 1) * 512],
                    t1b[:, spar * 512:(spar + 1) * 512],
                    x2ch[:, s * 512:(s + 1) * 512]), "sV")
                t_ev[spar] = t_o
                t_x2c[0] = t_o
            W(nc.gpsimd, "sV", t_o)
            t_ostd[0] = I(nc.gpsimd.dma_start(
                out=K.outfm[b, dc * 128:(dc + 1) * 128, :],
                in_=ost[:, :]), "d4")
    W(nc.gpsimd, "d4")
    W(nc.gpsimd, "dG")
    st.close()
    stW.close()


def _prep_inputs(inputs, r):
    f32 = np.float32
    u = np.ascontiguousarray(
        np.asarray(inputs["L0_attn_u"], f32)[4 * r:4 * r + 4].transpose(0, 2, 1))
    def _projp(w):
        wt = np.asarray(w, f32).T
        return np.ascontiguousarray(
            wt.reshape(8, 128, 8, 128).transpose(2, 1, 0, 3).reshape(D, D))

    wqT = _projp(inputs["L0_attn_Wq"])
    wkT = _projp(inputs["L0_attn_Wk"])
    wvT = _projp(inputs["L0_attn_Wv"])
    p = np.arange(1024)
    perm = (p % 8) * 128 + (p // 128) * 16 + ((p % 128) // 8)
    woTp = np.ascontiguousarray(np.asarray(inputs["L0_attn_Wo"], f32).T[perm])
    biases = np.ascontiguousarray(
        np.concatenate([np.asarray(inputs[k], f32).reshape(8, 128).T
                        for k in ("L0_attn_bq", "L0_attn_bk", "L0_attn_bv",
                                  "L0_attn_bo")], axis=1))
    kkpad = np.pad(np.asarray(inputs["L0_attn_ssm_k_kernel"], f32),
                   ((0, 0), (127, 1)))
    skD = np.ascontiguousarray(
        np.asarray(inputs["L0_attn_ssm_k_D"], f32).reshape(8, 128).T)
    kcpad = np.pad(np.asarray(inputs["L0_attn_ssm_kernel"], f32),
                   ((0, 0), (127, 1)))
    Dc = np.ascontiguousarray(np.asarray(inputs["L0_attn_D"], f32).reshape(128, 1))
    return {
        "u_fm": u, "wqT": wqT, "wkT": wkT, "wvT": wvT, "woTp": woTp,
        "biases": biases, "kkpad": kkpad, "skD": skD, "kcpad": kcpad, "Dc": Dc,
        "wg": np.asarray(inputs["L0_ffn_w_gate"], f32),
        "wu": np.asarray(inputs["L0_ffn_w_up"], f32),
        "wdn": np.asarray(inputs["L0_ffn_w_down"], f32),
        "wmod": np.asarray(inputs["L0_wmod"], f32),
        "c_own": np.ascontiguousarray(np.asarray(inputs["L0_c"], f32)[4 * r:4 * r + 4]),
    }


def kernel(**inputs):
    nc = build(dbg_taps=False)
    in_maps = [_prep_inputs(inputs, r) for r in range(8)]
    res = run_bass_kernel_spmd(nc, in_maps, core_ids=list(range(8)))
    out = np.concatenate([np.asarray(res.results[r]["outfm"]) for r in range(8)],
                         axis=0)
    return np.ascontiguousarray(out.transpose(0, 2, 1))


# version 05
# speedup vs baseline: 1.3124x; 1.1746x over previous
import numpy as np
import ml_dtypes
from contextlib import ExitStack

import concourse.bass as bass
import concourse.mybir as mybir
from concourse.ap import AP
from concourse.bass_utils import run_bass_kernel_spmd

F32 = mybir.dt.float32
BF16 = mybir.dt.bfloat16
AF = mybir.ActivationFunctionType
ALU = mybir.AluOpType
AX = mybir.AxisListType

CMAG = float(np.float32(1.5 * 2 ** 23))
NB, D, L, BL, NI = 4, 1024, 2048, 128, 16
NDC, NEC, NH = 8, 8, 8
FF, NIC, NS = 3072, 24, 4
EPS = 1e-5
BEI = NB * D * NI


def _h(t):
    return t.tensor if isinstance(t, AP) else t


class KB:
    def __init__(self, dbg_taps=False):
        nc = bass.Bass("TRN2", target_bir_lowering=False,
                       detect_race_conditions=False)
        self.nc = nc
        self.st = ExitStack()
        self.sem = {}
        self.cnt = {}
        for n in ("dS", "dG", "d2", "d3", "d4", "d5", "d6",
                  "d7", "d8", "d9", "dA", "dB",
                  "sT", "sV", "sA", "sCC"):
            self.sem[n] = self.st.enter_context(nc.semaphore(n))
            self.cnt[n] = 0
        self.dbg_taps = dbg_taps

    def I(self, instr, name, amt=None):
        if amt is None:
            amt = 16 if name.startswith("d") else 1
        if name.startswith("d") and self.cnt[name] > 0:
            instr._wait_ge(self.sem[name], self.cnt[name])
        instr.then_inc(self.sem[name], amt)
        self.cnt[name] += amt
        return self.cnt[name]

    def W(self, eng, name, val=None):
        v = self.cnt[name] if val is None else val
        if v > 0:
            eng.wait_ge(self.sem[name], v)

    def sb(self, stack, n, s, d):
        return stack.enter_context(self.nc.sbuf_tensor(n, s, d))

    def ps(self, stack, n, s, d=F32):
        return stack.enter_context(self.nc.psum_tensor(n, s, d))

    def bar(self):
        self.nc.all_engine_barrier()

    def joinD(self, eng):
        for n in ("dS", "dG", "d2", "d3", "d4", "d5", "d6",
                  "d7", "d8", "d9", "dA", "dB"):
            if self.cnt[n] > 0:
                eng.wait_ge(self.sem[n], self.cnt[n])


def build(dbg_taps=False, rg=None, upto='B'):
    K = KB(dbg_taps)
    K.rg = rg or [list(range(8))]
    K.upto = upto
    nc = K.nc

    di = lambda n, s: nc.dram_tensor(n, s, F32, kind="ExternalInput")
    K.u_fm = di("u_fm", [NB, D, L])
    K.wqT, K.wkT, K.wvT = di("wqT", [D, D]), di("wkT", [D, D]), di("wvT", [D, D])
    K.woTp = di("woTp", [D, D])
    K.biases = di("biases", [128, 32])
    K.kkpad = di("kkpad", [D, 2176])
    K.skD = di("skD", [128, 8])
    K.kcpad = di("kcpad", [128, 2176])
    K.Dc = di("Dc", [128, 1])
    K.wg, K.wu = di("wg", [FF, D]), di("wu", [FF, D])
    K.wdn = di("wdn", [D, FF])
    K.wmod = di("wmod", [D, 6 * D])
    K.c_own = di("c_own", [NB, D])
    K.outfm = nc.dram_tensor("outfm", [NB, D, L], F32, kind="ExternalOutput")

    ds = lambda n, s, d=F32: nc.dram_tensor(n, s, d)
    K.kkbf = ds("kkbf", [D, 2176], BF16)
    K.kcbf = ds("kcbf", [128, 2176], BF16)
    K.qtm, K.vtm = ds("qtm", [128, BEI], BF16), ds("vtm", [128, BEI], BF16)
    K.ktm, K.kptm = ds("ktm", [128, BEI], BF16), ds("kptm", [128, BEI], BF16)
    K.ytr = ds("ytr", [NB * 8, 128, L], BF16)
    K.x2dr = ds("x2dr", [NB, D, L], F32)
    K.st2dr = ds("st2dr", [NB, 2, L], F32)
    K.moddr = ds("moddr", [NB, 6 * D], F32)
    K.blhsdr = ds("blhsdr", [2, NB * 2 * D], F32)
    K.gatedr, K.updr = ds("gatedr", [NB, FF, L], BF16), ds("updr", [NB, FF, L], BF16)
    K.g2dr, K.odr = ds("g2dr", [NB, FF, L], BF16), ds("odr", [NB, FF, L], BF16)
    K.ar0i, K.ar0o = ds("ar0i", [1, 8]), ds("ar0o", [1, 8])
    K.ar1i, K.ar1o = ds("ar1i", [1, 8]), ds("ar1o", [1, 8])
    K.ar2i, K.ar2o = ds("ar2i", [1, 8]), ds("ar2o", [1, 8])
    K.ar3i, K.ar3o = ds("ar3i", [1, 8]), ds("ar3o", [1, 8])

    K.identbf = nc.inline_tensor(np.eye(128, dtype=ml_dtypes.bfloat16), "identbf")
    K.identf = nc.inline_tensor(np.eye(128, dtype=np.float32), "identf")
    K.c1024 = nc.inline_tensor(
        np.full((128, 1), 1.0 / 1024.0, dtype=ml_dtypes.bfloat16), "c1024")
    K.onescol = nc.inline_tensor(np.ones((1, 128), np.float32), "onescol")
    K.onesrow = nc.inline_tensor(np.ones((1, L), np.float32), "onesrow")

    if dbg_taps:
        K.taps = {}
        for n, s, d in [("tap_h", [128, NDC * L], BF16),
                        ("tap_ktm", [128, BEI], BF16),
                        ("tap_kptm", [128, BEI], BF16),
                        ("tap_ytr", [NB * 8, 128, L], BF16),
                        ("tap_x2", [NB, D, L], F32),
                        ("tap_scales", [128, 16], F32),
                        ("tap_oq", [NB, 128, NIC * L], BF16),
                        ("tap_vecs", [128, NB * 6 * NDC], F32),
                        ("tap_gabo", [128, NB * NDC], F32)]:
            K.taps[n] = nc.dram_tensor(n, s, d, kind="ExternalOutput")

    P = K.st
    K.ident_sb = K.sb(P, "ident_sb", [128, 128], BF16)
    K.identf_sb = K.sb(P, "identf_sb", [128, 128], F32)
    K.c1024_sb = K.sb(P, "c1024_sb", [128, 1], BF16)
    K.onescol_sb = K.sb(P, "onescol_sb", [1, 128], F32)
    K.bias_sb = K.sb(P, "bias_sb", [128, 32], F32)
    K.vecs = K.sb(P, "vecs", [128, NB * 6 * NDC], F32)
    K.gabo = K.sb(P, "gabo", [128, NB * NDC], F32)
    K.blhs = K.sb(P, "blhs", [2, D], F32)
    K.blhs_bf = K.sb(P, "blhs_bf", [2, D], BF16)
    K.rstd_bf = K.sb(P, "rstd_bf", [1, L], BF16)
    K.brhs_bf = K.sb(P, "brhs_bf", [2, L], BF16)
    K.brhs = K.sb(P, "brhs", [2, L], F32)
    K.rstd_sb = K.sb(P, "rstd_sb", [1, L], F32)
    K.scales = K.sb(P, "scales", [128, 16], F32)
    K.red1 = K.sb(P, "red1", [128, 1], F32)
    K.scrow = K.sb(P, "scrow", [1, 12], F32)
    K.redcg = K.sb(P, "redcg", [128, 384], F32)
    K.redcu = K.sb(P, "redcu", [128, 384], F32)

    K._blhs_cons = 0
    K._lnev = 0
    K._fin_t = 0

    if upto.startswith("BD"):
        order = [("P0", phase_P0), ("B", phase_B)]
    else:
        order = [("P0", phase_P0), ("A1", phase_A1), ("A2", phase_A2),
                 ("A3", phase_A3), ("A4", phase_A4)]
        if upto in ("B", "Bp", "B1", "B2", "B3", "B1h", "B1m", "B1m0", "B1m1",
                    "B1m2", "B1mA", "B1bar"):
            order.append(("B", phase_B))
    for name, fn in order:
        fn(K)
        K.bar()
        if name == upto or (name == "B"):
            break
    K.st.close()
    build._last_K = K
    return nc


def vcol(K, b, kind, dc):
    c = (kind * NDC + dc) * NB + b
    return K.vecs[:, c:c + 1]


def phase_P0(K):
    nc, I, W = K.nc, K.I, K.W
    st = ExitStack()
    kch = K.sb(st, "p0_kch", [128, 2176], F32)
    kchb = K.sb(st, "p0_kchb", [128, 2176], BF16)
    dcol = K.sb(st, "p0_dcol", [128, 8], F32)
    dccol = K.sb(st, "p0_dccol", [128, 1], F32)
    silc = K.sb(st, "p0_silc", [NB, D], F32)
    silcb = K.sb(st, "p0_silcb", [NB, D], BF16)
    silT = K.sb(st, "p0_silT", [128, NDC * NB], BF16)
    wmch = K.sb(st, "p0_wmch", [128, 2048], F32)
    wmchb = K.sb(st, "p0_wmchb", [128, 2 * 2048], BF16)
    mod_sb = K.sb(st, "p0_mod", [NB, 6 * D], F32)
    msk = K.sb(st, "p0_msk", [NB, 2 * D], mybir.dt.uint8)
    tpos = K.sb(st, "p0_tpos", [NB, 2 * D], F32)
    tneg = K.sb(st, "p0_tneg", [NB, 2 * D], F32)
    tps = K.ps(st, "p0_tps", [128, NB], BF16)
    tpsf = K.ps(st, "p0_tpsf", [128, 128], F32)
    mps = K.ps(st, "p0_mps", [NB, 2048], F32)

    nc.vector.memset(K.scales[:, :], 0.0)
    nc.vector.memset(K.scrow[:1, :], 0.0)
    I(nc.sync.dma_start(out=K.ident_sb[:, :], in_=K.identbf[:, :]), "dS")
    I(nc.sync.dma_start(out=K.identf_sb[:, :], in_=K.identf[:, :]), "dS")
    I(nc.sync.dma_start(out=K.c1024_sb[:, :], in_=K.c1024[:, :]), "dS")
    I(nc.sync.dma_start(out=K.onescol_sb[:, :], in_=K.onescol[:, :]), "dS")
    I(nc.sync.dma_start(out=K.bias_sb[:, :], in_=K.biases[:, :]), "dS")
    I(nc.sync.dma_start(out=K.brhs[1:2, :], in_=K.onesrow[:, :]), "dS")
    I(nc.sync.dma_start(out=dcol[:, :], in_=K.skD[:, :]), "dS")
    I(nc.sync.dma_start(out=dccol[:, :], in_=K.Dc[:, :]), "dS")
    t_small = I(nc.sync.dma_start(out=silc[:, :], in_=K.c_own[:, :]), "dS")

    t_store = 0
    for ci in range(9):
        src = K.kcpad[:, :] if ci == 0 else K.kkpad[(ci - 1) * 128:ci * 128, :]
        t_ld = I(nc.sync.dma_start(out=kch[:, :], in_=src), "dS")
        W(nc.vector, "dS", t_ld)
        W(nc.vector, "dG", t_store)
        dv = dccol[:, 0:1] if ci == 0 else dcol[:, ci - 1:ci]
        nc.vector.tensor_add(kch[:, 127:128], kch[:, 127:128], dv)
        t_cast = I(nc.vector.tensor_copy(kchb[:, :], kch[:, :]), "sV", 2)
        W(nc.sync, "sV", t_cast)
        W(nc.gpsimd, "sV", t_cast)
        dst = K.kcbf[:, :] if ci == 0 else K.kkbf[(ci - 1) * 128:ci * 128, :]
        t_store = I(nc.gpsimd.dma_start(out=dst, in_=kchb[:, :]), "dG")

    W(nc.scalar, "dS", t_small)
    t_sil = I(nc.scalar.activation(out=silc[:, :], in_=silc[:, :], func=AF.Silu),
              "sA")
    W(nc.vector, "sA", t_sil)
    t_silb = I(nc.vector.tensor_copy(silcb[:, :], silc[:, :]), "sV")
    for dc in range(NDC):
        W(nc.tensor, "sV", t_silb)
        if dc >= 1:
            W(nc.tensor, "sV", t_tpc)
        t_tp = I(nc.tensor.transpose(tps[:, :], silcb[:, dc * 128:(dc + 1) * 128],
                                     K.ident_sb[0:NB, 0:NB]), "sT")
        W(nc.vector, "sT", t_tp)
        t_tpc = I(nc.vector.tensor_copy(silT[:, dc * NB:(dc + 1) * NB], tps[:, :]),
                  "sV")
    t_mm = 0
    for s3 in range(3):
        for dc in range(NDC):
            W(nc.sync, "sV")
            t_wl = I(nc.sync.dma_start(
                out=wmch[:, :], in_=K.wmod[dc * 128:(dc + 1) * 128,
                                           s3 * 2048:(s3 + 1) * 2048]), "dS")
            W(nc.vector, "dS", t_wl)
            par = dc % 2
            if t_mm:
                W(nc.vector, "sT", t_mm)
            t_cb = I(nc.vector.tensor_copy(wmchb[:, par * 2048:(par + 1) * 2048],
                                           wmch[:, :]), "sV")
            W(nc.tensor, "sV", t_cb)
            for s5 in range(4):
                mm = nc.tensor.matmul(
                    mps[:, s5 * 512:(s5 + 1) * 512],
                    silT[:, dc * NB:(dc + 1) * NB],
                    wmchb[:, par * 2048 + s5 * 512: par * 2048 + (s5 + 1) * 512],
                    start=(dc == 0), stop=(dc == NDC - 1))
            t_mm = I(mm, "sT")
        W(nc.vector, "sT", t_mm)
        I(nc.vector.tensor_copy(mod_sb[:, s3 * 2048:(s3 + 1) * 2048], mps[:, :]),
          "sV")
    for kk_ in (2, 5):
        gsl = mod_sb[:, kk_ * D:(kk_ + 1) * D]
        nc.vector.tensor_scalar(out=msk[:, 0:D], in0=gsl, scalar1=0.0,
                                scalar2=None, op0=ALU.is_ge)
        nc.vector.tensor_scalar_max(tpos[:, 0:D], gsl, 0.25)
        nc.vector.tensor_scalar_min(tneg[:, 0:D], gsl, -0.25)
        nc.vector.select(gsl, msk[:, 0:D], tpos[:, 0:D], tneg[:, 0:D])
    csl = AP(_h(mod_sb), 1 * D, [[6 * D, NB], [3 * D, 2], [1, D]])
    t_mod = I(nc.vector.tensor_scalar_add(csl, csl, 1.0), "sV", 5)
    W(nc.gpsimd, "sV", t_mod)
    t_md = I(nc.gpsimd.dma_start(out=K.moddr[:, :], in_=mod_sb[:, :]), "dG")
    W(nc.gpsimd, "dG", t_md)
    for b in range(NB):
        for ln in range(2):
            base = (b * 2 + ln) * D
            I(nc.gpsimd.dma_start(
                out=AP(K.blhsdr, base, [[1, 1], [1, D]]),
                in_=K.moddr[b:b + 1, (1 + 3 * ln) * D:(2 + 3 * ln) * D]), "dG")
            I(nc.gpsimd.dma_start(
                out=AP(K.blhsdr, NB * 2 * D + base, [[1, 1], [1, D]]),
                in_=K.moddr[b:b + 1, (0 + 3 * ln) * D:(1 + 3 * ln) * D]), "dG")
    t_vc = t_mod
    for kind in range(6):
        for dc in range(NDC):
            if t_vc:
                W(nc.tensor, "sV", t_vc)
            t_tp = I(nc.tensor.transpose(
                tpsf[:, 0:NB], mod_sb[:, kind * D + dc * 128: kind * D + (dc + 1) * 128],
                K.identf_sb[0:NB, 0:NB]), "sT")
            W(nc.vector, "sT", t_tp)
            col = (kind * NDC + dc) * NB
            t_vc = I(nc.vector.tensor_copy(K.vecs[:, col:col + NB],
                                           tpsf[:, 0:NB]), "sV")
    for b in range(NB):
        gav = AP(_h(K.vecs), (2 * NDC) * NB + b, [[NB * 6 * NDC, 128], [NB, NDC]])
        t_gb = I(nc.vector.tensor_mul(K.gabo[:, b * NDC:(b + 1) * NDC], gav,
                                      K.bias_sb[:, 24:32]), "sV")
    if K.dbg_taps:
        W(nc.gpsimd, "sV", t_gb)
        I(nc.gpsimd.dma_start(out=K.taps["tap_vecs"][:, :], in_=K.vecs[:, :]), "dG")
        I(nc.gpsimd.dma_start(out=K.taps["tap_gabo"][:, :], in_=K.gabo[:, :]), "dG")
    st.close()


def row_stats(K, m_sb, s2_sb, rowt, rowt2):
    nc, I, W = K.nc, K.I, K.W
    v = nc.vector
    W(v, "sT", K._blhs_cons)
    v.tensor_mul(rowt[:1, :], m_sb[:1, :], m_sb[:1, :])
    v.tensor_sub(rowt[:1, :], s2_sb[:1, :], rowt[:1, :])
    t = I(v.tensor_scalar_add(rowt[:1, :], rowt[:1, :], EPS), "sV", 3)
    W(nc.scalar, "sV", t)
    t = I(nc.scalar.activation(out=K.rstd_sb[:1, :], in_=rowt[:1, :],
                               func=AF.Identity), "sA")
    W(v, "sA", t)
    t = I(v.reciprocal(rowt2[:1, :], K.rstd_sb[:1, :]), "sV", 1)
    W(nc.scalar, "sV", t)
    t = I(nc.scalar.activation(out=K.rstd_sb[:1, :], in_=rowt2[:1, :],
                               func=AF.Sqrt), "sA")
    W(v, "sA", t)
    v.tensor_mul(rowt2[:1, :], K.rstd_sb[:1, :], K.rstd_sb[:1, :])
    v.tensor_mul(rowt2[:1, :], rowt2[:1, :], rowt[:1, :])
    v.tensor_scalar(out=rowt2[:1, :], in0=rowt2[:1, :], scalar1=-0.5, scalar2=1.5,
                    op0=ALU.mult, op1=ALU.add)
    v.tensor_mul(K.rstd_sb[:1, :], K.rstd_sb[:1, :], rowt2[:1, :])
    return I(v.scalar_tensor_tensor(K.brhs[0:1, :], m_sb[:1, :], -1.0,
                                    K.rstd_sb[:1, :], ALU.mult, ALU.mult), "sV", 5)


def ln_stats(K, xsrc, m_sb, s2_sb, stps, xsq):
    nc, I, W = K.nc, K.I, K.W
    W(nc.scalar, "sV")
    t_mm = 0
    for s in range(NS):
        for dc in range(NDC):
            xs = xsrc(dc, s)
            if t_mm:
                W(nc.scalar, "sT", t_mm)
            t_sq = I(nc.scalar.activation(out=xsq[:, :], in_=xs, func=AF.Square),
                     "sA")
            W(nc.tensor, "sA", t_sq)
            if dc == 0:
                W(nc.tensor, "sV", K._lnev)
            nc.tensor.matmul(stps[0:1, :], K.c1024_sb[:, :], xs,
                             start=(dc == 0), stop=(dc == NDC - 1))
            t_mm = I(nc.tensor.matmul(stps[32:33, :], K.c1024_sb[:, :], xsq[:, :],
                                      start=(dc == 0), stop=(dc == NDC - 1)), "sT")
        W(nc.vector, "sT", t_mm)
        nc.vector.tensor_copy(m_sb[:1, s * 512:(s + 1) * 512], stps[0:1, :])
        K._lnev = I(nc.vector.tensor_copy(s2_sb[:1, s * 512:(s + 1) * 512],
                                          stps[32:33, :]), "sV", 2)


def ab_maps(K, dc, s, abA, abB):
    nc, I = K.nc, K.I
    lA = AP(_h(K.blhs_bf), dc * 128, [[D, 1], [1, 128]])
    lB = AP(_h(K.blhs_bf), dc * 128, [[D, 2], [1, 128]])
    nc.tensor.matmul(abA[:, :], lA, K.rstd_bf[:1, s * 512:(s + 1) * 512],
                     start=True, stop=True)
    t = I(nc.tensor.matmul(abB[:, :], lB, K.brhs_bf[:, s * 512:(s + 1) * 512],
                           start=True, stop=True), "sT")
    K._blhs_cons = t
    return t


def rows_to_bf(K):
    nc, I, W = K.nc, K.I, K.W
    W(nc.vector, "sT", K._blhs_cons)
    nc.vector.tensor_copy(K.rstd_bf[:1, :], K.rstd_sb[:1, :])
    return I(nc.vector.tensor_copy(K.brhs_bf[:, :], K.brhs[:, :]), "sV")


def blhs_to_bf(K):
    nc, I, W = K.nc, K.I, K.W
    W(nc.vector, "sT", K._blhs_cons)
    return I(nc.vector.tensor_copy(K.blhs_bf[:, :], K.blhs[:, :]), "sV")


def phase_A1(K):
    nc, I, W = K.nc, K.I, K.W
    nc.sync.wait_ge(K.sem["dG"], K.cnt["dG"])
    st = ExitStack()
    x_bf = K.sb(st, "a1_xbf", [128, NDC * L], BF16)
    h_bf = K.sb(st, "a1_hbf", [128, NDC * L], BF16)
    xch = K.sb(st, "a1_xch", [128, 2 * L], F32)
    xsq = K.sb(st, "a1_xsq", [128, 512], BF16)
    pfm = K.sb(st, "a1_pfm", [128, 2 * L], BF16)
    stag = K.sb(st, "a1_stag", [128, D * NI], BF16)
    wsl = K.sb(st, "a1_wsl", [128, 2 * 1024], F32)
    wslb = K.sb(st, "a1_wslb", [128, 2 * 1024], BF16)
    m_sb = K.sb(st, "a1_m", [1, L], F32)
    s2_sb = K.sb(st, "a1_s2", [1, L], F32)
    rowt = K.sb(st, "a1_rt", [1, L], F32)
    rowt2 = K.sb(st, "a1_rt2", [1, L], F32)
    stps = K.ps(st, "a1_st", [128, 512])
    abps = [K.ps(st, f"a1_ab{i}", [128, 512]) for i in range(2)]
    pps = [K.ps(st, f"a1_pp{i}", [128, 512]) for i in range(2)]
    tpsl = [K.ps(st, f"a1_tp{i}", [128, 128], BF16) for i in range(2)]

    t_abcons = [0, 0]
    t_tpev = [0, 0]
    t_lastTp = [0, 0]
    t_wmm = [0, 0]
    stg_tok = [("dG", 0), ("dB", 0), ("dG", 0), ("dB", 0)]
    for b in range(NB):
        W(nc.sync, "sT", K._blhs_cons)
        t_bl = I(nc.sync.dma_start(
            out=K.blhs[:, :], in_=AP(K.blhsdr, (b * 2 + 0) * D,
                                     [[NB * 2 * D, 2], [1, D]])), "dS")
        t_xc = [0, 0]
        for dc in range(NDC):
            par = dc % 2
            xs_ = ("d2", "d7")[par]
            W(nc.sync, "sV", t_xc[par])
            t_ld = I(nc.sync.dma_start(out=xch[:, par * L:(par + 1) * L],
                                       in_=K.u_fm[b, dc * 128:(dc + 1) * 128, :]),
                     xs_)
            W(nc.vector, xs_, t_ld)
            if b > 0 and dc == 0:
                W(nc.vector, "sT")
            t_xc[par] = I(nc.vector.tensor_copy(
                x_bf[:, dc * L:(dc + 1) * L], xch[:, par * L:(par + 1) * L]), "sV")
        ln_stats(K, lambda dc, s: x_bf[:, dc * L + s * 512: dc * L + (s + 1) * 512],
                 m_sb, s2_sb, stps, xsq)
        row_stats(K, m_sb, s2_sb, rowt, rowt2)
        W(nc.vector, "dS", t_bl)
        rows_to_bf(K)
        blhs_to_bf(K)
        W(nc.tensor, "sV")
        t_h = 0
        for s in range(NS):
            for dc in range(NDC):
                W(nc.tensor, "sV", t_abcons[0])
                t_ab = ab_maps(K, dc, s, abps[0], abps[1])
                W(nc.vector, "sT", t_ab)
                sl = h_bf[:, dc * L + s * 512: dc * L + (s + 1) * 512]
                xsl = x_bf[:, dc * L + s * 512: dc * L + (s + 1) * 512]
                nc.vector.scalar_tensor_tensor(sl, xsl, 1.0, abps[0][:, :],
                                               ALU.mult, ALU.mult)
                t_h = I(nc.vector.tensor_add(sl, sl, abps[1][:, :]),
                        "sV", 2)
                t_abcons[0] = t_h
        if not hasattr(K, "_a1wc"):
            K._a1wc = [0, 0]
        for ip, (nm, wdr, dr) in enumerate(
                (("q", K.wqT, K.qtm), ("k", K.wkT, K.ktm), ("v", K.wvT, K.vtm))):
            t_evprev = 0
            for ec in range(NEC):
                par = ec % 2
                ws_ = ("d3", "d8")[par]
                W(nc.sync, "sV", K._a1wc[par])
                t_wl = I(nc.sync.dma_start(
                    out=wsl[:, par * 1024:(par + 1) * 1024],
                    in_=AP(wdr, ec * 128 * D, [[D, 128], [1, D]])),
                    ws_)
                W(nc.vector, ws_, t_wl)
                W(nc.vector, "sT", t_wmm[par])
                K._a1wc[par] = I(nc.vector.tensor_copy(
                    wslb[:, par * 1024:(par + 1) * 1024],
                    wsl[:, par * 1024:(par + 1) * 1024]), "sV")
                W(nc.tensor, "sV", max(K._a1wc[par], t_h))
                for hf in range(2):
                    W(nc.tensor, "sA", t_evprev)
                    for dc in range(NDC):
                        for s in range(2):
                            mm = nc.tensor.matmul(
                                pps[s][:, :],
                                wslb[:, par * 1024 + dc * 128: par * 1024 + (dc + 1) * 128],
                                h_bf[:, dc * L + hf * 1024 + s * 512: dc * L + hf * 1024 + (s + 1) * 512],
                                start=(dc == 0), stop=(dc == NDC - 1))
                    t_mm = I(mm, "sT")
                    W(nc.scalar, "sT", t_mm)
                    if hf == 1:
                        W(nc.scalar, "sT", t_lastTp[par])
                    for s in range(2):
                        rev = AP(pps[s], 127, [[512, 128], [128, 4], [-1, 128]])
                        ev = nc.scalar.activation(
                            out=pfm[:, par * L + hf * 1024 + s * 512: par * L + hf * 1024 + (s + 1) * 512],
                            in_=rev, func=AF.Identity,
                            bias=K.bias_sb[:, ip * 8 + ec: ip * 8 + ec + 1], scale=1.0)
                    t_evprev = I(ev, "sA", 2)
                t_wmm[par] = t_mm
                W(nc.tensor, "sA", t_evprev)
                if ec % 2 == 0:
                    W(nc.tensor, *stg_tok[ec // 2])
                for Ib in range(NI):
                    tpar = Ib % 2
                    W(nc.tensor, "sV", t_tpev[tpar])
                    tt = nc.tensor.transpose(
                        tpsl[tpar][:, :],
                        pfm[:, par * L + Ib * 128: par * L + (Ib + 1) * 128],
                        K.ident_sb[:, :])
                    t_tp = I(tt, "sT")
                    t_lastTp[par] = t_tp
                    W(nc.vector, "sT", t_tp)
                    dst = AP(_h(stag), ec * 128 * NI + Ib,
                             [[D * NI, 128], [NI, 128]])
                    t_tpev[tpar] = I(nc.vector.tensor_copy(dst, tpsl[tpar][:, :]),
                                     "sV")
                if ec % 2 == 1:
                    g = ec // 2
                    sts_ = ("dG", "dB")[g % 2]
                    W(nc.gpsimd, "sV", max(t_tpev))
                    cw = 2 * 128 * NI
                    stg_tok[g] = (sts_, I(nc.gpsimd.dma_start(
                        out=dr[:, b * D * NI + g * cw: b * D * NI + (g + 1) * cw],
                        in_=stag[:, g * cw:(g + 1) * cw]), sts_))
            if K.dbg_taps and nm == "k" and b == 0:
                I(nc.gpsimd.dma_start(out=K.taps["tap_h"][:, :], in_=h_bf[:, :]),
                  "dG")
    st.close()


def phase_A2(K):
    nc, I, W = K.nc, K.I, K.W
    K.joinD(nc.sync)
    st = ExitStack()
    ksb = K.sb(st, "a2_ksb", [128, BEI], BF16)
    skwr = [K.sb(st, f"a2_skwr{i}", [128, 2048], BF16) for i in range(4)]
    skw = [K.sb(st, f"a2_skw{i}", [128, 2048], BF16) for i in range(2)]
    kstage = K.sb(st, "a2_kstage", [128, 4096], BF16)
    kps = [K.ps(st, f"a2_kps{i}", [128, 512]) for i in range(2)]

    t_kld = I(nc.sync.dma_start(out=ksb[:, :], in_=K.ktm[:, :]), "dS")
    SKS = ("d2", "d7", "d9", "dA")
    t_raw = [0, 0, 0, 0]
    t_rvw = [0, 0, 0, 0]
    t_rev = [0, 0]
    t_mmd = [0, 0]
    t_bankev = [0, 0]
    t_flush = 0

    def pf_skw(e):
        par4 = e % 4
        W(nc.sync, "sV", t_rvw[par4])
        t_raw[par4] = I(nc.sync.dma_start(
            out=skwr[par4][:, :], in_=AP(K.kkbf, e * 2176, [[1, 128], [1, 2048]])),
            SKS[par4])

    for ee in range(4):
        pf_skw(ee)
    for e in range(D):
        par = e % 2
        par4 = e % 4
        W(nc.vector, SKS[par4], t_raw[par4])
        W(nc.vector, "sT", t_mmd[par])
        rev = AP(_h(skwr[par4]), 2047, [[2048, 128], [-1, 2048]])
        t_rev[par] = t_rvw[par4] = I(nc.vector.tensor_copy(skw[par][:, :], rev),
                                     "sV")
        if e + 4 < D:
            pf_skw(e + 4)
        bpar = (e // 8) % 2
        bank = kps[bpar]
        eg = e % 8
        W(nc.tensor, "sV", t_rev[par])
        if e == 0:
            W(nc.tensor, "dS", t_kld)
        if eg == 0:
            W(nc.tensor, "sA", t_bankev[bpar])
        for dd in range(NI):
            outw = AP(bank, eg * 64 + dd * 4, [[512, 128], [4, NI - dd], [1, 4]])
            rhs = AP(_h(ksb), e * NI, [[BEI, 128], [1, NI - dd], [D * NI, NB]])
            mm = nc.tensor.matmul(outw,
                                  skw[par][:, (NI - 1 - dd) * 128:(NI - dd) * 128],
                                  rhs, start=(dd == 0), stop=(dd == NI - 1))
        t_mmd[par] = I(mm, "sT")
        if eg == 7:
            grp = e // 8
            sgrp = grp % 8
            W(nc.scalar, "sT", t_mmd[par])
            if sgrp == 0:
                W(nc.scalar, "dG", t_flush)
            dst = AP(_h(kstage), sgrp * 8 * NI,
                     [[4096, 128], [NI, 8], [1, NI], [64 * NI, NB]])
            src = AP(bank, 0, [[512, 128], [64, 8], [4, NI], [1, 4]])
            t_bankev[bpar] = I(nc.scalar.copy(dst, src), "sA")
            if sgrp == 7:
                W(nc.gpsimd, "sA", t_bankev[bpar])
                e_start = (grp - 7) * 8
                t_flush = I(nc.gpsimd.dma_start(
                    out=AP(K.kptm, e_start * NI,
                           [[BEI, 128], [D * NI, NB], [1, 64 * NI]]),
                    in_=AP(_h(kstage), 0, [[4096, 128], [64 * NI, NB],
                                           [1, 64 * NI]])), "dG")
    if K.dbg_taps:
        W(nc.gpsimd, "dG", t_flush)
        I(nc.gpsimd.dma_start(out=K.taps["tap_kptm"][:, :], in_=K.kptm[:, :]), "dG")
        I(nc.gpsimd.dma_start(out=K.taps["tap_ktm"][:, :], in_=ksb[:, :]), "dG")
    st.close()


def phase_A3(K):
    nc, I, W = K.nc, K.I, K.W
    K.joinD(nc.sync)
    st = ExitStack()
    skwr = [K.sb(st, f"a3_skwr{i}", [128, 2048], BF16) for i in range(4)]
    skw = [K.sb(st, f"a3_skw{i}", [128, 2048], BF16) for i in range(2)]
    qc = [K.sb(st, f"a3_qc{i}", [128, 4096], BF16) for i in range(2)]
    vc = [K.sb(st, f"a3_vc{i}", [128, 4096], BF16) for i in range(2)]
    kc = [K.sb(st, f"a3_kc{i}", [128, 4096], BF16) for i in range(2)]
    kv = [K.sb(st, f"a3_kv{i}", [128, 1024], BF16) for i in range(2)]
    cbf = [K.sb(st, f"a3_cbf{i}", [128, 1024], BF16) for i in range(2)]
    qxp = [K.sb(st, f"a3_qxp{i}", [128, 1024], BF16) for i in range(2)]
    zb = [K.sb(st, f"a3_zb{i}", [128, 1024], BF16) for i in range(2)]
    z1 = K.sb(st, "a3_z1", [128, 512], BF16)
    z2 = K.sb(st, "a3_z2", [128, 256], BF16)
    yslab = [K.sb(st, f"a3_ys{bq}", [128, 2048], BF16) for bq in range(NB)]
    ytst = K.sb(st, "a3_ytst", [128, 2048], BF16)
    cps = [K.ps(st, f"a3_cp{i}", [128, 1024]) for i in range(2)]
    tpsl = [K.ps(st, f"a3_tp{i}", [128, 128], BF16) for i in range(2)]

    t_ld = [0, 0]
    t_rev = [0, 0]
    t_raw = [0, 0, 0, 0]
    t_rvw = [0, 0, 0, 0]
    t_qvk_cons = [0, 0]
    t_mm = [0, 0]
    t_qc_done = [0, 0]
    t_cbf_cons = [0, 0]
    t_slabtpT = 0
    t_ytstd = 0
    t_tpev = [0, 0]

    SKS = ("d2", "d7", "d9", "dA")

    def prefetch_skw(c):
        par4 = c % 4
        W(nc.sync, "sV", t_rvw[par4])
        t_raw[par4] = I(nc.sync.dma_start(
            out=skwr[par4][:, :], in_=AP(K.kcbf, c * 2176, [[1, 128], [1, 2048]])),
            SKS[par4])

    def prefetch_qvk(n):
        par8 = n % 2
        W(nc.sync, "sV", t_qvk_cons[par8])
        for buf, dr in ((qc[par8], K.qtm), (vc[par8], K.vtm), (kc[par8], K.kptm)):
            t = I(nc.sync.dma_start(
                out=AP(_h(buf), 0, [[4096, 128], [1024, NB], [1, 1024]]),
                in_=AP(dr, n * 1024,
                       [[BEI, 128], [D * NI, NB], [1, 1024]])), "d3")
        t_ld[par8] = t

    pend = [None]

    def post(p):
        c_, bq_, cbpar_, par8_, cb_, cl_ = p
        W(nc.scalar, "sT", t_mm[cbpar_])
        W(nc.scalar, "sV", t_cbf_cons[cbpar_])
        t_cbf = I(nc.scalar.copy(cbf[cbpar_][:, :], cps[cbpar_][:, :]), "sA")
        qsrc = AP(_h(qc[par8_]), bq_ * 1024 + cb_,
                  [[4096, 128], [1, NI], [NI, 8], [0, 8]])
        W(nc.vector, "sA", t_cbf)
        t_qx = I(nc.vector.tensor_copy(qxp[cbpar_][:, :], qsrc), "sV")
        t_qvk_cons[par8_] = t_qx
        nc.vector.tensor_mul(zb[cbpar_][:, :], cbf[cbpar_][:, :],
                             qxp[cbpar_][:, :])
        t_qc_done[cbpar_] = K.cnt["sV"] + 4
        i0a = AP(_h(zb[cbpar_]), 0, [[1024, 128], [64, NI], [8, 4], [1, 8]])
        i0b = AP(_h(zb[cbpar_]), 32, [[1024, 128], [64, NI], [8, 4], [1, 8]])
        o1 = AP(_h(z1), 0, [[512, 128], [32, NI], [8, 4], [1, 8]])
        nc.vector.tensor_tensor(o1, i0a, i0b, ALU.add)
        i1a = AP(_h(z1), 0, [[512, 128], [32, NI], [8, 2], [1, 8]])
        i1b = AP(_h(z1), 16, [[512, 128], [32, NI], [8, 2], [1, 8]])
        o2 = AP(_h(z2), 0, [[256, 128], [16, NI], [8, 2], [1, 8]])
        nc.vector.tensor_tensor(o2, i1a, i1b, ALU.add)
        i2a = AP(_h(z2), 0, [[256, 128], [16, NI], [1, 8]])
        i2b = AP(_h(z2), 8, [[256, 128], [16, NI], [1, 8]])
        if cl_ == 0 and bq_ == 0:
            W(nc.vector, "sT", t_slabtpT)
        oy = AP(_h(yslab[bq_]), cl_ * 8, [[2048, 128], [128, NI], [1, 8]])
        t4 = I(nc.vector.tensor_tensor(oy, i2a, i2b, ALU.add), "sV", 4)
        t_cbf_cons[cbpar_] = t4

    def flushp():
        if pend[0] is not None:
            post(pend[0])
            pend[0] = None

    for cc in range(4):
        prefetch_skw(cc)
    prefetch_qvk(0)
    for c in range(128):
        par = c % 2
        par4 = c % 4
        par8 = (c // 8) % 2
        cb = (c % 8) * 128
        cl = c % 16
        cg = c // 16
        W(nc.vector, SKS[par4], t_raw[par4])
        W(nc.vector, "sT", t_mm[par])
        revv = AP(_h(skwr[par4]), 2047, [[2048, 128], [-1, 2048]])
        t_rev[par] = t_rvw[par4] = I(nc.vector.tensor_copy(skw[par][:, :], revv),
                                     "sV")
        if c + 4 < 128:
            prefetch_skw(c + 4)
        if c % 8 == 0:
            flushp()
            if c // 8 + 1 < 16:
                prefetch_qvk(c // 8 + 1)
        for bq in range(NB):
            cbpar = (c * NB + bq) % 2
            W(nc.vector, "d3", t_ld[par8])
            W(nc.vector, "sT", t_mm[cbpar])
            in0 = AP(_h(kc[par8]), bq * 1024 + cb,
                     [[4096, 128], [1, NI], [NI, 8], [0, 8]])
            in1 = AP(_h(vc[par8]), bq * 1024 + cb,
                     [[4096, 128], [1, NI], [0, 8], [NI, 8]])
            t_kvb = I(nc.vector.tensor_tensor(kv[cbpar][:, :], in0, in1, ALU.mult),
                      "sV")
            W(nc.tensor, "sV", max(t_kvb, t_rev[par], t_qc_done[cbpar]))
            for dd in range(NI):
                s0 = dd * 64
                lt = skw[par][:, (NI - 1 - dd) * 128:(NI - dd) * 128]
                if s0 < 512:
                    nc.tensor.matmul(cps[cbpar][:, s0:512], lt,
                                     kv[cbpar][:, 0:512 - s0],
                                     start=(dd == 0), stop=(dd == 7))
                    mm = nc.tensor.matmul(cps[cbpar][:, 512:1024], lt,
                                          kv[cbpar][:, 512 - s0:1024 - s0],
                                          start=(dd == 0), stop=(dd == NI - 1))
                else:
                    mm = nc.tensor.matmul(cps[cbpar][:, s0:1024], lt,
                                          kv[cbpar][:, 0:1024 - s0],
                                          start=False, stop=(dd == NI - 1))
            t_mm[cbpar] = I(mm, "sT")
            flushp()
            pend[0] = (c, bq, cbpar, par8, cb, cl)
        if cl == 15:
            flushp()
            W(nc.tensor, "sV")
            for bq in range(NB):
                for Ib in range(NI):
                    tpar = Ib % 2
                    W(nc.tensor, "sV", t_tpev[tpar])
                    tt = nc.tensor.transpose(
                        tpsl[tpar][:, :], yslab[bq][:, Ib * 128:(Ib + 1) * 128],
                        K.ident_sb[:, :])
                    t_tp = I(tt, "sT")
                    t_slabtpT = t_tp
                    W(nc.vector, "sT", t_tp)
                    if Ib == 0:
                        W(nc.vector, "dG", t_ytstd)
                    t_tpev[tpar] = I(nc.vector.tensor_copy(
                        ytst[:, Ib * 128:(Ib + 1) * 128], tpsl[tpar][:, :]), "sV")
                W(nc.gpsimd, "sV", max(t_tpev))
                t_ytstd = I(nc.gpsimd.dma_start(
                    out=K.ytr[bq * 8 + cg, :, :], in_=ytst[:, :]), "dG")
    if K.dbg_taps:
        W(nc.gpsimd, "dG", t_ytstd)
        I(nc.gpsimd.dma_start(out=K.taps["tap_ytr"][:, :, :], in_=K.ytr[:, :, :]),
          "dG")
    st.close()


def phase_A4(K):
    nc, I, W = K.nc, K.I, K.W
    K.joinD(nc.sync)
    st = ExitStack()
    wop = K.sb(st, "a4_wop", [128, 1024], F32)
    wopb = K.sb(st, "a4_wopb", [128, NDC * 1024], BF16)
    ysb = K.sb(st, "a4_ysb", [128, 8 * L], BF16)
    uch = K.sb(st, "a4_uch", [128, L], F32)
    x2sb = K.sb(st, "a4_x2", [128, NDC * L], F32)
    xbfs = K.sb(st, "a4_xbfs", [128, 512], BF16)
    xsq = K.sb(st, "a4_xsq", [128, 512], BF16)
    t1b = K.sb(st, "a4_t1b", [128, 2 * 512], F32)
    h2s = K.sb(st, "a4_h2s", [128, 512], F32)
    m_sb = K.sb(st, "a4_m", [1, L], F32)
    s2_sb = K.sb(st, "a4_s2", [1, L], F32)
    rowt = K.sb(st, "a4_rt", [1, L], F32)
    rowt2 = K.sb(st, "a4_rt2", [1, L], F32)
    stps = K.ps(st, "a4_st", [128, 512])
    abps = [K.ps(st, f"a4_ab{i}", [128, 512]) for i in range(2)]
    wps = [K.ps(st, f"a4_wp{i}", [128, 512]) for i in range(2)]

    t_wc = 0
    for ch in range(NDC):
        W(nc.sync, "sV", t_wc)
        t_l = I(nc.sync.dma_start(
            out=wop[:, :], in_=K.woTp[ch * 128:(ch + 1) * 128, :]), "d3")
        W(nc.vector, "d3", t_l)
        t_wc = I(nc.vector.tensor_copy(
            wopb[:, ch * 1024:(ch + 1) * 1024], wop[:, :]), "sV")

    nmax = 0
    t_ev = [0, 0]
    t_abcons = [0, 0]
    t_woMM = 0
    t_spill = 0
    for b in range(NB):
        W(nc.sync, "sT", max(t_woMM, K._blhs_cons))
        t_y = 0
        for cg in range(8):
            t_y = I(nc.sync.dma_start(out=ysb[:, cg * L:(cg + 1) * L],
                                      in_=K.ytr[b * 8 + cg, :, :]), "dS")
        t_bl = I(nc.sync.dma_start(
            out=K.blhs[:, :], in_=AP(K.blhsdr, (b * 2 + 1) * D,
                                     [[NB * 2 * D, 2], [1, D]])), "dS")
        t_uc = [0, 0]
        t_x2 = 0
        W(nc.vector, "d4", t_spill)
        for fc in range(NDC):
            par = 0
            W(nc.sync, "sV", t_uc[par])
            t_u = I(nc.sync.dma_start(out=uch[:, :],
                                      in_=K.u_fm[b, fc * 128:(fc + 1) * 128, :]),
                    "d2")
            for s in range(NS):
                spar = (fc * NS + s) % 2
                W(nc.tensor, "dS", t_y)
                W(nc.tensor, "sV", max(t_wc, t_ev[spar]))
                for cg in range(8):
                    mm = nc.tensor.matmul(
                        wps[spar][:, :],
                        wopb[:, cg * 1024 + fc * 128: cg * 1024 + (fc + 1) * 128],
                        ysb[:, cg * L + s * 512: cg * L + (s + 1) * 512],
                        start=(cg == 0), stop=(cg == 7))
                t_woMM = t_mm = I(mm, "sT")
                W(nc.scalar, "sT", t_mm)
                W(nc.scalar, "sV", t_ev[spar])
                rev = AP(wps[spar], 127, [[512, 128], [128, 4], [-1, 128]])
                t_t1 = I(nc.scalar.activation(
                    out=t1b[:, spar * 512:(spar + 1) * 512], in_=rev,
                    func=AF.Identity, scale=vcol(K, b, 2, fc),
                    bias=K.gabo[:, b * NDC + fc: b * NDC + fc + 1]), "sA")
                W(nc.vector, "sA", t_t1)
                W(nc.vector, "d2", t_u)
                t_x2 = I(nc.vector.tensor_add(
                    x2sb[:, fc * L + s * 512: fc * L + (s + 1) * 512],
                    t1b[:, spar * 512:(spar + 1) * 512],
                    uch[:, s * 512:(s + 1) * 512]), "sV")
                t_ev[spar] = t_x2
                t_uc[par] = t_x2

        def xsrc(dc, s):
            W(nc.vector, "sA")
            t = I(nc.vector.tensor_copy(
                xbfs[:, :], x2sb[:, dc * L + s * 512: dc * L + (s + 1) * 512]),
                "sV")
            W(nc.scalar, "sV", t)
            return xbfs[:, :]
        ln_stats(K, xsrc, m_sb, s2_sb, stps, xsq)
        row_stats(K, m_sb, s2_sb, rowt, rowt2)
        W(nc.vector, "dS", t_bl)
        rows_to_bf(K)
        blhs_to_bf(K)
        W(nc.gpsimd, "sV")
        I(nc.gpsimd.dma_start(out=K.st2dr[b, 0:1, :], in_=K.rstd_sb[:1, :]), "dG")
        t_st2d = I(nc.gpsimd.dma_start(out=K.st2dr[b, 1:2, :], in_=K.brhs[0:1, :]),
                   "dG")
        K._blhs_cons = max(K._blhs_cons, 0)
        W(nc.vector, "dG", t_st2d)
        for s in range(NS):
            for dc in range(NDC):
                W(nc.tensor, "sV", t_abcons[0])
                t_ab = ab_maps(K, dc, s, abps[0], abps[1])
                W(nc.vector, "sT", t_ab)
                xsl = x2sb[:, dc * L + s * 512: dc * L + (s + 1) * 512]
                nc.vector.scalar_tensor_tensor(h2s[:, :], xsl, 1.0,
                                               abps[0][:, :], ALU.mult,
                                               ALU.mult)
                nc.vector.tensor_add(h2s[:, :], h2s[:, :], abps[1][:, :])
                t_r = I(nc.vector.tensor_reduce(
                    K.redcg[:, nmax:nmax + 1], h2s[:, :], AX.X, ALU.max,
                    apply_absolute_value=True), "sV", 3)
                nmax += 1
                t_abcons[0] = t_r
        W(nc.gpsimd, "sV")
        for dc in range(NDC):
            t_spill = I(nc.gpsimd.dma_start(
                out=K.x2dr[b, dc * 128:(dc + 1) * 128, :],
                in_=x2sb[:, dc * L:(dc + 1) * L]), "d4")
        if K.dbg_taps:
            W(nc.gpsimd, "d4", t_spill)
            I(nc.gpsimd.dma_start(out=K.taps["tap_x2"][b, :, :],
                                  in_=K.x2dr[b, :, :]), "dG")
    finalize_scale(K, K.redcg, nmax, K.ar0i, K.ar0o, [(0, 1)], 1)
    st.close()


def finalize_scale(K, redc, nidx, ari, aro, colpairs, n_sc):
    nc, I, W = K.nc, K.I, K.W
    v = nc.vector
    stf = ExitStack()
    K._fsn = getattr(K, "_fsn", 0) + 1
    rps = K.ps(stf, f"fs_rps{K._fsn}", [1, 128])
    bcps = K.ps(stf, f"fs_bcps{K._fsn}", [128, 1])
    reds = redc if isinstance(redc, list) else [redc]
    t = 0
    for i, rc in enumerate(reds):
        W(v, "sT", K._fin_t)
        t = I(v.tensor_reduce(K.red1[:, :], rc[:, :nidx], AX.X, ALU.max,
                              apply_absolute_value=True), "sV")
        W(nc.tensor, "sV", t)
        K._fin_t = tt = I(nc.tensor.transpose(rps[:, :], K.red1[:, :],
                                              K.identf_sb[:, :]), "sT")
        W(v, "sT", tt)
        t = I(v.tensor_reduce(K.scrow[:1, i:i + 1], rps[:1, :], AX.X, ALU.max,
                              apply_absolute_value=True), "sV")
    W(nc.gpsimd, "sV", t)
    td = I(nc.gpsimd.dma_start(out=ari[:, :], in_=K.scrow[:1, 0:8]), "dG")
    W(nc.gpsimd, "dG", td)
    tcc = I(nc.gpsimd.collective_compute(
        "AllReduce", ALU.max, replica_groups=K.rg,
        ins=[ari.ap().opt()], outs=[aro.ap().opt()]), "sCC")
    W(nc.sync, "sCC", tcc)
    tl = I(nc.sync.dma_start(out=K.scrow[:1, 4:4 + n_sc], in_=aro[:, 0:n_sc]), "dS")
    W(v, "dS", tl)
    t = I(v.tensor_scalar(out=K.scrow[:1, 4:4 + n_sc], in0=K.scrow[:1, 4:4 + n_sc],
                          scalar1=float(np.float32(1 / 32767.0)), scalar2=1e-12,
                          op0=ALU.mult, op1=ALU.max), "sV")
    W(nc.scalar, "sV", t)
    t = I(nc.scalar.activation(out=K.scrow[:1, 8:8 + n_sc],
                               in_=K.scrow[:1, 4:4 + n_sc],
                               func=AF.Identity), "sA")
    W(v, "sA", t)
    t = I(v.reciprocal(K.scrow[:1, 6:6 + n_sc], K.scrow[:1, 8:8 + n_sc]), "sV")
    for i, (icol, scol) in enumerate(colpairs):
        for src_off, dcol in ((6 + i, icol), (4 + i, scol)):
            W(nc.tensor, "sV", t)
            tm = I(nc.tensor.matmul(bcps[:, :], K.onescol_sb[:, :],
                                    K.scrow[:1, src_off:src_off + 1],
                                    start=True, stop=True), "sT")
            W(v, "sT", tm)
            t = I(v.tensor_copy(K.scales[:, dcol:dcol + 1], bcps[:, :]), "sV")
    if K.dbg_taps:
        W(nc.gpsimd, "sV", t)
        I(nc.gpsimd.dma_start(out=K.taps["tap_scales"][:, :], in_=K.scales[:, :]),
          "dG")
    W(nc.tensor, "sV", t)
    stf.close()


def phase_B(K):
    nc, I, W = K.nc, K.I, K.W
    BD = K.upto if K.upto.startswith("BD") else ""
    skip_prep = BD in ("BDmm",)
    stGU = ExitStack()
    gstage_lo = K.sb(stGU, "b_gstlo", [128, 2 * L], BF16)
    wgT = K.sb(stGU, "b_wgT", [128, NDC * FF], BF16)
    wuT = K.sb(stGU, "b_wuT", [128, NDC * FF], BF16)

    st = ExitStack()
    tpsl = [K.ps(st, f"bp_tp{i}", [128, 128], BF16) for i in range(2)]
    t_tpev = [0, 0]
    for (srcdr, dstsb, nch, R, tag) in ([] if skip_prep else
                                        [(K.wg, wgT, 24, 1024, "g"),
                                         (K.wu, wuT, 24, 1024, "u")][
                                            0:(1 if BD == "BDmm2g" else 2)]):
        nblk = R // 32
        stj = ExitStack()
        wraw = K.sb(stj, f"bp_wraw{tag}", [128, 2 * R], F32)
        blkm = K.sb(stj, f"bp_blkm{tag}", [128, nblk], F32)
        sblk = K.sb(stj, f"bp_sblk{tag}", [128, nblk], F32)
        sblk2 = K.sb(stj, f"bp_sblk2{tag}", [128, nblk], F32)
        iblk = K.sb(stj, f"bp_iblk{tag}", [128, nblk], F32)
        tf = K.sb(stj, f"bp_tf{tag}", [128, R], F32)
        wqb = K.sb(stj, f"bp_wqb{tag}", [128, R], BF16)
        t_wl = [0, 0]
        t_cons = [0, 0]
        for ch in range(nch):
            par = ch % 2
            W(nc.sync, "sV", t_cons[par])
            if tag == "d":
                srcsl = srcdr[(ch // 2) * 128:(ch // 2 + 1) * 128,
                              (ch % 2) * 1536:(ch % 2 + 1) * 1536]
            else:
                srcsl = srcdr[ch * 128:(ch + 1) * 128, :]
            t_wl[par] = I(nc.sync.dma_start(
                out=wraw[:, par * R: par * R + R], in_=srcsl), "dS")
            W(nc.vector, "dS", t_wl[par])
            W(nc.vector, "sT")
            wr = wraw[:, par * R: par * R + R]
            wv = wr.rearrange("p (nb t) -> p nb t", t=32)
            nc.vector.tensor_reduce(blkm[:, :nblk], wv, AX.X, ALU.max,
                                    apply_absolute_value=True)
            t_ts = I(nc.vector.tensor_scalar(out=sblk[:, :nblk], in0=blkm[:, :nblk],
                                             scalar1=float(np.float32(1 / 7.0)),
                                             scalar2=1e-12, op0=ALU.mult,
                                             op1=ALU.max), "sV", 2)
            W(nc.scalar, "sV", t_ts)
            t_cp = I(nc.scalar.activation(out=sblk2[:, :nblk], in_=sblk[:, :nblk],
                                          func=AF.Identity), "sA")
            W(nc.vector, "sA", t_cp)
            nc.vector.reciprocal(iblk[:, :nblk], sblk2[:, :nblk])
            ib = AP(_h(iblk), 0, [[nblk, 128], [1, nblk], [0, 32]])
            sb_ = AP(_h(sblk), 0, [[nblk, 128], [1, nblk], [0, 32]])
            nc.vector.tensor_tensor(tf[:, :R], wr, ib, ALU.mult)
            nc.vector.tensor_scalar(out=tf[:, :R], in0=tf[:, :R], scalar1=CMAG,
                                    scalar2=CMAG, op0=ALU.add, op1=ALU.subtract)
            t_cons[par] = I(nc.vector.tensor_tensor(wqb[:, :R], tf[:, :R], sb_,
                                                    ALU.mult), "sV", 4)
            W(nc.tensor, "sV", t_cons[par])
            for ti in range(R // 128):
                tpar = ti % 2
                W(nc.tensor, "sV", t_tpev[tpar])
                tt = nc.tensor.transpose(tpsl[tpar][:, :],
                                         wqb[:, ti * 128:(ti + 1) * 128],
                                         K.ident_sb[:, :])
                t_tp = I(tt, "sT")
                W(nc.vector, "sT", t_tp)
                if tag in ("g", "u"):
                    dst = dstsb[:, ti * FF + ch * 128: ti * FF + (ch + 1) * 128]
                else:
                    tig = (ch % 2) * 12 + ti
                    chg = ch // 2
                    dst = dstsb[:, tig * 1024 + chg * 128: tig * 1024 + (chg + 1) * 128]
                t_tpev[tpar] = I(nc.vector.tensor_copy(dst, tpsl[tpar][:, :]), "sV")
        W(nc.sync, "sV")
        stj.close()
    st.close()
    if K.upto == "Bp" or BD == "BDp":
        stGU.close()
        return

    K.joinD(nc.sync)
    nc.sync.wait_ge(K.sem["d4"], K.cnt["d4"])
    st = ExitStack()
    x2ch = K.sb(st, "b1_x2ch", [128, 2 * L], F32)
    h2q = K.sb(st, "b1_h2q", [128, NDC * L], BF16)
    h2s = K.sb(st, "b1_h2s", [128, 512], F32)
    gstage = gstage_lo
    abps = [K.ps(st, f"b1_ab{i}", [128, 512]) for i in range(2)]
    gups = [K.ps(st, f"b1_gu{i}", [128, 512]) for i in range(2)]
    ng, nu = 0, 0
    t_gup_red = [0, 0]
    t_gst_d = [0, 0]
    t_act = [0, 0]
    t_abc = [0, 0]
    t_q = 0
    t_gupMM = 0
    nb_lim = 1 if BD else NB
    nic_lim = 2 if BD else NIC
    for b in range(nb_lim):
        if BD.startswith("BDmm"):
            break
        W(nc.sync, "sT", max(K._blhs_cons, t_gupMM))
        t_bl = I(nc.sync.dma_start(
            out=K.blhs[:, :], in_=AP(K.blhsdr, (b * 2 + 1) * D,
                                     [[NB * 2 * D, 2], [1, D]])), "dS")
        I(nc.sync.dma_start(out=K.rstd_sb[:1, :], in_=K.st2dr[b, 0:1, :]), "dS")
        t_st2 = I(nc.sync.dma_start(out=K.brhs[0:1, :], in_=K.st2dr[b, 1:2, :]),
                  "dS")
        W(nc.vector, "sT")
        W(nc.vector, "dS", t_st2)
        rows_to_bf(K)
        blhs_to_bf(K)
        t_xc = [0, 0]
        for dc in range(0 if BD in ("BDonly", "BDmm") else NDC):
            par = dc % 2
            W(nc.sync, "sV", t_xc[par])
            t_x = I(nc.sync.dma_start(out=x2ch[:, par * L:(par + 1) * L],
                                      in_=K.x2dr[b, dc * 128:(dc + 1) * 128, :]),
                    "d3")
            for s in range(NS):
                W(nc.tensor, "dS", max(t_bl, t_st2))
                W(nc.tensor, "sV", t_abc[0])
                t_ab = ab_maps(K, dc, s, abps[0], abps[1])
                W(nc.vector, "sT", t_ab)
                W(nc.vector, "d3", t_x)
                xsl = x2ch[:, par * L + s * 512: par * L + (s + 1) * 512]
                nc.vector.scalar_tensor_tensor(h2s[:, :], xsl, 1.0,
                                               abps[0][:, :], ALU.mult,
                                               ALU.mult)
                nc.vector.tensor_add(h2s[:, :], h2s[:, :], abps[1][:, :])
                nc.vector.tensor_scalar(out=h2s[:, :], in0=h2s[:, :],
                                        scalar1=K.scales[:, 0:1], scalar2=CMAG,
                                        op0=ALU.mult, op1=ALU.add)
                t_q = I(nc.vector.tensor_scalar(
                    out=h2q[:, dc * L + s * 512: dc * L + (s + 1) * 512],
                    in0=h2s[:, :], scalar1=CMAG, scalar2=K.scales[:, 1:2],
                    op0=ALU.subtract, op1=ALU.mult), "sV", 4)
                t_abc[0] = t_q
                t_xc[par] = t_q
        if K.upto == "B1h" or BD == "BDh":
            continue
        if BD in ("BDonly", "BDmm"):
            t_q = K.cnt["sV"]
        for Ic in range(nic_lim):
            if K.upto == "B1bar":
                K.bar()
            for it, (wsb_, drt, redc) in enumerate(((wgT, K.gatedr, K.redcg),
                                                    (wuT, K.updr, K.redcu))):
                for s in range(NS):
                    W(nc.tensor, "sV", t_q)
                    W(nc.tensor, "sA", t_act[it])
                    for dc in range(NDC):
                        lhsw = (K.ident_sb[:, :] if BD in ("BDident", "BDmm") else
                                wsb_[:, dc * FF + Ic * 128: dc * FF + (Ic + 1) * 128])
                        mm = nc.tensor.matmul(
                            gups[it][:, :], lhsw,
                            h2q[:, dc * L + s * 512: dc * L + (s + 1) * 512],
                            start=(dc == 0), stop=(dc == NDC - 1))
                    t_gupMM = t_mm = I(mm, "sT")
                    W(nc.vector, "sT", t_mm)
                    if s == 0:
                        W(nc.vector, "d4" if it == 0 else "d5", t_gst_d[it])
                    t_act[it] = I(nc.vector.tensor_copy(
                        gstage[:, it * L + s * 512: it * L + (s + 1) * 512],
                        gups[it][:, :]), "sA")
                idx = ng if it == 0 else nu
                t_gup_red[it] = I(nc.vector.tensor_reduce(
                    redc[:, idx:idx + 1], gstage[:, it * L:(it + 1) * L],
                    AX.X, ALU.max, apply_absolute_value=True), "sV")
                if it == 0:
                    ng += 1
                else:
                    nu += 1
                W(nc.gpsimd, "sA", t_act[it])
                W(nc.gpsimd, "sV", t_gup_red[it])
                t_gst_d[it] = I(nc.gpsimd.dma_start(
                    out=drt[b, Ic * 128:(Ic + 1) * 128, :],
                    in_=gstage[:, it * L:(it + 1) * L]), "d4" if it == 0 else "d5")
    if BD in ("BDmm", "BDmm2", "BDmm2g", "BDmm3", "BDmm4", "BDmm5", "BDmm6", "BDmm7", "BDmm8"):
        if BD in ("BDmm3", "BDmm4", "BDmm5", "BDmm6", "BDmm7", "BDmm8"):
            t_bl = I(nc.sync.dma_start(
                out=K.blhs[:, :], in_=AP(K.blhsdr, D, [[NB * 2 * D, 2], [1, D]])),
                "dS")
            I(nc.sync.dma_start(out=K.rstd_sb[:1, :], in_=K.st2dr[0, 0:1, :]), "dS")
            t_st2 = I(nc.sync.dma_start(out=K.brhs[0:1, :], in_=K.st2dr[0, 1:2, :]),
                      "dS")
        if BD in ("BDmm4", "BDmm5", "BDmm6", "BDmm7", "BDmm8"):
            W(nc.vector, "dS", t_st2)
            rows_to_bf(K)
            blhs_to_bf(K)
        t_act2 = [0, 0]
        for Ic in range(2):
            for it in range(2):
                for sx in range(NS):
                    W(nc.tensor, "sA", t_act2[it])
                    for dc in range(NDC):
                        lw = ((wgT if it == 0 else wuT)[:, dc * FF + Ic * 128:
                                                        dc * FF + (Ic + 1) * 128]
                              if BD == "BDmm8" else K.ident_sb[:, :])
                        mm = nc.tensor.matmul(
                            gups[it][:, :], lw,
                            h2q[:, dc * L + sx * 512: dc * L + (sx + 1) * 512],
                            start=(dc == 0), stop=(dc == NDC - 1))
                    t_mm = I(mm, "sT")
                    W(nc.vector, "sT", t_mm)
                    if BD in ("BDmm6", "BDmm7", "BDmm8"):
                        t_act2[it] = I(nc.vector.tensor_copy(
                            gstage[:, it * L + sx * 512: it * L + (sx + 1) * 512],
                            gups[it][:, :]), "sA")
                    else:
                        t_act2[it] = I(nc.vector.tensor_copy(h2s[:, :],
                                                             gups[it][:, :]), "sA")
                if BD in ("BDmm5", "BDmm7", "BDmm8"):
                    I(nc.vector.tensor_reduce(
                        K.redcg[:, Ic * 2 + it: Ic * 2 + it + 1],
                        gstage[:, it * L:(it + 1) * L], AX.X, ALU.max,
                        apply_absolute_value=True), "sV")
    if not BD and K.upto not in ("B1h", "B1m", "B1m0", "B1m1", "B1m2", "B1mA",
                                 "B1bar"):
        finalize_scale(K, [K.redcg, K.redcu], 96, K.ar1i, K.ar1o,
                       [(2, 3), (4, 5)], 2)
    st.close()
    if BD or K.upto in ("B1h", "B1m", "B1m0", "B1m1", "B1m2", "B1mA", "B1bar"):
        stGU.close()
        return
    if K.upto == "B1":
        stGU.close()
        return

    nc.sync.wait_ge(K.sem["d4"], K.cnt["d4"])
    nc.sync.wait_ge(K.sem["d5"], K.cnt["d5"])
    st = ExitStack()
    gch = K.sb(st, "b2_gch", [128, 2 * L], BF16)
    gq = K.sb(st, "b2_gq", [128, L], BF16)
    qsc = K.sb(st, "b2_qsc", [128, L], F32)
    sig = K.sb(st, "b2_sig", [128, L], F32)
    g2b = K.sb(st, "b2_g2", [128, 2 * L], BF16)
    n2 = 0
    t_g2d = [0, 0]
    t_cons = [0, 0]
    for b in range(NB):
        for Ic in range(NIC):
            par = (b * NIC + Ic) % 2
            W(nc.sync, "sV", t_cons[par])
            t_in = I(nc.sync.dma_start(
                out=gch[:, par * L:(par + 1) * L],
                in_=K.gatedr[b, Ic * 128:(Ic + 1) * 128, :]), "d2")
            W(nc.vector, "d2", t_in)
            gs = gch[:, par * L:(par + 1) * L]
            nc.vector.tensor_scalar(out=qsc[:, :], in0=gs, scalar1=K.scales[:, 2:3],
                                    scalar2=CMAG, op0=ALU.mult, op1=ALU.add)
            t_gq = I(nc.vector.tensor_scalar(
                out=gq[:, :], in0=qsc[:, :], scalar1=CMAG, scalar2=K.scales[:, 3:4],
                op0=ALU.subtract, op1=ALU.mult), "sV", 2)
            W(nc.scalar, "sV", t_gq)
            t_sg = I(nc.scalar.activation(out=sig[:, :], in_=gq[:, :],
                                          func=AF.Sigmoid), "sA")
            W(nc.vector, "sA", t_sg)
            nc.vector.tensor_scalar(out=sig[:, :], in0=sig[:, :], scalar1=65536.0,
                                    scalar2=CMAG, op0=ALU.mult, op1=ALU.add)
            nc.vector.tensor_scalar(out=sig[:, :], in0=sig[:, :], scalar1=CMAG,
                                    scalar2=65535.0, op0=ALU.subtract, op1=ALU.min)
            W(nc.vector, "d4", t_g2d[par])
            nc.vector.scalar_tensor_tensor(g2b[:, par * L:(par + 1) * L], gq[:, :],
                                           float(np.float32(1 / 65536.0)),
                                           sig[:, :], ALU.mult, ALU.mult)
            t_cons[par] = I(nc.vector.tensor_reduce(
                K.redcg[:, n2:n2 + 1], g2b[:, par * L:(par + 1) * L], AX.X,
                ALU.max, apply_absolute_value=True), "sV", 4)
            n2 += 1
            W(nc.gpsimd, "sV", t_cons[par])
            t_g2d[par] = I(nc.gpsimd.dma_start(
                out=K.g2dr[b, Ic * 128:(Ic + 1) * 128, :],
                in_=g2b[:, par * L:(par + 1) * L]), "d4")
    finalize_scale(K, K.redcg, n2, K.ar2i, K.ar2o, [(6, 7)], 1)
    st.close()
    if K.upto == "B2":
        stGU.close()
        return

    nc.sync.wait_ge(K.sem["d4"], K.cnt["d4"])
    nc.sync.wait_ge(K.sem["d5"], K.cnt["d5"])
    st = ExitStack()
    g2ch = K.sb(st, "b3_g2ch", [128, 2 * L], BF16)
    upch = K.sb(st, "b3_upch", [128, 2 * L], BF16)
    oq = K.sb(st, "b3_oq", [128, L], BF16)
    qsc3 = K.sb(st, "b3_qsc", [128, L], F32)
    upq = K.sb(st, "b3_upq", [128, L], BF16)
    ob = K.sb(st, "b3_ob", [128, 2 * L], BF16)
    n3 = 0
    t_od = [0, 0]
    t_cons = [0, 0]
    for b in range(NB):
        for Ic in range(NIC):
            par = (b * NIC + Ic) % 2
            W(nc.sync, "sV", t_cons[par])
            I(nc.sync.dma_start(out=g2ch[:, par * L:(par + 1) * L],
                                in_=K.g2dr[b, Ic * 128:(Ic + 1) * 128, :]), "d2")
            t_in = I(nc.sync.dma_start(
                out=upch[:, par * L:(par + 1) * L],
                in_=K.updr[b, Ic * 128:(Ic + 1) * 128, :]), "d2")
            W(nc.vector, "d2", t_in)
            g2s = g2ch[:, par * L:(par + 1) * L]
            ups = upch[:, par * L:(par + 1) * L]
            nc.vector.tensor_scalar(out=qsc3[:, :], in0=g2s, scalar1=K.scales[:, 6:7],
                                    scalar2=CMAG, op0=ALU.mult, op1=ALU.add)
            nc.vector.tensor_scalar(out=oq[:, :], in0=qsc3[:, :], scalar1=CMAG,
                                    scalar2=K.scales[:, 7:8], op0=ALU.subtract,
                                    op1=ALU.mult)
            nc.vector.tensor_scalar(out=qsc3[:, :], in0=ups, scalar1=K.scales[:, 4:5],
                                    scalar2=CMAG, op0=ALU.mult, op1=ALU.add)
            nc.vector.tensor_scalar(out=upq[:, :], in0=qsc3[:, :], scalar1=CMAG,
                                    scalar2=K.scales[:, 5:6], op0=ALU.subtract,
                                    op1=ALU.mult)
            W(nc.vector, "d4", t_od[par])
            nc.vector.tensor_mul(ob[:, par * L:(par + 1) * L], oq[:, :], upq[:, :])
            t_cons[par] = I(nc.vector.tensor_reduce(
                K.redcg[:, n3:n3 + 1], ob[:, par * L:(par + 1) * L], AX.X,
                ALU.max, apply_absolute_value=True), "sV", 6)
            n3 += 1
            W(nc.gpsimd, "sV", t_cons[par])
            t_od[par] = I(nc.gpsimd.dma_start(
                out=K.odr[b, Ic * 128:(Ic + 1) * 128, :],
                in_=ob[:, par * L:(par + 1) * L]), "d4")
    finalize_scale(K, K.redcg, n3, K.ar3i, K.ar3o, [(8, 9)], 1)
    st.close()
    stGU.close()
    if K.upto == "B3":
        return

    nc.sync.wait_ge(K.sem["d4"], K.cnt["d4"])
    stW = ExitStack()
    wdT = K.sb(stW, "b_wdT", [128, NIC * 1024], BF16)
    stp = ExitStack()
    tpsl = [K.ps(stp, f"b4p_tp{i}", [128, 128], BF16) for i in range(2)]
    t_tpev = [0, 0]
    for (srcdr, dstsb, nch, R, tag) in [(K.wdn, wdT, 16, 1536, "d")]:
        nblk = R // 32
        stj = ExitStack()
        wraw = K.sb(stj, f"bp_wraw{tag}", [128, 2 * R], F32)
        blkm = K.sb(stj, f"bp_blkm{tag}", [128, nblk], F32)
        sblk = K.sb(stj, f"bp_sblk{tag}", [128, nblk], F32)
        sblk2 = K.sb(stj, f"bp_sblk2{tag}", [128, nblk], F32)
        iblk = K.sb(stj, f"bp_iblk{tag}", [128, nblk], F32)
        tf = K.sb(stj, f"bp_tf{tag}", [128, R], F32)
        wqb = K.sb(stj, f"bp_wqb{tag}", [128, R], BF16)
        t_wl = [0, 0]
        t_cons = [0, 0]
        for ch in range(nch):
            par = ch % 2
            W(nc.sync, "sV", t_cons[par])
            srcsl = srcdr[(ch // 2) * 128:(ch // 2 + 1) * 128,
                          (ch % 2) * 1536:(ch % 2 + 1) * 1536]
            t_wl[par] = I(nc.sync.dma_start(
                out=wraw[:, par * R: par * R + R], in_=srcsl), "d6")
            W(nc.vector, "d6", t_wl[par])
            W(nc.vector, "sT")
            wr = wraw[:, par * R: par * R + R]
            wv = wr.rearrange("p (nb t) -> p nb t", t=32)
            nc.vector.tensor_reduce(blkm[:, :nblk], wv, AX.X, ALU.max,
                                    apply_absolute_value=True)
            t_ts = I(nc.vector.tensor_scalar(out=sblk[:, :nblk], in0=blkm[:, :nblk],
                                             scalar1=float(np.float32(1 / 7.0)),
                                             scalar2=1e-12, op0=ALU.mult,
                                             op1=ALU.max), "sV", 2)
            W(nc.scalar, "sV", t_ts)
            t_cp = I(nc.scalar.activation(out=sblk2[:, :nblk], in_=sblk[:, :nblk],
                                          func=AF.Identity), "sA")
            W(nc.vector, "sA", t_cp)
            nc.vector.reciprocal(iblk[:, :nblk], sblk2[:, :nblk])
            ib = AP(_h(iblk), 0, [[nblk, 128], [1, nblk], [0, 32]])
            sb_ = AP(_h(sblk), 0, [[nblk, 128], [1, nblk], [0, 32]])
            nc.vector.tensor_tensor(tf[:, :R], wr, ib, ALU.mult)
            nc.vector.tensor_scalar(out=tf[:, :R], in0=tf[:, :R], scalar1=CMAG,
                                    scalar2=CMAG, op0=ALU.add, op1=ALU.subtract)
            t_cons[par] = I(nc.vector.tensor_tensor(wqb[:, :R], tf[:, :R], sb_,
                                                    ALU.mult), "sV", 4)
            W(nc.tensor, "sV", t_cons[par])
            for ti in range(R // 128):
                tpar = ti % 2
                W(nc.tensor, "sV", t_tpev[tpar])
                tt = nc.tensor.transpose(tpsl[tpar][:, :],
                                         wqb[:, ti * 128:(ti + 1) * 128],
                                         K.ident_sb[:, :])
                t_tp = I(tt, "sT")
                W(nc.vector, "sT", t_tp)
                tig = (ch % 2) * 12 + ti
                chg = ch // 2
                dst = dstsb[:, tig * 1024 + chg * 128: tig * 1024 + (chg + 1) * 128]
                t_tpev[tpar] = I(nc.vector.tensor_copy(dst, tpsl[tpar][:, :]), "sV")
        W(nc.sync, "sV")
        stj.close()
    stp.close()
    st = ExitStack()
    oqsb = K.sb(st, "b4_oq", [128, NIC * L], BF16)
    och = K.sb(st, "b4_och", [128, L], BF16)
    x2ch = K.sb(st, "b4_x2", [128, L], F32)
    qsc4 = x2ch
    t1b = K.sb(st, "b4_t1b", [128, 2 * 512], F32)
    ost = K.sb(st, "b4_ost", [128, L], F32)
    dnps = [K.ps(st, f"b4_dn{i}", [128, 512]) for i in range(2)]
    t_in = [0, 0]
    t_oq = 0
    t_dnMM = 0
    t_ostd = [0, 0]
    t_ev = [0, 0]
    t_t1c = 0
    for b in range(NB):
        W(nc.vector, "sT", t_dnMM)
        for Ic in range(NIC):
            par = 0
            W(nc.sync, "sV", t_in[par])
            t_l = I(nc.sync.dma_start(out=och[:, :],
                                      in_=K.odr[b, Ic * 128:(Ic + 1) * 128, :]),
                    "d2")
            W(nc.vector, "d2", t_l)
            nc.vector.tensor_scalar(out=qsc4[:, :], in0=och[:, :],
                                    scalar1=K.scales[:, 8:9], scalar2=CMAG,
                                    op0=ALU.mult, op1=ALU.add)
            t_oq = I(nc.vector.tensor_scalar(
                out=oqsb[:, Ic * L:(Ic + 1) * L], in0=qsc4[:, :],
                scalar1=CMAG, scalar2=K.scales[:, 9:10], op0=ALU.subtract,
                op1=ALU.mult), "sV", 2)
            t_in[par] = t_oq
        if K.dbg_taps:
            W(nc.gpsimd, "sV", t_oq)
            I(nc.gpsimd.dma_start(out=K.taps["tap_oq"][b, :, :], in_=oqsb[:, :]),
              "dG")
        t_x2c = [0, 0]
        for dc in range(NDC):
            par = dc % 2
            W(nc.sync, "sV", max(t_x2c[0], t_oq))
            t_x = I(nc.sync.dma_start(out=x2ch[:, :],
                                      in_=K.x2dr[b, dc * 128:(dc + 1) * 128, :]),
                    "d3")
            for s in range(NS):
                spar = s % 2
                W(nc.tensor, "sV", max(t_oq, t_ev[spar]))
                W(nc.tensor, "sA", t_t1c)
                for Ic in range(NIC):
                    mm = nc.tensor.matmul(
                        dnps[spar][:, :],
                        wdT[:, Ic * 1024 + dc * 128: Ic * 1024 + (dc + 1) * 128],
                        oqsb[:, Ic * L + s * 512: Ic * L + (s + 1) * 512],
                        start=(Ic == 0), stop=(Ic == NIC - 1))
                t_dnMM = t_mm = I(mm, "sT")
                W(nc.scalar, "sT", t_mm)
                W(nc.scalar, "sV", t_ev[spar])
                t_t1c = I(nc.scalar.activation(
                    out=t1b[:, spar * 512:(spar + 1) * 512], in_=dnps[spar][:, :],
                    func=AF.Identity, scale=vcol(K, b, 5, dc), bias=0.0), "sA")
                W(nc.vector, "sA", t_t1c)
                W(nc.vector, "d3", t_x)
                if s == 0:
                    W(nc.vector, "d4", t_ostd[0])
                t_o = I(nc.vector.tensor_add(
                    ost[:, s * 512:(s + 1) * 512],
                    t1b[:, spar * 512:(spar + 1) * 512],
                    x2ch[:, s * 512:(s + 1) * 512]), "sV")
                t_ev[spar] = t_o
                t_x2c[0] = t_o
            W(nc.gpsimd, "sV", t_o)
            t_ostd[0] = I(nc.gpsimd.dma_start(
                out=K.outfm[b, dc * 128:(dc + 1) * 128, :],
                in_=ost[:, :]), "d4")
    W(nc.gpsimd, "d4")
    W(nc.gpsimd, "dG")
    st.close()
    stW.close()


def _prep_inputs(inputs, r):
    f32 = np.float32
    u = np.ascontiguousarray(
        np.asarray(inputs["L0_attn_u"], f32)[4 * r:4 * r + 4].transpose(0, 2, 1))
    def _projp(w):
        wt = np.asarray(w, f32).T
        return np.ascontiguousarray(
            wt.reshape(8, 128, 8, 128).transpose(2, 1, 0, 3).reshape(D, D))

    wqT = _projp(inputs["L0_attn_Wq"])
    wkT = _projp(inputs["L0_attn_Wk"])
    wvT = _projp(inputs["L0_attn_Wv"])
    p = np.arange(1024)
    perm = (p % 8) * 128 + (p // 128) * 16 + ((p % 128) // 8)
    woTp = np.ascontiguousarray(np.asarray(inputs["L0_attn_Wo"], f32).T[perm])
    biases = np.ascontiguousarray(
        np.concatenate([np.asarray(inputs[k], f32).reshape(8, 128).T
                        for k in ("L0_attn_bq", "L0_attn_bk", "L0_attn_bv",
                                  "L0_attn_bo")], axis=1))
    kkpad = np.pad(np.asarray(inputs["L0_attn_ssm_k_kernel"], f32),
                   ((0, 0), (127, 1)))
    skD = np.ascontiguousarray(
        np.asarray(inputs["L0_attn_ssm_k_D"], f32).reshape(8, 128).T)
    kcpad = np.pad(np.asarray(inputs["L0_attn_ssm_kernel"], f32),
                   ((0, 0), (127, 1)))
    Dc = np.ascontiguousarray(np.asarray(inputs["L0_attn_D"], f32).reshape(128, 1))
    return {
        "u_fm": u, "wqT": wqT, "wkT": wkT, "wvT": wvT, "woTp": woTp,
        "biases": biases, "kkpad": kkpad, "skD": skD, "kcpad": kcpad, "Dc": Dc,
        "wg": np.asarray(inputs["L0_ffn_w_gate"], f32),
        "wu": np.asarray(inputs["L0_ffn_w_up"], f32),
        "wdn": np.asarray(inputs["L0_ffn_w_down"], f32),
        "wmod": np.asarray(inputs["L0_wmod"], f32),
        "c_own": np.ascontiguousarray(np.asarray(inputs["L0_c"], f32)[4 * r:4 * r + 4]),
    }


def kernel(**inputs):
    nc = build(dbg_taps=False)
    in_maps = [_prep_inputs(inputs, r) for r in range(8)]
    res = run_bass_kernel_spmd(nc, in_maps, core_ids=list(range(8)))
    out = np.concatenate([np.asarray(res.results[r]["outfm"]) for r in range(8)],
                         axis=0)
    return np.ascontiguousarray(out.transpose(0, 2, 1))


# version 06 (final)
# speedup vs baseline: 1.3183x; 1.0045x over previous
import numpy as np
import ml_dtypes
from contextlib import ExitStack

import concourse.bass as bass
import concourse.mybir as mybir
from concourse.ap import AP
from concourse.bass_utils import run_bass_kernel_spmd

F32 = mybir.dt.float32
BF16 = mybir.dt.bfloat16
AF = mybir.ActivationFunctionType
ALU = mybir.AluOpType
AX = mybir.AxisListType

CMAG = float(np.float32(1.5 * 2 ** 23))
NB, D, L, BL, NI = 4, 1024, 2048, 128, 16
NDC, NEC, NH = 8, 8, 8
FF, NIC, NS = 3072, 24, 4
EPS = 1e-5
BEI = NB * D * NI


def _h(t):
    return t.tensor if isinstance(t, AP) else t


class KB:
    def __init__(self, dbg_taps=False):
        nc = bass.Bass("TRN2", target_bir_lowering=False,
                       detect_race_conditions=False)
        self.nc = nc
        self.st = ExitStack()
        self.sem = {}
        self.cnt = {}
        for n in ("dS", "dG", "d2", "d3", "d4", "d5", "d6",
                  "d7", "d8", "d9", "dA", "dB",
                  "sT", "sV", "sA", "sCC"):
            self.sem[n] = self.st.enter_context(nc.semaphore(n))
            self.cnt[n] = 0
        self.dbg_taps = dbg_taps

    def I(self, instr, name, amt=None):
        if amt is None:
            amt = 16 if name.startswith("d") else 1
        if name.startswith("d") and self.cnt[name] > 0:
            instr._wait_ge(self.sem[name], self.cnt[name])
        instr.then_inc(self.sem[name], amt)
        self.cnt[name] += amt
        return self.cnt[name]

    def W(self, eng, name, val=None):
        v = self.cnt[name] if val is None else val
        if v > 0:
            eng.wait_ge(self.sem[name], v)

    def sb(self, stack, n, s, d):
        return stack.enter_context(self.nc.sbuf_tensor(n, s, d))

    def ps(self, stack, n, s, d=F32):
        return stack.enter_context(self.nc.psum_tensor(n, s, d))

    def bar(self):
        self.nc.all_engine_barrier()

    def joinD(self, eng):
        for n in ("dS", "dG", "d2", "d3", "d4", "d5", "d6",
                  "d7", "d8", "d9", "dA", "dB"):
            if self.cnt[n] > 0:
                eng.wait_ge(self.sem[n], self.cnt[n])


def build(dbg_taps=False, rg=None, upto='B'):
    K = KB(dbg_taps)
    K.rg = rg or [list(range(8))]
    K.upto = upto
    nc = K.nc

    di = lambda n, s: nc.dram_tensor(n, s, F32, kind="ExternalInput")
    K.u_fm = di("u_fm", [NB, D, L])
    K.wqT, K.wkT, K.wvT = di("wqT", [D, D]), di("wkT", [D, D]), di("wvT", [D, D])
    K.woTp = di("woTp", [D, D])
    K.biases = di("biases", [128, 32])
    K.kkpad = di("kkpad", [D, 2176])
    K.skD = di("skD", [128, 8])
    K.kcpad = di("kcpad", [128, 2176])
    K.Dc = di("Dc", [128, 1])
    K.wg, K.wu = di("wg", [FF, D]), di("wu", [FF, D])
    K.wdn = di("wdn", [D, FF])
    K.wmod = di("wmod", [D, 6 * D])
    K.c_own = di("c_own", [NB, D])
    K.outfm = nc.dram_tensor("outfm", [NB, D, L], F32, kind="ExternalOutput")

    ds = lambda n, s, d=F32: nc.dram_tensor(n, s, d)
    K.kkbf = ds("kkbf", [D, 2176], BF16)
    K.kcbf = ds("kcbf", [128, 2176], BF16)
    K.qtm, K.vtm = ds("qtm", [128, BEI], BF16), ds("vtm", [128, BEI], BF16)
    K.ktm, K.kptm = ds("ktm", [128, BEI], BF16), ds("kptm", [128, BEI], BF16)
    K.ytr = ds("ytr", [NB * 8, 128, L], BF16)
    K.x2dr = ds("x2dr", [NB, D, L], F32)
    K.st2dr = ds("st2dr", [NB, 2, L], F32)
    K.moddr = ds("moddr", [NB, 6 * D], F32)
    K.blhsdr = ds("blhsdr", [2, NB * 2 * D], F32)
    K.gatedr, K.updr = ds("gatedr", [NB, FF, L], BF16), ds("updr", [NB, FF, L], BF16)
    K.g2dr, K.odr = ds("g2dr", [NB, FF, L], BF16), ds("odr", [NB, FF, L], BF16)
    K.ar0i, K.ar0o = ds("ar0i", [1, 8]), ds("ar0o", [1, 8])
    K.ar1i, K.ar1o = ds("ar1i", [1, 8]), ds("ar1o", [1, 8])
    K.ar2i, K.ar2o = ds("ar2i", [1, 8]), ds("ar2o", [1, 8])
    K.ar3i, K.ar3o = ds("ar3i", [1, 8]), ds("ar3o", [1, 8])

    K.identbf = nc.inline_tensor(np.eye(128, dtype=ml_dtypes.bfloat16), "identbf")
    K.identf = nc.inline_tensor(np.eye(128, dtype=np.float32), "identf")
    K.c1024 = nc.inline_tensor(
        np.full((128, 1), 1.0 / 1024.0, dtype=ml_dtypes.bfloat16), "c1024")
    K.onescol = nc.inline_tensor(np.ones((1, 128), np.float32), "onescol")
    K.onesrow = nc.inline_tensor(np.ones((1, L), np.float32), "onesrow")

    if dbg_taps:
        K.taps = {}
        for n, s, d in [("tap_h", [128, NDC * L], BF16),
                        ("tap_ktm", [128, BEI], BF16),
                        ("tap_kptm", [128, BEI], BF16),
                        ("tap_ytr", [NB * 8, 128, L], BF16),
                        ("tap_x2", [NB, D, L], F32),
                        ("tap_scales", [128, 16], F32),
                        ("tap_oq", [NB, 128, NIC * L], BF16),
                        ("tap_vecs", [128, NB * 6 * NDC], F32),
                        ("tap_gabo", [128, NB * NDC], F32)]:
            K.taps[n] = nc.dram_tensor(n, s, d, kind="ExternalOutput")

    P = K.st
    K.ident_sb = K.sb(P, "ident_sb", [128, 128], BF16)
    K.identf_sb = K.sb(P, "identf_sb", [128, 128], F32)
    K.c1024_sb = K.sb(P, "c1024_sb", [128, 1], BF16)
    K.onescol_sb = K.sb(P, "onescol_sb", [1, 128], F32)
    K.bias_sb = K.sb(P, "bias_sb", [128, 32], F32)
    K.vecs = K.sb(P, "vecs", [128, NB * 6 * NDC], F32)
    K.gabo = K.sb(P, "gabo", [128, NB * NDC], F32)
    K.blhs = K.sb(P, "blhs", [2, D], F32)
    K.blhs_bf = K.sb(P, "blhs_bf", [2, D], BF16)
    K.rstd_bf = K.sb(P, "rstd_bf", [1, L], BF16)
    K.brhs_bf = K.sb(P, "brhs_bf", [2, L], BF16)
    K.brhs = K.sb(P, "brhs", [2, L], F32)
    K.rstd_sb = K.sb(P, "rstd_sb", [1, L], F32)
    K.scales = K.sb(P, "scales", [128, 16], F32)
    K.red1 = K.sb(P, "red1", [128, 1], F32)
    K.scrow = K.sb(P, "scrow", [1, 12], F32)
    K.redcg = K.sb(P, "redcg", [128, 384], F32)
    K.redcu = K.sb(P, "redcu", [128, 384], F32)

    K._blhs_cons = 0
    K._lnev = 0
    K._fin_t = 0

    if upto.startswith("BD"):
        order = [("P0", phase_P0), ("B", phase_B)]
    else:
        order = [("P0", phase_P0), ("A1", phase_A1), ("A2", phase_A2),
                 ("A3", phase_A3), ("A4", phase_A4)]
        if upto in ("B", "Bp", "B1", "B2", "B3", "B1h", "B1m", "B1m0", "B1m1",
                    "B1m2", "B1mA", "B1bar"):
            order.append(("B", phase_B))
    for name, fn in order:
        fn(K)
        K.bar()
        if name == upto or (name == "B"):
            break
    K.st.close()
    build._last_K = K
    return nc


def vcol(K, b, kind, dc):
    c = (kind * NDC + dc) * NB + b
    return K.vecs[:, c:c + 1]


def phase_P0(K):
    nc, I, W = K.nc, K.I, K.W
    st = ExitStack()
    kch = K.sb(st, "p0_kch", [128, 2176], F32)
    kchb = K.sb(st, "p0_kchb", [128, 2176], BF16)
    dcol = K.sb(st, "p0_dcol", [128, 8], F32)
    dccol = K.sb(st, "p0_dccol", [128, 1], F32)
    silc = K.sb(st, "p0_silc", [NB, D], F32)
    silcb = K.sb(st, "p0_silcb", [NB, D], BF16)
    silT = K.sb(st, "p0_silT", [128, NDC * NB], BF16)
    wmch = K.sb(st, "p0_wmch", [128, 2048], F32)
    wmchb = K.sb(st, "p0_wmchb", [128, 2 * 2048], BF16)
    mod_sb = K.sb(st, "p0_mod", [NB, 6 * D], F32)
    msk = K.sb(st, "p0_msk", [NB, 2 * D], mybir.dt.uint8)
    tpos = K.sb(st, "p0_tpos", [NB, 2 * D], F32)
    tneg = K.sb(st, "p0_tneg", [NB, 2 * D], F32)
    tps = K.ps(st, "p0_tps", [128, NB], BF16)
    tpsf = K.ps(st, "p0_tpsf", [128, 128], F32)
    mps = K.ps(st, "p0_mps", [NB, 2048], F32)

    nc.vector.memset(K.scales[:, :], 0.0)
    nc.vector.memset(K.scrow[:1, :], 0.0)
    I(nc.sync.dma_start(out=K.ident_sb[:, :], in_=K.identbf[:, :]), "dS")
    I(nc.sync.dma_start(out=K.identf_sb[:, :], in_=K.identf[:, :]), "dS")
    I(nc.sync.dma_start(out=K.c1024_sb[:, :], in_=K.c1024[:, :]), "dS")
    I(nc.sync.dma_start(out=K.onescol_sb[:, :], in_=K.onescol[:, :]), "dS")
    I(nc.sync.dma_start(out=K.bias_sb[:, :], in_=K.biases[:, :]), "dS")
    I(nc.sync.dma_start(out=K.brhs[1:2, :], in_=K.onesrow[:, :]), "dS")
    I(nc.sync.dma_start(out=dcol[:, :], in_=K.skD[:, :]), "dS")
    I(nc.sync.dma_start(out=dccol[:, :], in_=K.Dc[:, :]), "dS")
    t_small = I(nc.sync.dma_start(out=silc[:, :], in_=K.c_own[:, :]), "dS")

    t_store = 0
    for ci in range(9):
        src = K.kcpad[:, :] if ci == 0 else K.kkpad[(ci - 1) * 128:ci * 128, :]
        t_ld = I(nc.sync.dma_start(out=kch[:, :], in_=src), "dS")
        W(nc.vector, "dS", t_ld)
        W(nc.vector, "dG", t_store)
        dv = dccol[:, 0:1] if ci == 0 else dcol[:, ci - 1:ci]
        nc.vector.tensor_add(kch[:, 127:128], kch[:, 127:128], dv)
        t_cast = I(nc.vector.tensor_copy(kchb[:, :], kch[:, :]), "sV", 2)
        W(nc.sync, "sV", t_cast)
        W(nc.gpsimd, "sV", t_cast)
        dst = K.kcbf[:, :] if ci == 0 else K.kkbf[(ci - 1) * 128:ci * 128, :]
        t_store = I(nc.gpsimd.dma_start(out=dst, in_=kchb[:, :]), "dG")

    W(nc.scalar, "dS", t_small)
    t_sil = I(nc.scalar.activation(out=silc[:, :], in_=silc[:, :], func=AF.Silu),
              "sA")
    W(nc.vector, "sA", t_sil)
    t_silb = I(nc.vector.tensor_copy(silcb[:, :], silc[:, :]), "sV")
    for dc in range(NDC):
        W(nc.tensor, "sV", t_silb)
        if dc >= 1:
            W(nc.tensor, "sV", t_tpc)
        t_tp = I(nc.tensor.transpose(tps[:, :], silcb[:, dc * 128:(dc + 1) * 128],
                                     K.ident_sb[0:NB, 0:NB]), "sT")
        W(nc.vector, "sT", t_tp)
        t_tpc = I(nc.vector.tensor_copy(silT[:, dc * NB:(dc + 1) * NB], tps[:, :]),
                  "sV")
    t_mm = 0
    for s3 in range(3):
        for dc in range(NDC):
            W(nc.sync, "sV")
            t_wl = I(nc.sync.dma_start(
                out=wmch[:, :], in_=K.wmod[dc * 128:(dc + 1) * 128,
                                           s3 * 2048:(s3 + 1) * 2048]), "dS")
            W(nc.vector, "dS", t_wl)
            par = dc % 2
            if t_mm:
                W(nc.vector, "sT", t_mm)
            t_cb = I(nc.vector.tensor_copy(wmchb[:, par * 2048:(par + 1) * 2048],
                                           wmch[:, :]), "sV")
            W(nc.tensor, "sV", t_cb)
            for s5 in range(4):
                mm = nc.tensor.matmul(
                    mps[:, s5 * 512:(s5 + 1) * 512],
                    silT[:, dc * NB:(dc + 1) * NB],
                    wmchb[:, par * 2048 + s5 * 512: par * 2048 + (s5 + 1) * 512],
                    start=(dc == 0), stop=(dc == NDC - 1))
            t_mm = I(mm, "sT")
        W(nc.vector, "sT", t_mm)
        I(nc.vector.tensor_copy(mod_sb[:, s3 * 2048:(s3 + 1) * 2048], mps[:, :]),
          "sV")
    for kk_ in (2, 5):
        gsl = mod_sb[:, kk_ * D:(kk_ + 1) * D]
        nc.vector.tensor_scalar(out=msk[:, 0:D], in0=gsl, scalar1=0.0,
                                scalar2=None, op0=ALU.is_ge)
        nc.vector.tensor_scalar_max(tpos[:, 0:D], gsl, 0.25)
        nc.vector.tensor_scalar_min(tneg[:, 0:D], gsl, -0.25)
        nc.vector.select(gsl, msk[:, 0:D], tpos[:, 0:D], tneg[:, 0:D])
    csl = AP(_h(mod_sb), 1 * D, [[6 * D, NB], [3 * D, 2], [1, D]])
    t_mod = I(nc.vector.tensor_scalar_add(csl, csl, 1.0), "sV", 5)
    W(nc.gpsimd, "sV", t_mod)
    t_md = I(nc.gpsimd.dma_start(out=K.moddr[:, :], in_=mod_sb[:, :]), "dG")
    W(nc.gpsimd, "dG", t_md)
    for b in range(NB):
        for ln in range(2):
            base = (b * 2 + ln) * D
            I(nc.gpsimd.dma_start(
                out=AP(K.blhsdr, base, [[1, 1], [1, D]]),
                in_=K.moddr[b:b + 1, (1 + 3 * ln) * D:(2 + 3 * ln) * D]), "dG")
            I(nc.gpsimd.dma_start(
                out=AP(K.blhsdr, NB * 2 * D + base, [[1, 1], [1, D]]),
                in_=K.moddr[b:b + 1, (0 + 3 * ln) * D:(1 + 3 * ln) * D]), "dG")
    t_vc = t_mod
    for kind in range(6):
        for dc in range(NDC):
            if t_vc:
                W(nc.tensor, "sV", t_vc)
            t_tp = I(nc.tensor.transpose(
                tpsf[:, 0:NB], mod_sb[:, kind * D + dc * 128: kind * D + (dc + 1) * 128],
                K.identf_sb[0:NB, 0:NB]), "sT")
            W(nc.vector, "sT", t_tp)
            col = (kind * NDC + dc) * NB
            t_vc = I(nc.vector.tensor_copy(K.vecs[:, col:col + NB],
                                           tpsf[:, 0:NB]), "sV")
    for b in range(NB):
        gav = AP(_h(K.vecs), (2 * NDC) * NB + b, [[NB * 6 * NDC, 128], [NB, NDC]])
        t_gb = I(nc.vector.tensor_mul(K.gabo[:, b * NDC:(b + 1) * NDC], gav,
                                      K.bias_sb[:, 24:32]), "sV")
    if K.dbg_taps:
        W(nc.gpsimd, "sV", t_gb)
        I(nc.gpsimd.dma_start(out=K.taps["tap_vecs"][:, :], in_=K.vecs[:, :]), "dG")
        I(nc.gpsimd.dma_start(out=K.taps["tap_gabo"][:, :], in_=K.gabo[:, :]), "dG")
    st.close()


def row_stats(K, m_sb, s2_sb, rowt, rowt2):
    nc, I, W = K.nc, K.I, K.W
    v = nc.vector
    W(v, "sT", K._blhs_cons)
    v.tensor_mul(rowt[:1, :], m_sb[:1, :], m_sb[:1, :])
    v.tensor_sub(rowt[:1, :], s2_sb[:1, :], rowt[:1, :])
    t = I(v.tensor_scalar_add(rowt[:1, :], rowt[:1, :], EPS), "sV", 3)
    W(nc.scalar, "sV", t)
    t = I(nc.scalar.activation(out=K.rstd_sb[:1, :], in_=rowt[:1, :],
                               func=AF.Identity), "sA")
    W(v, "sA", t)
    t = I(v.reciprocal(rowt2[:1, :], K.rstd_sb[:1, :]), "sV", 1)
    W(nc.scalar, "sV", t)
    t = I(nc.scalar.activation(out=K.rstd_sb[:1, :], in_=rowt2[:1, :],
                               func=AF.Sqrt), "sA")
    W(v, "sA", t)
    v.tensor_mul(rowt2[:1, :], K.rstd_sb[:1, :], K.rstd_sb[:1, :])
    v.tensor_mul(rowt2[:1, :], rowt2[:1, :], rowt[:1, :])
    v.tensor_scalar(out=rowt2[:1, :], in0=rowt2[:1, :], scalar1=-0.5, scalar2=1.5,
                    op0=ALU.mult, op1=ALU.add)
    v.tensor_mul(K.rstd_sb[:1, :], K.rstd_sb[:1, :], rowt2[:1, :])
    return I(v.scalar_tensor_tensor(K.brhs[0:1, :], m_sb[:1, :], -1.0,
                                    K.rstd_sb[:1, :], ALU.mult, ALU.mult), "sV", 5)


def ln_stats(K, xsrc, m_sb, s2_sb, stps, xsq):
    nc, I, W = K.nc, K.I, K.W
    W(nc.scalar, "sV")
    t_mm = 0
    for s in range(NS):
        for dc in range(NDC):
            xs = xsrc(dc, s)
            if t_mm:
                W(nc.scalar, "sT", t_mm)
            t_sq = I(nc.scalar.activation(out=xsq[:, :], in_=xs, func=AF.Square),
                     "sA")
            W(nc.tensor, "sA", t_sq)
            if dc == 0:
                W(nc.tensor, "sV", K._lnev)
            nc.tensor.matmul(stps[0:1, :], K.c1024_sb[:, :], xs,
                             start=(dc == 0), stop=(dc == NDC - 1))
            t_mm = I(nc.tensor.matmul(stps[32:33, :], K.c1024_sb[:, :], xsq[:, :],
                                      start=(dc == 0), stop=(dc == NDC - 1)), "sT")
        W(nc.vector, "sT", t_mm)
        nc.vector.tensor_copy(m_sb[:1, s * 512:(s + 1) * 512], stps[0:1, :])
        K._lnev = I(nc.vector.tensor_copy(s2_sb[:1, s * 512:(s + 1) * 512],
                                          stps[32:33, :]), "sV", 2)


def ab_maps(K, dc, s, abA, abB):
    nc, I = K.nc, K.I
    lA = AP(_h(K.blhs_bf), dc * 128, [[D, 1], [1, 128]])
    lB = AP(_h(K.blhs_bf), dc * 128, [[D, 2], [1, 128]])
    nc.tensor.matmul(abA[:, :], lA, K.rstd_bf[:1, s * 512:(s + 1) * 512],
                     start=True, stop=True)
    t = I(nc.tensor.matmul(abB[:, :], lB, K.brhs_bf[:, s * 512:(s + 1) * 512],
                           start=True, stop=True), "sT")
    K._blhs_cons = t
    return t


def rows_to_bf(K):
    nc, I, W = K.nc, K.I, K.W
    W(nc.vector, "sT", K._blhs_cons)
    nc.vector.tensor_copy(K.rstd_bf[:1, :], K.rstd_sb[:1, :])
    return I(nc.vector.tensor_copy(K.brhs_bf[:, :], K.brhs[:, :]), "sV")


def blhs_to_bf(K):
    nc, I, W = K.nc, K.I, K.W
    W(nc.vector, "sT", K._blhs_cons)
    return I(nc.vector.tensor_copy(K.blhs_bf[:, :], K.blhs[:, :]), "sV")


def phase_A1(K):
    nc, I, W = K.nc, K.I, K.W
    nc.sync.wait_ge(K.sem["dG"], K.cnt["dG"])
    st = ExitStack()
    x_bf = K.sb(st, "a1_xbf", [128, NDC * L], BF16)
    h_bf = K.sb(st, "a1_hbf", [128, NDC * L], BF16)
    xch = K.sb(st, "a1_xch", [128, 2 * L], F32)
    xsq = K.sb(st, "a1_xsq", [128, 512], BF16)
    pfm = K.sb(st, "a1_pfm", [128, 2 * L], BF16)
    stag = K.sb(st, "a1_stag", [128, D * NI], BF16)
    wsl = K.sb(st, "a1_wsl", [128, 2 * 1024], F32)
    wslb = K.sb(st, "a1_wslb", [128, 2 * 1024], BF16)
    m_sb = K.sb(st, "a1_m", [1, L], F32)
    s2_sb = K.sb(st, "a1_s2", [1, L], F32)
    rowt = K.sb(st, "a1_rt", [1, L], F32)
    rowt2 = K.sb(st, "a1_rt2", [1, L], F32)
    stps = K.ps(st, "a1_st", [128, 512])
    abps = [K.ps(st, f"a1_ab{i}", [128, 512]) for i in range(2)]
    pps = [K.ps(st, f"a1_pp{i}", [128, 512]) for i in range(2)]
    tpsl = [K.ps(st, f"a1_tp{i}", [128, 128], BF16) for i in range(2)]

    t_abcons = [0, 0]
    t_tpev = [0, 0]
    t_lastTp = [0, 0]
    t_wmm = [0, 0]
    stg_tok = [("dG", 0), ("dB", 0), ("dG", 0), ("dB", 0)]
    for b in range(NB):
        W(nc.sync, "sT", K._blhs_cons)
        t_bl = I(nc.sync.dma_start(
            out=K.blhs[:, :], in_=AP(K.blhsdr, (b * 2 + 0) * D,
                                     [[NB * 2 * D, 2], [1, D]])), "dS")
        t_xc = [0, 0]
        for dc in range(NDC):
            par = dc % 2
            xs_ = ("d2", "d7")[par]
            W(nc.sync, "sV", t_xc[par])
            t_ld = I(nc.sync.dma_start(out=xch[:, par * L:(par + 1) * L],
                                       in_=K.u_fm[b, dc * 128:(dc + 1) * 128, :]),
                     xs_)
            W(nc.vector, xs_, t_ld)
            if b > 0 and dc == 0:
                W(nc.vector, "sT")
            t_xc[par] = I(nc.vector.tensor_copy(
                x_bf[:, dc * L:(dc + 1) * L], xch[:, par * L:(par + 1) * L]), "sV")
        ln_stats(K, lambda dc, s: x_bf[:, dc * L + s * 512: dc * L + (s + 1) * 512],
                 m_sb, s2_sb, stps, xsq)
        row_stats(K, m_sb, s2_sb, rowt, rowt2)
        W(nc.vector, "dS", t_bl)
        rows_to_bf(K)
        blhs_to_bf(K)
        W(nc.tensor, "sV")
        t_h = 0
        for s in range(NS):
            for dc in range(NDC):
                W(nc.tensor, "sV", t_abcons[0])
                t_ab = ab_maps(K, dc, s, abps[0], abps[1])
                W(nc.vector, "sT", t_ab)
                sl = h_bf[:, dc * L + s * 512: dc * L + (s + 1) * 512]
                xsl = x_bf[:, dc * L + s * 512: dc * L + (s + 1) * 512]
                nc.vector.scalar_tensor_tensor(sl, xsl, 1.0, abps[0][:, :],
                                               ALU.mult, ALU.mult)
                t_h = I(nc.vector.tensor_add(sl, sl, abps[1][:, :]),
                        "sV", 2)
                t_abcons[0] = t_h
        if not hasattr(K, "_a1wc"):
            K._a1wc = [0, 0]
        for ip, (nm, wdr, dr) in enumerate(
                (("q", K.wqT, K.qtm), ("k", K.wkT, K.ktm), ("v", K.wvT, K.vtm))):
            t_evprev = 0
            for ec in range(NEC):
                par = ec % 2
                ws_ = ("d3", "d8")[par]
                W(nc.sync, "sV", K._a1wc[par])
                t_wl = I(nc.sync.dma_start(
                    out=wsl[:, par * 1024:(par + 1) * 1024],
                    in_=AP(wdr, ec * 128 * D, [[D, 128], [1, D]])),
                    ws_)
                W(nc.vector, ws_, t_wl)
                W(nc.vector, "sT", t_wmm[par])
                K._a1wc[par] = I(nc.vector.tensor_copy(
                    wslb[:, par * 1024:(par + 1) * 1024],
                    wsl[:, par * 1024:(par + 1) * 1024]), "sV")
                W(nc.tensor, "sV", max(K._a1wc[par], t_h))
                for hf in range(2):
                    W(nc.tensor, "sA", t_evprev)
                    for dc in range(NDC):
                        for s in range(2):
                            mm = nc.tensor.matmul(
                                pps[s][:, :],
                                wslb[:, par * 1024 + dc * 128: par * 1024 + (dc + 1) * 128],
                                h_bf[:, dc * L + hf * 1024 + s * 512: dc * L + hf * 1024 + (s + 1) * 512],
                                start=(dc == 0), stop=(dc == NDC - 1))
                    t_mm = I(mm, "sT")
                    W(nc.scalar, "sT", t_mm)
                    if hf == 1:
                        W(nc.scalar, "sT", t_lastTp[par])
                    for s in range(2):
                        rev = AP(pps[s], 127, [[512, 128], [128, 4], [-1, 128]])
                        ev = nc.scalar.activation(
                            out=pfm[:, par * L + hf * 1024 + s * 512: par * L + hf * 1024 + (s + 1) * 512],
                            in_=rev, func=AF.Identity,
                            bias=K.bias_sb[:, ip * 8 + ec: ip * 8 + ec + 1], scale=1.0)
                    t_evprev = I(ev, "sA", 2)
                t_wmm[par] = t_mm
                W(nc.tensor, "sA", t_evprev)
                if ec % 2 == 0:
                    W(nc.tensor, *stg_tok[ec // 2])
                for Ib in range(NI):
                    tpar = Ib % 2
                    W(nc.tensor, "sV", t_tpev[tpar])
                    tt = nc.tensor.transpose(
                        tpsl[tpar][:, :],
                        pfm[:, par * L + Ib * 128: par * L + (Ib + 1) * 128],
                        K.ident_sb[:, :])
                    t_tp = I(tt, "sT")
                    t_lastTp[par] = t_tp
                    W(nc.vector, "sT", t_tp)
                    dst = AP(_h(stag), ec * 128 * NI + Ib,
                             [[D * NI, 128], [NI, 128]])
                    t_tpev[tpar] = I(nc.vector.tensor_copy(dst, tpsl[tpar][:, :]),
                                     "sV")
                if ec % 2 == 1:
                    g = ec // 2
                    sts_ = ("dG", "dB")[g % 2]
                    W(nc.gpsimd, "sV", max(t_tpev))
                    cw = 2 * 128 * NI
                    stg_tok[g] = (sts_, I(nc.gpsimd.dma_start(
                        out=dr[:, b * D * NI + g * cw: b * D * NI + (g + 1) * cw],
                        in_=stag[:, g * cw:(g + 1) * cw]), sts_))
            if K.dbg_taps and nm == "k" and b == 0:
                I(nc.gpsimd.dma_start(out=K.taps["tap_h"][:, :], in_=h_bf[:, :]),
                  "dG")
    st.close()


def phase_A2(K):
    nc, I, W = K.nc, K.I, K.W
    K.joinD(nc.sync)
    st = ExitStack()
    ksb = K.sb(st, "a2_ksb", [128, BEI], BF16)
    skwr = [K.sb(st, f"a2_skwr{i}", [128, 2048], BF16) for i in range(4)]
    skw = [K.sb(st, f"a2_skw{i}", [128, 2048], BF16) for i in range(2)]
    kstage = K.sb(st, "a2_kstage", [128, 4096], BF16)
    kps = [K.ps(st, f"a2_kps{i}", [128, 512]) for i in range(2)]

    t_kld = I(nc.sync.dma_start(out=ksb[:, :], in_=K.ktm[:, :]), "dS")
    SKS = ("d2", "d7", "d9", "dA")
    t_raw = [0, 0, 0, 0]
    t_rvw = [0, 0, 0, 0]
    t_rev = [0, 0]
    t_mmd = [0, 0]
    t_bankev = [0, 0]
    t_flush = 0

    def pf_skw(e):
        par4 = e % 4
        W(nc.sync, "sV", t_rvw[par4])
        t_raw[par4] = I(nc.sync.dma_start(
            out=skwr[par4][:, :], in_=AP(K.kkbf, e * 2176, [[1, 128], [1, 2048]])),
            SKS[par4])

    for ee in range(4):
        pf_skw(ee)
    for e in range(D):
        par = e % 2
        par4 = e % 4
        W(nc.vector, SKS[par4], t_raw[par4])
        W(nc.vector, "sT", t_mmd[par])
        rev = AP(_h(skwr[par4]), 2047, [[2048, 128], [-1, 2048]])
        t_rev[par] = t_rvw[par4] = I(nc.vector.tensor_copy(skw[par][:, :], rev),
                                     "sV")
        if e + 4 < D:
            pf_skw(e + 4)
        bpar = (e // 8) % 2
        bank = kps[bpar]
        eg = e % 8
        W(nc.tensor, "sV", t_rev[par])
        if e == 0:
            W(nc.tensor, "dS", t_kld)
        if eg == 0:
            W(nc.tensor, "sA", t_bankev[bpar])
        for dd in range(NI):
            outw = AP(bank, eg * 64 + dd * 4, [[512, 128], [4, NI - dd], [1, 4]])
            rhs = AP(_h(ksb), e * NI, [[BEI, 128], [1, NI - dd], [D * NI, NB]])
            mm = nc.tensor.matmul(outw,
                                  skw[par][:, (NI - 1 - dd) * 128:(NI - dd) * 128],
                                  rhs, start=(dd == 0), stop=(dd == NI - 1))
        t_mmd[par] = I(mm, "sT")
        if eg == 7:
            grp = e // 8
            sgrp = grp % 8
            W(nc.scalar, "sT", t_mmd[par])
            if sgrp == 0:
                W(nc.scalar, "dG", t_flush)
            dst = AP(_h(kstage), sgrp * 8 * NI,
                     [[4096, 128], [NI, 8], [1, NI], [64 * NI, NB]])
            src = AP(bank, 0, [[512, 128], [64, 8], [4, NI], [1, 4]])
            t_bankev[bpar] = I(nc.scalar.copy(dst, src), "sA")
            if sgrp == 7:
                W(nc.gpsimd, "sA", t_bankev[bpar])
                e_start = (grp - 7) * 8
                t_flush = I(nc.gpsimd.dma_start(
                    out=AP(K.kptm, e_start * NI,
                           [[BEI, 128], [D * NI, NB], [1, 64 * NI]]),
                    in_=AP(_h(kstage), 0, [[4096, 128], [64 * NI, NB],
                                           [1, 64 * NI]])), "dG")
    if K.dbg_taps:
        W(nc.gpsimd, "dG", t_flush)
        I(nc.gpsimd.dma_start(out=K.taps["tap_kptm"][:, :], in_=K.kptm[:, :]), "dG")
        I(nc.gpsimd.dma_start(out=K.taps["tap_ktm"][:, :], in_=ksb[:, :]), "dG")
    st.close()


def phase_A3(K):
    nc, I, W = K.nc, K.I, K.W
    K.joinD(nc.sync)
    st = ExitStack()
    skwr = [K.sb(st, f"a3_skwr{i}", [128, 2048], BF16) for i in range(4)]
    skw = [K.sb(st, f"a3_skw{i}", [128, 2048], BF16) for i in range(2)]
    qc = [K.sb(st, f"a3_qc{i}", [128, 4096], BF16) for i in range(2)]
    vc = [K.sb(st, f"a3_vc{i}", [128, 4096], BF16) for i in range(2)]
    kc = [K.sb(st, f"a3_kc{i}", [128, 4096], BF16) for i in range(2)]
    kv = [K.sb(st, f"a3_kv{i}", [128, 1024], BF16) for i in range(2)]
    cbf = [K.sb(st, f"a3_cbf{i}", [128, 1024], BF16) for i in range(2)]
    qxp = [K.sb(st, f"a3_qxp{i}", [128, 1024], BF16) for i in range(2)]
    zb = [K.sb(st, f"a3_zb{i}", [128, 1024], BF16) for i in range(2)]
    z1 = K.sb(st, "a3_z1", [128, 512], BF16)
    z2 = K.sb(st, "a3_z2", [128, 256], BF16)
    yslab = [K.sb(st, f"a3_ys{bq}", [128, 2048], BF16) for bq in range(NB)]
    ytst = K.sb(st, "a3_ytst", [128, 2048], BF16)
    cps = [K.ps(st, f"a3_cp{i}", [128, 1024]) for i in range(2)]
    tpsl = [K.ps(st, f"a3_tp{i}", [128, 128], BF16) for i in range(2)]

    t_ld = [0, 0]
    t_rev = [0, 0]
    t_raw = [0, 0, 0, 0]
    t_rvw = [0, 0, 0, 0]
    t_qvk_cons = [0, 0]
    t_mm = [0, 0]
    t_qc_done = [0, 0]
    t_cbf_cons = [0, 0]
    t_slabtpT = 0
    t_ytstd = 0
    t_tpev = [0, 0]

    SKS = ("d2", "d7", "d9", "dA")

    def prefetch_skw(c):
        par4 = c % 4
        W(nc.sync, "sV", t_rvw[par4])
        t_raw[par4] = I(nc.sync.dma_start(
            out=skwr[par4][:, :], in_=AP(K.kcbf, c * 2176, [[1, 128], [1, 2048]])),
            SKS[par4])

    def prefetch_qvk(n):
        par8 = n % 2
        W(nc.sync, "sV", t_qvk_cons[par8])
        for buf, dr in ((qc[par8], K.qtm), (vc[par8], K.vtm), (kc[par8], K.kptm)):
            t = I(nc.sync.dma_start(
                out=AP(_h(buf), 0, [[4096, 128], [1024, NB], [1, 1024]]),
                in_=AP(dr, n * 1024,
                       [[BEI, 128], [D * NI, NB], [1, 1024]])), "d3")
        t_ld[par8] = t

    pend = [None]

    def post(p):
        c_, bq_, cbpar_, par8_, cb_, cl_ = p
        W(nc.scalar, "sT", t_mm[cbpar_])
        W(nc.scalar, "sV", t_cbf_cons[cbpar_])
        t_cbf = I(nc.scalar.copy(cbf[cbpar_][:, :], cps[cbpar_][:, :]), "sA")
        qsrc = AP(_h(qc[par8_]), bq_ * 1024 + cb_,
                  [[4096, 128], [1, NI], [NI, 8], [0, 8]])
        W(nc.vector, "sA", t_cbf)
        t_qx = I(nc.vector.tensor_copy(qxp[cbpar_][:, :], qsrc), "sV")
        t_qvk_cons[par8_] = t_qx
        nc.vector.tensor_mul(zb[cbpar_][:, :], cbf[cbpar_][:, :],
                             qxp[cbpar_][:, :])
        t_qc_done[cbpar_] = K.cnt["sV"] + 4
        i0a = AP(_h(zb[cbpar_]), 0, [[1024, 128], [64, NI], [8, 4], [1, 8]])
        i0b = AP(_h(zb[cbpar_]), 32, [[1024, 128], [64, NI], [8, 4], [1, 8]])
        o1 = AP(_h(z1), 0, [[512, 128], [32, NI], [8, 4], [1, 8]])
        nc.vector.tensor_tensor(o1, i0a, i0b, ALU.add)
        i1a = AP(_h(z1), 0, [[512, 128], [32, NI], [8, 2], [1, 8]])
        i1b = AP(_h(z1), 16, [[512, 128], [32, NI], [8, 2], [1, 8]])
        o2 = AP(_h(z2), 0, [[256, 128], [16, NI], [8, 2], [1, 8]])
        nc.vector.tensor_tensor(o2, i1a, i1b, ALU.add)
        i2a = AP(_h(z2), 0, [[256, 128], [16, NI], [1, 8]])
        i2b = AP(_h(z2), 8, [[256, 128], [16, NI], [1, 8]])
        if cl_ == 0 and bq_ == 0:
            W(nc.vector, "sT", t_slabtpT)
        oy = AP(_h(yslab[bq_]), cl_ * 8, [[2048, 128], [128, NI], [1, 8]])
        t4 = I(nc.vector.tensor_tensor(oy, i2a, i2b, ALU.add), "sV", 4)
        t_cbf_cons[cbpar_] = t4

    def flushp():
        if pend[0] is not None:
            post(pend[0])
            pend[0] = None

    for cc in range(4):
        prefetch_skw(cc)
    prefetch_qvk(0)
    for c in range(128):
        par = c % 2
        par4 = c % 4
        par8 = (c // 8) % 2
        cb = (c % 8) * 128
        cl = c % 16
        cg = c // 16
        W(nc.vector, SKS[par4], t_raw[par4])
        W(nc.vector, "sT", t_mm[par])
        revv = AP(_h(skwr[par4]), 2047, [[2048, 128], [-1, 2048]])
        t_rev[par] = t_rvw[par4] = I(nc.vector.tensor_copy(skw[par][:, :], revv),
                                     "sV")
        if c + 4 < 128:
            prefetch_skw(c + 4)
        if c % 8 == 0:
            flushp()
            if c // 8 + 1 < 16:
                prefetch_qvk(c // 8 + 1)
        for bq in range(NB):
            cbpar = (c * NB + bq) % 2
            W(nc.vector, "d3", t_ld[par8])
            W(nc.vector, "sT", t_mm[cbpar])
            in0 = AP(_h(kc[par8]), bq * 1024 + cb,
                     [[4096, 128], [1, NI], [NI, 8], [0, 8]])
            in1 = AP(_h(vc[par8]), bq * 1024 + cb,
                     [[4096, 128], [1, NI], [0, 8], [NI, 8]])
            t_kvb = I(nc.vector.tensor_tensor(kv[cbpar][:, :], in0, in1, ALU.mult),
                      "sV")
            W(nc.tensor, "sV", max(t_kvb, t_rev[par], t_qc_done[cbpar]))
            for dd in range(NI):
                s0 = dd * 64
                lt = skw[par][:, (NI - 1 - dd) * 128:(NI - dd) * 128]
                if s0 < 512:
                    nc.tensor.matmul(cps[cbpar][:, s0:512], lt,
                                     kv[cbpar][:, 0:512 - s0],
                                     start=(dd == 0), stop=(dd == 7))
                    mm = nc.tensor.matmul(cps[cbpar][:, 512:1024], lt,
                                          kv[cbpar][:, 512 - s0:1024 - s0],
                                          start=(dd == 0), stop=(dd == NI - 1))
                else:
                    mm = nc.tensor.matmul(cps[cbpar][:, s0:1024], lt,
                                          kv[cbpar][:, 0:1024 - s0],
                                          start=False, stop=(dd == NI - 1))
            t_mm[cbpar] = I(mm, "sT")
            flushp()
            pend[0] = (c, bq, cbpar, par8, cb, cl)
        if cl == 15:
            flushp()
            W(nc.tensor, "sV")
            for bq in range(NB):
                for Ib in range(NI):
                    tpar = Ib % 2
                    W(nc.tensor, "sV", t_tpev[tpar])
                    tt = nc.tensor.transpose(
                        tpsl[tpar][:, :], yslab[bq][:, Ib * 128:(Ib + 1) * 128],
                        K.ident_sb[:, :])
                    t_tp = I(tt, "sT")
                    t_slabtpT = t_tp
                    W(nc.vector, "sT", t_tp)
                    if Ib == 0:
                        W(nc.vector, "dG", t_ytstd)
                    t_tpev[tpar] = I(nc.vector.tensor_copy(
                        ytst[:, Ib * 128:(Ib + 1) * 128], tpsl[tpar][:, :]), "sV")
                W(nc.gpsimd, "sV", max(t_tpev))
                t_ytstd = I(nc.gpsimd.dma_start(
                    out=K.ytr[bq * 8 + cg, :, :], in_=ytst[:, :]), "dG")
    if K.dbg_taps:
        W(nc.gpsimd, "dG", t_ytstd)
        I(nc.gpsimd.dma_start(out=K.taps["tap_ytr"][:, :, :], in_=K.ytr[:, :, :]),
          "dG")
    st.close()


def phase_A4(K):
    nc, I, W = K.nc, K.I, K.W
    K.joinD(nc.sync)
    st = ExitStack()
    wop = K.sb(st, "a4_wop", [128, 1024], F32)
    wopb = K.sb(st, "a4_wopb", [128, NDC * 1024], BF16)
    ysb = K.sb(st, "a4_ysb", [128, 8 * L], BF16)
    uch = K.sb(st, "a4_uch", [128, L], F32)
    x2sb = K.sb(st, "a4_x2", [128, NDC * L], F32)
    xbfs = K.sb(st, "a4_xbfs", [128, 512], BF16)
    xsq = K.sb(st, "a4_xsq", [128, 512], BF16)
    t1b = K.sb(st, "a4_t1b", [128, 2 * 512], F32)
    h2s = K.sb(st, "a4_h2s", [128, 512], F32)
    m_sb = K.sb(st, "a4_m", [1, L], F32)
    s2_sb = K.sb(st, "a4_s2", [1, L], F32)
    rowt = K.sb(st, "a4_rt", [1, L], F32)
    rowt2 = K.sb(st, "a4_rt2", [1, L], F32)
    stps = K.ps(st, "a4_st", [128, 512])
    abps = [K.ps(st, f"a4_ab{i}", [128, 512]) for i in range(2)]
    wps = [K.ps(st, f"a4_wp{i}", [128, 512]) for i in range(2)]

    t_wc = 0
    for ch in range(NDC):
        W(nc.sync, "sV", t_wc)
        t_l = I(nc.sync.dma_start(
            out=wop[:, :], in_=K.woTp[ch * 128:(ch + 1) * 128, :]), "d3")
        W(nc.vector, "d3", t_l)
        t_wc = I(nc.vector.tensor_copy(
            wopb[:, ch * 1024:(ch + 1) * 1024], wop[:, :]), "sV")

    nmax = 0
    t_ev = [0, 0]
    t_abcons = [0, 0]
    t_woMM = 0
    t_spill = (0, 0)
    for b in range(NB):
        W(nc.sync, "sT", max(t_woMM, K._blhs_cons))
        t_y = 0
        for cg in range(8):
            t_y = I(nc.sync.dma_start(out=ysb[:, cg * L:(cg + 1) * L],
                                      in_=K.ytr[b * 8 + cg, :, :]), "dS")
        t_bl = I(nc.sync.dma_start(
            out=K.blhs[:, :], in_=AP(K.blhsdr, (b * 2 + 1) * D,
                                     [[NB * 2 * D, 2], [1, D]])), "dS")
        t_uc = [0, 0]
        t_x2 = 0
        W(nc.vector, "d4", t_spill[0])
        W(nc.vector, "dB", t_spill[1])
        for fc in range(NDC):
            par = 0
            W(nc.sync, "sV", t_uc[par])
            t_u = I(nc.sync.dma_start(out=uch[:, :],
                                      in_=K.u_fm[b, fc * 128:(fc + 1) * 128, :]),
                    "d2")
            for s in range(NS):
                spar = (fc * NS + s) % 2
                W(nc.tensor, "dS", t_y)
                W(nc.tensor, "sV", max(t_wc, t_ev[spar]))
                for cg in range(8):
                    mm = nc.tensor.matmul(
                        wps[spar][:, :],
                        wopb[:, cg * 1024 + fc * 128: cg * 1024 + (fc + 1) * 128],
                        ysb[:, cg * L + s * 512: cg * L + (s + 1) * 512],
                        start=(cg == 0), stop=(cg == 7))
                t_woMM = t_mm = I(mm, "sT")
                W(nc.scalar, "sT", t_mm)
                W(nc.scalar, "sV", t_ev[spar])
                rev = AP(wps[spar], 127, [[512, 128], [128, 4], [-1, 128]])
                t_t1 = I(nc.scalar.activation(
                    out=t1b[:, spar * 512:(spar + 1) * 512], in_=rev,
                    func=AF.Identity, scale=vcol(K, b, 2, fc),
                    bias=K.gabo[:, b * NDC + fc: b * NDC + fc + 1]), "sA")
                W(nc.vector, "sA", t_t1)
                W(nc.vector, "d2", t_u)
                t_x2 = I(nc.vector.tensor_add(
                    x2sb[:, fc * L + s * 512: fc * L + (s + 1) * 512],
                    t1b[:, spar * 512:(spar + 1) * 512],
                    uch[:, s * 512:(s + 1) * 512]), "sV")
                t_ev[spar] = t_x2
                t_uc[par] = t_x2

        def xsrc(dc, s):
            W(nc.vector, "sA")
            t = I(nc.vector.tensor_copy(
                xbfs[:, :], x2sb[:, dc * L + s * 512: dc * L + (s + 1) * 512]),
                "sV")
            W(nc.scalar, "sV", t)
            return xbfs[:, :]
        ln_stats(K, xsrc, m_sb, s2_sb, stps, xsq)
        row_stats(K, m_sb, s2_sb, rowt, rowt2)
        W(nc.vector, "dS", t_bl)
        rows_to_bf(K)
        blhs_to_bf(K)
        W(nc.gpsimd, "sV")
        I(nc.gpsimd.dma_start(out=K.st2dr[b, 0:1, :], in_=K.rstd_sb[:1, :]), "dG")
        t_st2d = I(nc.gpsimd.dma_start(out=K.st2dr[b, 1:2, :], in_=K.brhs[0:1, :]),
                   "dG")
        K._blhs_cons = max(K._blhs_cons, 0)
        W(nc.vector, "dG", t_st2d)
        for s in range(NS):
            for dc in range(NDC):
                W(nc.tensor, "sV", t_abcons[0])
                t_ab = ab_maps(K, dc, s, abps[0], abps[1])
                W(nc.vector, "sT", t_ab)
                xsl = x2sb[:, dc * L + s * 512: dc * L + (s + 1) * 512]
                nc.vector.scalar_tensor_tensor(h2s[:, :], xsl, 1.0,
                                               abps[0][:, :], ALU.mult,
                                               ALU.mult)
                nc.vector.tensor_add(h2s[:, :], h2s[:, :], abps[1][:, :])
                t_r = I(nc.vector.tensor_reduce(
                    K.redcg[:, nmax:nmax + 1], h2s[:, :], AX.X, ALU.max,
                    apply_absolute_value=True), "sV", 3)
                nmax += 1
                t_abcons[0] = t_r
        W(nc.gpsimd, "sV")
        for dc in range(NDC):
            I(nc.gpsimd.dma_start(
                out=K.x2dr[b, dc * 128:(dc + 1) * 128, :],
                in_=x2sb[:, dc * L:(dc + 1) * L]), ("d4", "dB")[dc % 2])
        t_spill = (K.cnt["d4"], K.cnt["dB"])
        if K.dbg_taps:
            W(nc.gpsimd, "d4", t_spill[0])
            W(nc.gpsimd, "dB", t_spill[1])
            I(nc.gpsimd.dma_start(out=K.taps["tap_x2"][b, :, :],
                                  in_=K.x2dr[b, :, :]), "dG")
    finalize_scale(K, K.redcg, nmax, K.ar0i, K.ar0o, [(0, 1)], 1)
    st.close()


def finalize_scale(K, redc, nidx, ari, aro, colpairs, n_sc):
    nc, I, W = K.nc, K.I, K.W
    v = nc.vector
    stf = ExitStack()
    K._fsn = getattr(K, "_fsn", 0) + 1
    rps = K.ps(stf, f"fs_rps{K._fsn}", [1, 128])
    bcps = K.ps(stf, f"fs_bcps{K._fsn}", [128, 1])
    reds = redc if isinstance(redc, list) else [redc]
    t = 0
    for i, rc in enumerate(reds):
        W(v, "sT", K._fin_t)
        t = I(v.tensor_reduce(K.red1[:, :], rc[:, :nidx], AX.X, ALU.max,
                              apply_absolute_value=True), "sV")
        W(nc.tensor, "sV", t)
        K._fin_t = tt = I(nc.tensor.transpose(rps[:, :], K.red1[:, :],
                                              K.identf_sb[:, :]), "sT")
        W(v, "sT", tt)
        t = I(v.tensor_reduce(K.scrow[:1, i:i + 1], rps[:1, :], AX.X, ALU.max,
                              apply_absolute_value=True), "sV")
    W(nc.gpsimd, "sV", t)
    td = I(nc.gpsimd.dma_start(out=ari[:, :], in_=K.scrow[:1, 0:8]), "dG")
    W(nc.gpsimd, "dG", td)
    tcc = I(nc.gpsimd.collective_compute(
        "AllReduce", ALU.max, replica_groups=K.rg,
        ins=[ari.ap().opt()], outs=[aro.ap().opt()]), "sCC")
    W(nc.sync, "sCC", tcc)
    tl = I(nc.sync.dma_start(out=K.scrow[:1, 4:4 + n_sc], in_=aro[:, 0:n_sc]), "dS")
    W(v, "dS", tl)
    t = I(v.tensor_scalar(out=K.scrow[:1, 4:4 + n_sc], in0=K.scrow[:1, 4:4 + n_sc],
                          scalar1=float(np.float32(1 / 32767.0)), scalar2=1e-12,
                          op0=ALU.mult, op1=ALU.max), "sV")
    W(nc.scalar, "sV", t)
    t = I(nc.scalar.activation(out=K.scrow[:1, 8:8 + n_sc],
                               in_=K.scrow[:1, 4:4 + n_sc],
                               func=AF.Identity), "sA")
    W(v, "sA", t)
    t = I(v.reciprocal(K.scrow[:1, 6:6 + n_sc], K.scrow[:1, 8:8 + n_sc]), "sV")
    for i, (icol, scol) in enumerate(colpairs):
        for src_off, dcol in ((6 + i, icol), (4 + i, scol)):
            W(nc.tensor, "sV", t)
            tm = I(nc.tensor.matmul(bcps[:, :], K.onescol_sb[:, :],
                                    K.scrow[:1, src_off:src_off + 1],
                                    start=True, stop=True), "sT")
            W(v, "sT", tm)
            t = I(v.tensor_copy(K.scales[:, dcol:dcol + 1], bcps[:, :]), "sV")
    if K.dbg_taps:
        W(nc.gpsimd, "sV", t)
        I(nc.gpsimd.dma_start(out=K.taps["tap_scales"][:, :], in_=K.scales[:, :]),
          "dG")
    W(nc.tensor, "sV", t)
    stf.close()


def phase_B(K):
    nc, I, W = K.nc, K.I, K.W
    BD = K.upto if K.upto.startswith("BD") else ""
    skip_prep = BD in ("BDmm",)
    stGU = ExitStack()
    gstage_lo = K.sb(stGU, "b_gstlo", [128, 2 * L], BF16)
    wgT = K.sb(stGU, "b_wgT", [128, NDC * FF], BF16)
    wuT = K.sb(stGU, "b_wuT", [128, NDC * FF], BF16)

    st = ExitStack()
    tpsl = [K.ps(st, f"bp_tp{i}", [128, 128], BF16) for i in range(2)]
    t_tpev = [0, 0]
    for (srcdr, dstsb, nch, R, tag) in ([] if skip_prep else
                                        [(K.wg, wgT, 24, 1024, "g"),
                                         (K.wu, wuT, 24, 1024, "u")][
                                            0:(1 if BD == "BDmm2g" else 2)]):
        nblk = R // 32
        stj = ExitStack()
        wraw = K.sb(stj, f"bp_wraw{tag}", [128, 2 * R], F32)
        blkm = K.sb(stj, f"bp_blkm{tag}", [128, nblk], F32)
        sblk = K.sb(stj, f"bp_sblk{tag}", [128, nblk], F32)
        sblk2 = K.sb(stj, f"bp_sblk2{tag}", [128, nblk], F32)
        iblk = K.sb(stj, f"bp_iblk{tag}", [128, nblk], F32)
        tf = K.sb(stj, f"bp_tf{tag}", [128, R], F32)
        wqb = K.sb(stj, f"bp_wqb{tag}", [128, R], BF16)
        t_wl = [0, 0]
        t_cons = [0, 0]
        for ch in range(nch):
            par = ch % 2
            W(nc.sync, "sV", t_cons[par])
            if tag == "d":
                srcsl = srcdr[(ch // 2) * 128:(ch // 2 + 1) * 128,
                              (ch % 2) * 1536:(ch % 2 + 1) * 1536]
            else:
                srcsl = srcdr[ch * 128:(ch + 1) * 128, :]
            t_wl[par] = I(nc.sync.dma_start(
                out=wraw[:, par * R: par * R + R], in_=srcsl), "dS")
            W(nc.vector, "dS", t_wl[par])
            W(nc.vector, "sT")
            wr = wraw[:, par * R: par * R + R]
            wv = wr.rearrange("p (nb t) -> p nb t", t=32)
            nc.vector.tensor_reduce(blkm[:, :nblk], wv, AX.X, ALU.max,
                                    apply_absolute_value=True)
            t_ts = I(nc.vector.tensor_scalar(out=sblk[:, :nblk], in0=blkm[:, :nblk],
                                             scalar1=float(np.float32(1 / 7.0)),
                                             scalar2=1e-12, op0=ALU.mult,
                                             op1=ALU.max), "sV", 2)
            W(nc.scalar, "sV", t_ts)
            t_cp = I(nc.scalar.activation(out=sblk2[:, :nblk], in_=sblk[:, :nblk],
                                          func=AF.Identity), "sA")
            W(nc.vector, "sA", t_cp)
            nc.vector.reciprocal(iblk[:, :nblk], sblk2[:, :nblk])
            ib = AP(_h(iblk), 0, [[nblk, 128], [1, nblk], [0, 32]])
            sb_ = AP(_h(sblk), 0, [[nblk, 128], [1, nblk], [0, 32]])
            nc.vector.tensor_tensor(tf[:, :R], wr, ib, ALU.mult)
            nc.vector.tensor_scalar(out=tf[:, :R], in0=tf[:, :R], scalar1=CMAG,
                                    scalar2=CMAG, op0=ALU.add, op1=ALU.subtract)
            t_cons[par] = I(nc.vector.tensor_tensor(wqb[:, :R], tf[:, :R], sb_,
                                                    ALU.mult), "sV", 4)
            W(nc.tensor, "sV", t_cons[par])
            for ti in range(R // 128):
                tpar = ti % 2
                W(nc.tensor, "sV", t_tpev[tpar])
                tt = nc.tensor.transpose(tpsl[tpar][:, :],
                                         wqb[:, ti * 128:(ti + 1) * 128],
                                         K.ident_sb[:, :])
                t_tp = I(tt, "sT")
                W(nc.vector, "sT", t_tp)
                if tag in ("g", "u"):
                    dst = dstsb[:, ti * FF + ch * 128: ti * FF + (ch + 1) * 128]
                else:
                    tig = (ch % 2) * 12 + ti
                    chg = ch // 2
                    dst = dstsb[:, tig * 1024 + chg * 128: tig * 1024 + (chg + 1) * 128]
                t_tpev[tpar] = I(nc.vector.tensor_copy(dst, tpsl[tpar][:, :]), "sV")
        W(nc.sync, "sV")
        stj.close()
    st.close()
    if K.upto == "Bp" or BD == "BDp":
        stGU.close()
        return

    K.joinD(nc.sync)
    K.joinD(nc.sync)
    st = ExitStack()
    x2ch = K.sb(st, "b1_x2ch", [128, 2 * L], F32)
    h2q = K.sb(st, "b1_h2q", [128, NDC * L], BF16)
    h2s = K.sb(st, "b1_h2s", [128, 512], F32)
    gstage = gstage_lo
    abps = [K.ps(st, f"b1_ab{i}", [128, 512]) for i in range(2)]
    gups = [K.ps(st, f"b1_gu{i}", [128, 512]) for i in range(2)]
    ng, nu = 0, 0
    t_gup_red = [0, 0]
    t_gst_d = [0, 0]
    t_act = [0, 0]
    t_abc = [0, 0]
    t_q = 0
    t_gupMM = 0
    nb_lim = 1 if BD else NB
    nic_lim = 2 if BD else NIC
    for b in range(nb_lim):
        if BD.startswith("BDmm"):
            break
        W(nc.sync, "sT", max(K._blhs_cons, t_gupMM))
        t_bl = I(nc.sync.dma_start(
            out=K.blhs[:, :], in_=AP(K.blhsdr, (b * 2 + 1) * D,
                                     [[NB * 2 * D, 2], [1, D]])), "dS")
        I(nc.sync.dma_start(out=K.rstd_sb[:1, :], in_=K.st2dr[b, 0:1, :]), "dS")
        t_st2 = I(nc.sync.dma_start(out=K.brhs[0:1, :], in_=K.st2dr[b, 1:2, :]),
                  "dS")
        W(nc.vector, "sT")
        W(nc.vector, "dS", t_st2)
        rows_to_bf(K)
        blhs_to_bf(K)
        t_xc = [0, 0]
        for dc in range(0 if BD in ("BDonly", "BDmm") else NDC):
            par = dc % 2
            W(nc.sync, "sV", t_xc[par])
            t_x = I(nc.sync.dma_start(out=x2ch[:, par * L:(par + 1) * L],
                                      in_=K.x2dr[b, dc * 128:(dc + 1) * 128, :]),
                    "d3")
            for s in range(NS):
                W(nc.tensor, "dS", max(t_bl, t_st2))
                W(nc.tensor, "sV", t_abc[0])
                t_ab = ab_maps(K, dc, s, abps[0], abps[1])
                W(nc.vector, "sT", t_ab)
                W(nc.vector, "d3", t_x)
                xsl = x2ch[:, par * L + s * 512: par * L + (s + 1) * 512]
                nc.vector.scalar_tensor_tensor(h2s[:, :], xsl, 1.0,
                                               abps[0][:, :], ALU.mult,
                                               ALU.mult)
                nc.vector.tensor_add(h2s[:, :], h2s[:, :], abps[1][:, :])
                nc.vector.tensor_scalar(out=h2s[:, :], in0=h2s[:, :],
                                        scalar1=K.scales[:, 0:1], scalar2=CMAG,
                                        op0=ALU.mult, op1=ALU.add)
                t_q = I(nc.vector.tensor_scalar(
                    out=h2q[:, dc * L + s * 512: dc * L + (s + 1) * 512],
                    in0=h2s[:, :], scalar1=CMAG, scalar2=K.scales[:, 1:2],
                    op0=ALU.subtract, op1=ALU.mult), "sV", 4)
                t_abc[0] = t_q
                t_xc[par] = t_q
        if K.upto == "B1h" or BD == "BDh":
            continue
        if BD in ("BDonly", "BDmm"):
            t_q = K.cnt["sV"]
        for Ic in range(nic_lim):
            if K.upto == "B1bar":
                K.bar()
            for it, (wsb_, drt, redc) in enumerate(((wgT, K.gatedr, K.redcg),
                                                    (wuT, K.updr, K.redcu))):
                for s in range(NS):
                    W(nc.tensor, "sV", t_q)
                    W(nc.tensor, "sA", t_act[it])
                    for dc in range(NDC):
                        lhsw = (K.ident_sb[:, :] if BD in ("BDident", "BDmm") else
                                wsb_[:, dc * FF + Ic * 128: dc * FF + (Ic + 1) * 128])
                        mm = nc.tensor.matmul(
                            gups[it][:, :], lhsw,
                            h2q[:, dc * L + s * 512: dc * L + (s + 1) * 512],
                            start=(dc == 0), stop=(dc == NDC - 1))
                    t_gupMM = t_mm = I(mm, "sT")
                    W(nc.vector, "sT", t_mm)
                    if s == 0:
                        W(nc.vector, "d4" if it == 0 else "d5", t_gst_d[it])
                    t_act[it] = I(nc.vector.tensor_copy(
                        gstage[:, it * L + s * 512: it * L + (s + 1) * 512],
                        gups[it][:, :]), "sA")
                idx = ng if it == 0 else nu
                t_gup_red[it] = I(nc.vector.tensor_reduce(
                    redc[:, idx:idx + 1], gstage[:, it * L:(it + 1) * L],
                    AX.X, ALU.max, apply_absolute_value=True), "sV")
                if it == 0:
                    ng += 1
                else:
                    nu += 1
                W(nc.gpsimd, "sA", t_act[it])
                W(nc.gpsimd, "sV", t_gup_red[it])
                t_gst_d[it] = I(nc.gpsimd.dma_start(
                    out=drt[b, Ic * 128:(Ic + 1) * 128, :],
                    in_=gstage[:, it * L:(it + 1) * L]), "d4" if it == 0 else "d5")
    if BD in ("BDmm", "BDmm2", "BDmm2g", "BDmm3", "BDmm4", "BDmm5", "BDmm6", "BDmm7", "BDmm8"):
        if BD in ("BDmm3", "BDmm4", "BDmm5", "BDmm6", "BDmm7", "BDmm8"):
            t_bl = I(nc.sync.dma_start(
                out=K.blhs[:, :], in_=AP(K.blhsdr, D, [[NB * 2 * D, 2], [1, D]])),
                "dS")
            I(nc.sync.dma_start(out=K.rstd_sb[:1, :], in_=K.st2dr[0, 0:1, :]), "dS")
            t_st2 = I(nc.sync.dma_start(out=K.brhs[0:1, :], in_=K.st2dr[0, 1:2, :]),
                      "dS")
        if BD in ("BDmm4", "BDmm5", "BDmm6", "BDmm7", "BDmm8"):
            W(nc.vector, "dS", t_st2)
            rows_to_bf(K)
            blhs_to_bf(K)
        t_act2 = [0, 0]
        for Ic in range(2):
            for it in range(2):
                for sx in range(NS):
                    W(nc.tensor, "sA", t_act2[it])
                    for dc in range(NDC):
                        lw = ((wgT if it == 0 else wuT)[:, dc * FF + Ic * 128:
                                                        dc * FF + (Ic + 1) * 128]
                              if BD == "BDmm8" else K.ident_sb[:, :])
                        mm = nc.tensor.matmul(
                            gups[it][:, :], lw,
                            h2q[:, dc * L + sx * 512: dc * L + (sx + 1) * 512],
                            start=(dc == 0), stop=(dc == NDC - 1))
                    t_mm = I(mm, "sT")
                    W(nc.vector, "sT", t_mm)
                    if BD in ("BDmm6", "BDmm7", "BDmm8"):
                        t_act2[it] = I(nc.vector.tensor_copy(
                            gstage[:, it * L + sx * 512: it * L + (sx + 1) * 512],
                            gups[it][:, :]), "sA")
                    else:
                        t_act2[it] = I(nc.vector.tensor_copy(h2s[:, :],
                                                             gups[it][:, :]), "sA")
                if BD in ("BDmm5", "BDmm7", "BDmm8"):
                    I(nc.vector.tensor_reduce(
                        K.redcg[:, Ic * 2 + it: Ic * 2 + it + 1],
                        gstage[:, it * L:(it + 1) * L], AX.X, ALU.max,
                        apply_absolute_value=True), "sV")
    if not BD and K.upto not in ("B1h", "B1m", "B1m0", "B1m1", "B1m2", "B1mA",
                                 "B1bar"):
        finalize_scale(K, [K.redcg, K.redcu], 96, K.ar1i, K.ar1o,
                       [(2, 3), (4, 5)], 2)
    st.close()
    if BD or K.upto in ("B1h", "B1m", "B1m0", "B1m1", "B1m2", "B1mA", "B1bar"):
        stGU.close()
        return
    if K.upto == "B1":
        stGU.close()
        return

    K.joinD(nc.sync)
    st = ExitStack()
    gch = K.sb(st, "b2_gch", [128, 2 * L], BF16)
    gq = K.sb(st, "b2_gq", [128, L], BF16)
    qsc = K.sb(st, "b2_qsc", [128, L], F32)
    sig = K.sb(st, "b2_sig", [128, L], F32)
    g2b = K.sb(st, "b2_g2", [128, 2 * L], BF16)
    n2 = 0
    t_g2d = [0, 0]
    t_cons = [0, 0]
    for b in range(NB):
        for Ic in range(NIC):
            par = (b * NIC + Ic) % 2
            W(nc.sync, "sV", t_cons[par])
            t_in = I(nc.sync.dma_start(
                out=gch[:, par * L:(par + 1) * L],
                in_=K.gatedr[b, Ic * 128:(Ic + 1) * 128, :]), "d2")
            W(nc.vector, "d2", t_in)
            gs = gch[:, par * L:(par + 1) * L]
            nc.vector.tensor_scalar(out=qsc[:, :], in0=gs, scalar1=K.scales[:, 2:3],
                                    scalar2=CMAG, op0=ALU.mult, op1=ALU.add)
            t_gq = I(nc.vector.tensor_scalar(
                out=gq[:, :], in0=qsc[:, :], scalar1=CMAG, scalar2=K.scales[:, 3:4],
                op0=ALU.subtract, op1=ALU.mult), "sV", 2)
            W(nc.scalar, "sV", t_gq)
            t_sg = I(nc.scalar.activation(out=sig[:, :], in_=gq[:, :],
                                          func=AF.Sigmoid), "sA")
            W(nc.vector, "sA", t_sg)
            nc.vector.tensor_scalar(out=sig[:, :], in0=sig[:, :], scalar1=65536.0,
                                    scalar2=CMAG, op0=ALU.mult, op1=ALU.add)
            nc.vector.tensor_scalar(out=sig[:, :], in0=sig[:, :], scalar1=CMAG,
                                    scalar2=65535.0, op0=ALU.subtract, op1=ALU.min)
            W(nc.vector, ("d4", "dB")[par], t_g2d[par])
            nc.vector.scalar_tensor_tensor(g2b[:, par * L:(par + 1) * L], gq[:, :],
                                           float(np.float32(1 / 65536.0)),
                                           sig[:, :], ALU.mult, ALU.mult)
            t_cons[par] = I(nc.vector.tensor_reduce(
                K.redcg[:, n2:n2 + 1], g2b[:, par * L:(par + 1) * L], AX.X,
                ALU.max, apply_absolute_value=True), "sV", 4)
            n2 += 1
            W(nc.gpsimd, "sV", t_cons[par])
            t_g2d[par] = I(nc.gpsimd.dma_start(
                out=K.g2dr[b, Ic * 128:(Ic + 1) * 128, :],
                in_=g2b[:, par * L:(par + 1) * L]), ("d4", "dB")[par])
    finalize_scale(K, K.redcg, n2, K.ar2i, K.ar2o, [(6, 7)], 1)
    st.close()
    if K.upto == "B2":
        stGU.close()
        return

    K.joinD(nc.sync)
    st = ExitStack()
    g2ch = K.sb(st, "b3_g2ch", [128, 2 * L], BF16)
    upch = K.sb(st, "b3_upch", [128, 2 * L], BF16)
    oq = K.sb(st, "b3_oq", [128, L], BF16)
    qsc3 = K.sb(st, "b3_qsc", [128, L], F32)
    upq = K.sb(st, "b3_upq", [128, L], BF16)
    ob = K.sb(st, "b3_ob", [128, 2 * L], BF16)
    n3 = 0
    t_od = [0, 0]
    t_cons = [0, 0]
    for b in range(NB):
        for Ic in range(NIC):
            par = (b * NIC + Ic) % 2
            W(nc.sync, "sV", t_cons[par])
            I(nc.sync.dma_start(out=g2ch[:, par * L:(par + 1) * L],
                                in_=K.g2dr[b, Ic * 128:(Ic + 1) * 128, :]), "d2")
            t_in = I(nc.sync.dma_start(
                out=upch[:, par * L:(par + 1) * L],
                in_=K.updr[b, Ic * 128:(Ic + 1) * 128, :]), "d2")
            W(nc.vector, "d2", t_in)
            g2s = g2ch[:, par * L:(par + 1) * L]
            ups = upch[:, par * L:(par + 1) * L]
            nc.vector.tensor_scalar(out=qsc3[:, :], in0=g2s, scalar1=K.scales[:, 6:7],
                                    scalar2=CMAG, op0=ALU.mult, op1=ALU.add)
            nc.vector.tensor_scalar(out=oq[:, :], in0=qsc3[:, :], scalar1=CMAG,
                                    scalar2=K.scales[:, 7:8], op0=ALU.subtract,
                                    op1=ALU.mult)
            nc.vector.tensor_scalar(out=qsc3[:, :], in0=ups, scalar1=K.scales[:, 4:5],
                                    scalar2=CMAG, op0=ALU.mult, op1=ALU.add)
            nc.vector.tensor_scalar(out=upq[:, :], in0=qsc3[:, :], scalar1=CMAG,
                                    scalar2=K.scales[:, 5:6], op0=ALU.subtract,
                                    op1=ALU.mult)
            W(nc.vector, ("d4", "dB")[par], t_od[par])
            nc.vector.tensor_mul(ob[:, par * L:(par + 1) * L], oq[:, :], upq[:, :])
            t_cons[par] = I(nc.vector.tensor_reduce(
                K.redcg[:, n3:n3 + 1], ob[:, par * L:(par + 1) * L], AX.X,
                ALU.max, apply_absolute_value=True), "sV", 6)
            n3 += 1
            W(nc.gpsimd, "sV", t_cons[par])
            t_od[par] = I(nc.gpsimd.dma_start(
                out=K.odr[b, Ic * 128:(Ic + 1) * 128, :],
                in_=ob[:, par * L:(par + 1) * L]), ("d4", "dB")[par])
    finalize_scale(K, K.redcg, n3, K.ar3i, K.ar3o, [(8, 9)], 1)
    st.close()
    stGU.close()
    if K.upto == "B3":
        return

    K.joinD(nc.sync)
    stW = ExitStack()
    wdT = K.sb(stW, "b_wdT", [128, NIC * 1024], BF16)
    stp = ExitStack()
    tpsl = [K.ps(stp, f"b4p_tp{i}", [128, 128], BF16) for i in range(2)]
    t_tpev = [0, 0]
    for (srcdr, dstsb, nch, R, tag) in [(K.wdn, wdT, 16, 1536, "d")]:
        nblk = R // 32
        stj = ExitStack()
        wraw = K.sb(stj, f"bp_wraw{tag}", [128, 2 * R], F32)
        blkm = K.sb(stj, f"bp_blkm{tag}", [128, nblk], F32)
        sblk = K.sb(stj, f"bp_sblk{tag}", [128, nblk], F32)
        sblk2 = K.sb(stj, f"bp_sblk2{tag}", [128, nblk], F32)
        iblk = K.sb(stj, f"bp_iblk{tag}", [128, nblk], F32)
        tf = K.sb(stj, f"bp_tf{tag}", [128, R], F32)
        wqb = K.sb(stj, f"bp_wqb{tag}", [128, R], BF16)
        t_wl = [0, 0]
        t_cons = [0, 0]
        for ch in range(nch):
            par = ch % 2
            W(nc.sync, "sV", t_cons[par])
            srcsl = srcdr[(ch // 2) * 128:(ch // 2 + 1) * 128,
                          (ch % 2) * 1536:(ch % 2 + 1) * 1536]
            t_wl[par] = I(nc.sync.dma_start(
                out=wraw[:, par * R: par * R + R], in_=srcsl), "d6")
            W(nc.vector, "d6", t_wl[par])
            W(nc.vector, "sT")
            wr = wraw[:, par * R: par * R + R]
            wv = wr.rearrange("p (nb t) -> p nb t", t=32)
            nc.vector.tensor_reduce(blkm[:, :nblk], wv, AX.X, ALU.max,
                                    apply_absolute_value=True)
            t_ts = I(nc.vector.tensor_scalar(out=sblk[:, :nblk], in0=blkm[:, :nblk],
                                             scalar1=float(np.float32(1 / 7.0)),
                                             scalar2=1e-12, op0=ALU.mult,
                                             op1=ALU.max), "sV", 2)
            W(nc.scalar, "sV", t_ts)
            t_cp = I(nc.scalar.activation(out=sblk2[:, :nblk], in_=sblk[:, :nblk],
                                          func=AF.Identity), "sA")
            W(nc.vector, "sA", t_cp)
            nc.vector.reciprocal(iblk[:, :nblk], sblk2[:, :nblk])
            ib = AP(_h(iblk), 0, [[nblk, 128], [1, nblk], [0, 32]])
            sb_ = AP(_h(sblk), 0, [[nblk, 128], [1, nblk], [0, 32]])
            nc.vector.tensor_tensor(tf[:, :R], wr, ib, ALU.mult)
            nc.vector.tensor_scalar(out=tf[:, :R], in0=tf[:, :R], scalar1=CMAG,
                                    scalar2=CMAG, op0=ALU.add, op1=ALU.subtract)
            t_cons[par] = I(nc.vector.tensor_tensor(wqb[:, :R], tf[:, :R], sb_,
                                                    ALU.mult), "sV", 4)
            W(nc.tensor, "sV", t_cons[par])
            for ti in range(R // 128):
                tpar = ti % 2
                W(nc.tensor, "sV", t_tpev[tpar])
                tt = nc.tensor.transpose(tpsl[tpar][:, :],
                                         wqb[:, ti * 128:(ti + 1) * 128],
                                         K.ident_sb[:, :])
                t_tp = I(tt, "sT")
                W(nc.vector, "sT", t_tp)
                tig = (ch % 2) * 12 + ti
                chg = ch // 2
                dst = dstsb[:, tig * 1024 + chg * 128: tig * 1024 + (chg + 1) * 128]
                t_tpev[tpar] = I(nc.vector.tensor_copy(dst, tpsl[tpar][:, :]), "sV")
        W(nc.sync, "sV")
        stj.close()
    stp.close()
    st = ExitStack()
    oqsb = K.sb(st, "b4_oq", [128, NIC * L], BF16)
    och = K.sb(st, "b4_och", [128, L], BF16)
    x2ch = K.sb(st, "b4_x2", [128, L], F32)
    qsc4 = x2ch
    t1b = K.sb(st, "b4_t1b", [128, 2 * 512], F32)
    ost = K.sb(st, "b4_ost", [128, L], F32)
    dnps = [K.ps(st, f"b4_dn{i}", [128, 512]) for i in range(2)]
    t_in = [0, 0]
    t_oq = 0
    t_dnMM = 0
    t_ostd = [0, 0]
    t_ev = [0, 0]
    t_t1c = 0
    for b in range(NB):
        W(nc.vector, "sT", t_dnMM)
        for Ic in range(NIC):
            par = 0
            W(nc.sync, "sV", t_in[par])
            t_l = I(nc.sync.dma_start(out=och[:, :],
                                      in_=K.odr[b, Ic * 128:(Ic + 1) * 128, :]),
                    "d2")
            W(nc.vector, "d2", t_l)
            nc.vector.tensor_scalar(out=qsc4[:, :], in0=och[:, :],
                                    scalar1=K.scales[:, 8:9], scalar2=CMAG,
                                    op0=ALU.mult, op1=ALU.add)
            t_oq = I(nc.vector.tensor_scalar(
                out=oqsb[:, Ic * L:(Ic + 1) * L], in0=qsc4[:, :],
                scalar1=CMAG, scalar2=K.scales[:, 9:10], op0=ALU.subtract,
                op1=ALU.mult), "sV", 2)
            t_in[par] = t_oq
        if K.dbg_taps:
            W(nc.gpsimd, "sV", t_oq)
            I(nc.gpsimd.dma_start(out=K.taps["tap_oq"][b, :, :], in_=oqsb[:, :]),
              "dG")
        t_x2c = [0, 0]
        for dc in range(NDC):
            par = dc % 2
            W(nc.sync, "sV", max(t_x2c[0], t_oq))
            t_x = I(nc.sync.dma_start(out=x2ch[:, :],
                                      in_=K.x2dr[b, dc * 128:(dc + 1) * 128, :]),
                    "d3")
            for s in range(NS):
                spar = s % 2
                W(nc.tensor, "sV", max(t_oq, t_ev[spar]))
                W(nc.tensor, "sA", t_t1c)
                for Ic in range(NIC):
                    mm = nc.tensor.matmul(
                        dnps[spar][:, :],
                        wdT[:, Ic * 1024 + dc * 128: Ic * 1024 + (dc + 1) * 128],
                        oqsb[:, Ic * L + s * 512: Ic * L + (s + 1) * 512],
                        start=(Ic == 0), stop=(Ic == NIC - 1))
                t_dnMM = t_mm = I(mm, "sT")
                W(nc.scalar, "sT", t_mm)
                W(nc.scalar, "sV", t_ev[spar])
                t_t1c = I(nc.scalar.activation(
                    out=t1b[:, spar * 512:(spar + 1) * 512], in_=dnps[spar][:, :],
                    func=AF.Identity, scale=vcol(K, b, 5, dc), bias=0.0), "sA")
                W(nc.vector, "sA", t_t1c)
                W(nc.vector, "d3", t_x)
                if s == 0:
                    W(nc.vector, "d4", t_ostd[0])
                t_o = I(nc.vector.tensor_add(
                    ost[:, s * 512:(s + 1) * 512],
                    t1b[:, spar * 512:(spar + 1) * 512],
                    x2ch[:, s * 512:(s + 1) * 512]), "sV")
                t_ev[spar] = t_o
                t_x2c[0] = t_o
            W(nc.gpsimd, "sV", t_o)
            t_ostd[0] = I(nc.gpsimd.dma_start(
                out=K.outfm[b, dc * 128:(dc + 1) * 128, :],
                in_=ost[:, :]), "d4")
    W(nc.gpsimd, "d4")
    W(nc.gpsimd, "dG")
    st.close()
    stW.close()


def _prep_inputs(inputs, r):
    f32 = np.float32
    u = np.ascontiguousarray(
        np.asarray(inputs["L0_attn_u"], f32)[4 * r:4 * r + 4].transpose(0, 2, 1))
    def _projp(w):
        wt = np.asarray(w, f32).T
        return np.ascontiguousarray(
            wt.reshape(8, 128, 8, 128).transpose(2, 1, 0, 3).reshape(D, D))

    wqT = _projp(inputs["L0_attn_Wq"])
    wkT = _projp(inputs["L0_attn_Wk"])
    wvT = _projp(inputs["L0_attn_Wv"])
    p = np.arange(1024)
    perm = (p % 8) * 128 + (p // 128) * 16 + ((p % 128) // 8)
    woTp = np.ascontiguousarray(np.asarray(inputs["L0_attn_Wo"], f32).T[perm])
    biases = np.ascontiguousarray(
        np.concatenate([np.asarray(inputs[k], f32).reshape(8, 128).T
                        for k in ("L0_attn_bq", "L0_attn_bk", "L0_attn_bv",
                                  "L0_attn_bo")], axis=1))
    kkpad = np.pad(np.asarray(inputs["L0_attn_ssm_k_kernel"], f32),
                   ((0, 0), (127, 1)))
    skD = np.ascontiguousarray(
        np.asarray(inputs["L0_attn_ssm_k_D"], f32).reshape(8, 128).T)
    kcpad = np.pad(np.asarray(inputs["L0_attn_ssm_kernel"], f32),
                   ((0, 0), (127, 1)))
    Dc = np.ascontiguousarray(np.asarray(inputs["L0_attn_D"], f32).reshape(128, 1))
    return {
        "u_fm": u, "wqT": wqT, "wkT": wkT, "wvT": wvT, "woTp": woTp,
        "biases": biases, "kkpad": kkpad, "skD": skD, "kcpad": kcpad, "Dc": Dc,
        "wg": np.asarray(inputs["L0_ffn_w_gate"], f32),
        "wu": np.asarray(inputs["L0_ffn_w_up"], f32),
        "wdn": np.asarray(inputs["L0_ffn_w_down"], f32),
        "wmod": np.asarray(inputs["L0_wmod"], f32),
        "c_own": np.ascontiguousarray(np.asarray(inputs["L0_c"], f32)[4 * r:4 * r + 4]),
    }


def kernel(**inputs):
    nc = build(dbg_taps=False)
    in_maps = [_prep_inputs(inputs, r) for r in range(8)]
    res = run_bass_kernel_spmd(nc, in_maps, core_ids=list(range(8)))
    out = np.concatenate([np.asarray(res.results[r]["outfm"]) for r in range(8)],
                         axis=0)
    return np.ascontiguousarray(out.transpose(0, 2, 1))
